# version 18
# baseline (speedup 1.0000x reference)
import math
import numpy as np
import ml_dtypes
import concourse.bass as bass
import concourse.mybir as mybir
from concourse.bass_utils import run_bass_kernel_spmd
from contextlib import ExitStack

F32 = mybir.dt.float32
BF16 = mybir.dt.bfloat16
AF = mybir.ActivationFunctionType
ALU = mybir.AluOpType
AX = mybir.AxisListType

T = 2048
D = 1024
H, DK, DV = 4, 256, 256
NBLK = 16
CH = 128
NT = 4
TT = 512

ENGINES = ("pe", "act", "dve", "pool", "sp")
ATTNB_PARTS = set("ABCDEFG")
EMIT_DRAINS = True


def weave(*gens):
    gens = [g for g in gens if g is not None]
    while gens:
        for g in list(gens):
            try:
                next(g)
            except StopIteration:
                gens.remove(g)



class Op:
    __slots__ = ("eng", "emit", "order_i", "mark", "is_dma", "dma_idx", "waits",
                 "tag", "spos", "drain_pos")

    def __init__(self, eng, emit, is_dma=False, tag=""):
        self.eng = eng
        self.emit = emit
        self.mark = False
        self.is_dma = is_dma
        self.dma_idx = -1
        self.waits = []
        self.order_i = -1
        self.spos = -1
        self.drain_pos = -1
        self.tag = tag


class Sched:

    def __init__(self, nc):
        self.nc = nc
        self.ops = {e: [] for e in ENGINES}
        self.order = []
        self.res = {}
        self.carry = {}
        self._dma_counts = {}

    def _st(self, name, key):
        d = self.res.setdefault(name, {})
        return d.setdefault(key, {"w": None, "r": []})

    def _dep_ops(self, o, reads, writes):
        deps = []
        for (name, key) in reads:
            keys = list(self.res.get(name, {})) if key is None else [key, None]
            psum = name.startswith("pf") or name.startswith("pb")
            for k in keys:
                st = self.res.get(name, {}).get(k)
                if st and st["w"] is not None:
                    deps.append(st["w"])
                if psum and st:
                    deps.extend(r for r in st["r"] if r.eng != o.eng)
            deps.extend(self.carry.get(name, ()))
        for (name, key) in writes:
            keys = list(self.res.get(name, {})) if key is None else [key, None]
            for k in keys:
                st = self.res.get(name, {}).get(k)
                if st:
                    if st["w"] is not None:
                        deps.append(st["w"])
                    deps.extend(st["r"])
            deps.extend(self.carry.get(name, ()))
        return deps

    def _commit(self, o, reads, writes):
        for (name, key) in reads:
            self._st(name, key)["r"].append(o)
        for (name, key) in writes:
            if key is None:
                d = self.res.setdefault(name, {})
                d.clear()
                self.carry.pop(name, None)
            st = self._st(name, key)
            st["w"] = o
            st["r"] = []

    def op(self, eng, emit, reads=(), writes=(), is_dma=False, tag=""):
        o = Op(eng, emit, is_dma=is_dma, tag=tag)
        deps = self._dep_ops(o, reads, writes)
        best = {}
        for d in deps:
            if d is o:
                continue
            if not d.is_dma and d.eng == eng:
                if eng != "pe":
                    o.drain_pos = max(o.drain_pos, d.spos)
                continue
            stream = ("dma", d.eng, d.dma_idx % self.NDMA) if d.is_dma else ("eng", d.eng)
            cur = best.get(stream)
            if cur is None or d.order_i > cur.order_i:
                best[stream] = d
        o.waits = list(best.values())
        o.spos = len(self.ops[eng])
        if is_dma:
            o.dma_idx = self._dma_counts.get(eng, 0)
            self._dma_counts[eng] = o.dma_idx + 1
        o.order_i = len(self.order)
        self.order.append(o)
        self.ops[eng].append(o)
        self._commit(o, reads, writes)
        return o

    @staticmethod
    def _best_per_stream(ops):
        best = {}
        for d in ops:
            stream = ("dma", d.eng, d.dma_idx % Sched.NDMA) if d.is_dma else ("eng", d.eng)
            cur = best.get(stream)
            if cur is None or d.order_i > cur.order_i:
                best[stream] = d
        return list(best.values())

    def retag(self, name):
        ops = []
        for st in self.res.get(name, {}).values():
            if st["w"] is not None:
                ops.append(st["w"])
            ops.extend(st["r"])
        ops.extend(self.carry.get(name, ()))
        self.res[name] = {}
        self.carry[name] = self._best_per_stream(ops)

    NDMA = 8

    def finalize_marks(self):
        for o in self.order:
            for d in o.waits:
                if not d.is_dma:
                    d.mark = True
        self.mnum = {}
        self.dma_slot = {}
        self.dma_final = {}
        for e in ENGINES:
            n = 0
            for o in self.ops[e]:
                if o.is_dma:
                    slot = o.dma_idx % self.NDMA
                    use = o.dma_idx // self.NDMA + 1
                    self.dma_slot[id(o)] = (slot, use)
                    self.dma_final[(e, slot)] = use
                elif o.mark:
                    n += 1
                self.mnum[id(o)] = n if not o.is_dma else None

    def emit_all(self, sems, dma_sems):
        self.finalize_marks()
        seen = {e: {} for e in ENGINES}

        def run_stream(e, engine):
            sn = seen[e]
            drained_upto = -1
            for o in self.ops[e]:
                if EMIT_DRAINS and o.drain_pos >= 0 and o.drain_pos > drained_upto:
                    engine.drain()
                    drained_upto = o.spos - 1
                for d in o.waits:
                    if d.is_dma:
                        slot, use = self.dma_slot[id(d)]
                        key = ("dma", d.eng, slot)
                        val = 16 * use
                        sem = dma_sems[d.eng][slot]
                    else:
                        key = ("eng", d.eng)
                        val = self.mnum[id(d)]
                        sem = sems[d.eng]
                    if sn.get(key, 0) >= val:
                        continue
                    engine.wait_ge(sem, val)
                    sn[key] = val
                if o.is_dma:
                    slot, use = self.dma_slot[id(o)]
                    key = ("dma", o.eng, slot)
                    if use >= 2 and sn.get(key, 0) < 16 * (use - 1):
                        engine.wait_ge(dma_sems[o.eng][slot], 16 * (use - 1))
                        sn[key] = 16 * (use - 1)
                    inst = o.emit(engine)
                    inst.then_inc(dma_sems[o.eng][slot], 16)
                else:
                    inst = o.emit(engine)
                    if o.mark:
                        if inst is None:
                            raise RuntimeError(f"marked op has no instruction: {o.tag}")
                        inst.then_inc(sems[o.eng], 1)
        return run_stream



def build_toeplitz(filt, K):
    C = filt.shape[0]
    out = np.zeros((127, C, 64), np.float32)
    s = np.arange(127)[:, None]
    i = np.arange(64)[None, :]
    j = s - i + K - 64
    valid = (j >= 0) & (j < K)
    js = np.clip(j, 0, K - 1)
    for c in range(C):
        out[:, c, :] = np.where(valid, filt[c][js], 0.0)
    return out.reshape(127, C * 64)


def prep_consts():
    c = {}
    c["identity"] = np.eye(128, dtype=np.float32)
    c["ones_row"] = np.ones((1, 128), np.float32)
    c["ones_col"] = np.ones((128, 1), np.float32)
    c["maskSL"] = np.tril(np.ones((128, 128), np.float32), -1)
    c["maskIU"] = np.triu(np.ones((128, 128), np.float32), 0)
    p = np.arange(128, dtype=np.float32)
    c["iota_p"] = p[:, None].copy()
    c["iota_rev"] = (127.0 - p)[:, None].copy()
    DT = np.clip(p[None, :] - 1 - p[:, None], 0, None)
    c["DT"] = DT.astype(np.float32)
    c["CZ"] = (p[None, :] > p[:, None]).astype(np.float32)
    k = np.zeros((128, 8), np.float32)
    k[:, 0] = 1e-5; k[:, 1] = 1e-6; k[:, 2] = 1.0; k[:, 3] = 0.02
    c["konst"] = k
    return c


def prep_weights(ins):
    w = {}
    w["qwT"] = np.ascontiguousarray(ins["L0_attn_a_q_w"].T)
    w["kwT"] = np.ascontiguousarray(ins["L0_attn_a_k_w"].T)
    w["vwT"] = np.ascontiguousarray(ins["L0_attn_a_v_w"].T)
    w["bwT"] = np.ascontiguousarray(ins["L0_attn_a_b_w"].T)
    w["owT"] = np.ascontiguousarray(ins["L0_attn_a_o_w"].T)
    w["m1main"] = np.ascontiguousarray(ins["L0_attn_a_mlp1_w"][:, :D].T)
    w["m1tail"] = np.ascontiguousarray(ins["L0_attn_a_mlp1_w"][:, D:].T)
    w["m2T"] = np.ascontiguousarray(ins["L0_attn_a_mlp2_w"].T)
    for nm, key in (("qcw", "L0_attn_a_q_cw"), ("kcw", "L0_attn_a_k_cw"), ("vcw", "L0_attn_a_v_cw")):
        w[nm] = np.ascontiguousarray(ins[key].reshape(8, 128, 4).transpose(1, 0, 2))
    w["toep_s"] = build_toeplitz(ins["L0_attn_a_fir_short_w"].reshape(H * DV, 5), 5)
    w["toep_l"] = build_toeplitz(ins["L0_attn_a_fir_long_w"].reshape(H * DV, 64), 64)
    w["b1f"] = np.ascontiguousarray(ins["L0_attn_a_mlp1_b"].reshape(8, 128).T)
    w["normw_row"] = ins["L0_attn_a_norm_w"][None, :].copy()
    w["m2b_row"] = ins["L0_attn_a_mlp2_b"][None, :].copy()
    w["crl_row"] = ins["L0_attn_a_conv_res_logit"][None, :].copy()
    w["ltemp"] = ins["L0_attn_a_logit_temp"].reshape(1, 1).copy()
    w["cn_w1"] = ins["L0_ffn_cn_w1"]
    w["cn_w2"] = ins["L0_ffn_cn_w2"]
    w["cn_w3"] = ins["L0_ffn_cn_w3"]
    w["cn_w4"] = np.ascontiguousarray(ins["L0_ffn_cn_w4"].reshape(2, 128).T)
    w["cb1f"] = np.ascontiguousarray(ins["L0_ffn_cn_b1"].reshape(8, 128).T)
    w["cb2f"] = np.ascontiguousarray(ins["L0_ffn_cn_b2"].reshape(4, 128).T)
    w["cb3f"] = np.ascontiguousarray(ins["L0_ffn_cn_b3"].reshape(2, 128).T)
    w["cb4"] = ins["L0_ffn_cn_b4"].reshape(1, 1).copy()
    w["gc_w1"] = ins["L0_ffn_gc_w1"]
    w["gc_w2"] = ins["L0_ffn_gc_w2"]
    w["gc_w3"] = np.ascontiguousarray(ins["L0_ffn_gc_w3"].reshape(2, 128).T)
    w["gb1_row"] = ins["L0_ffn_gc_b1"][None, :].copy()
    w["gb2_row"] = ins["L0_ffn_gc_b2"][None, :].copy()
    w["gb3"] = ins["L0_ffn_gc_b3"].reshape(1, 1).copy()
    w["ar"] = ins["L0_ffn_adaptive_range"].reshape(1, 1).copy()
    w["WwT"] = np.ascontiguousarray(ins["L0_attn_b_W_write"].T)
    w["WrT"] = np.ascontiguousarray(ins["L0_attn_b_W_read"].T)
    w["decay"] = np.asarray(ins["L0_attn_b_decay"]).reshape(1, 1).copy()
    w["log_alpha"] = np.asarray(ins["L0_attn_b_log_alpha"]).reshape(1, 1).copy()
    return w



class KB:
    def __init__(self, nc, dbg=False):
        self.nc = nc
        self.s = Sched(nc)
        self.es = ExitStack()
        self.sb = {}
        self.dbg = dbg
        self.dbg_views = {}

    def sbuf(self, name, shape, dt):
        cm = self.nc.sbuf_tensor(name, shape, dt)
        t = cm.__enter__()
        self.sb[name] = t
        self._cms = getattr(self, "_cms", {})
        self._cms[name] = cm
        self._order = getattr(self, "_order", [])
        self._order.append(name)
        pend = getattr(self, "_pending_carry", None)
        if pend:
            self.s.carry.setdefault(name, []).extend(pend)
        return t

    def free(self, *names):
        pend = getattr(self, "_pending_carry", None)
        if pend is None:
            pend = self._pending_carry = []
        for name in names:
            for st in self.s.res.get(name, {}).values():
                if st["w"] is not None:
                    pend.append(st["w"])
                pend.extend(st["r"])
            pend.extend(self.s.carry.get(name, ()))
            self.s.res.pop(name, None)
            self.s.carry.pop(name, None)
            self._cms[name].__exit__(None, None, None)
            del self.sb[name]
            self._order.remove(name)
        self._pending_carry = pend = self.s._best_per_stream(pend)

    def free_from(self, marker):
        i = self._order.index(marker)
        for name in reversed(self._order[i:]):
            self.free(name)

    def psum(self, name, shape, dt):
        t = self.es.enter_context(self.nc.psum_tensor(name, shape, dt))
        self.sb[name] = t
        return t

    def din(self, name, shape, dt=F32):
        return self.nc.dram_tensor(name, shape, dt, kind="ExternalInput").ap()

    def dout(self, name, shape, dt=F32):
        return self.nc.dram_tensor(name, shape, dt, kind="ExternalOutput").ap()

    def dint(self, name, shape, dt=F32):
        kind = "ExternalOutput" if self.dbg else "Internal"
        return self.nc.dram_tensor(name, shape, dt, kind=kind).ap()

    def dma(self, q, out_ap, in_ap, reads=(), writes=(), tag=""):
        return self.s.op(q, lambda e, o=out_ap, i=in_ap: e.dma_start(out=o, in_=i),
                         reads=reads, writes=writes, is_dma=True, tag=tag)

    def mm(self, out_ap, lhsT, rhs, start, stop, reads=(), writes=(), tag="", tile_position=None):
        return self.s.op("pe",
                         lambda e, o=out_ap, l=lhsT, r=rhs, st=start, sp=stop, tp=tile_position:
                             e.matmul(o, l, r, start=st, stop=sp, tile_position=tp),
                         reads=reads, writes=writes, tag=tag)

    def tr(self, out_ap, in_ap, ident, reads=(), writes=(), tag=""):
        return self.s.op("pe",
                         lambda e, o=out_ap, i=in_ap, d=ident: e.transpose(o, i, d),
                         reads=reads, writes=writes, tag=tag)

    def act(self, out_ap, in_ap, func, bias=None, scale=None, accum_out=None,
            reads=(), writes=(), tag=""):
        kw = {}
        if bias is not None:
            kw["bias"] = bias
        if scale is not None:
            kw["scale"] = scale
        if accum_out is not None:
            kw["accum_out"] = accum_out
        return self.s.op("act",
                         lambda e, o=out_ap, i=in_ap, f=func, kw=kw:
                             e.activation(out=o, in_=i, func=f, **kw),
                         reads=reads, writes=writes, tag=tag)

    def tt(self, out, in0, in1, op, eng="dve", reads=(), writes=(), tag=""):
        return self.s.op(eng,
                         lambda e, o=out, a=in0, b=in1, p=op: e.tensor_tensor(out=o, in0=a, in1=b, op=p),
                         reads=reads, writes=writes, tag=tag)

    def ts(self, out, in0, s1, s2=None, op0=ALU.mult, op1=None, eng="dve",
           reads=(), writes=(), tag=""):
        def f(e, o=out, a=in0, s1=s1, s2=s2, p0=op0, p1=op1):
            kw = dict(out=o, in0=a, scalar1=s1, scalar2=s2, op0=p0)
            if p1 is not None:
                kw["op1"] = p1
            return e.tensor_scalar(**kw)
        return self.s.op(eng, f, reads=reads, writes=writes, tag=tag)

    def stt(self, out, in0, scalar, in1, op0, op1, eng="dve", reads=(), writes=(), tag=""):
        return self.s.op(eng,
                         lambda e, o=out, a=in0, s=scalar, b=in1, p0=op0, p1=op1:
                             e.scalar_tensor_tensor(out=o, in0=a, scalar=s, in1=b, op0=p0, op1=p1),
                         reads=reads, writes=writes, tag=tag)

    def red(self, out, in_, op, absval=False, eng="dve", reads=(), writes=(), tag=""):
        return self.s.op(eng,
                         lambda e, o=out, i=in_, p=op, av=absval:
                             e.tensor_reduce(out=o, in_=i, axis=AX.X, op=p, apply_absolute_value=av),
                         reads=reads, writes=writes, tag=tag)

    def cp(self, out, in_, eng="dve", reads=(), writes=(), tag=""):
        if eng == "act":
            return self.s.op(eng, lambda e, o=out, i=in_: e.copy(out=o, in_=i),
                             reads=reads, writes=writes, tag=tag)
        return self.s.op(eng, lambda e, o=out, i=in_: e.tensor_copy(out=o, in_=i),
                         reads=reads, writes=writes, tag=tag)

    def memset(self, ap, val, eng="dve", reads=(), writes=(), tag=""):
        return self.s.op(eng, lambda e, a=ap, v=val: e.memset(a, v),
                         reads=reads, writes=writes, tag=tag)

    def recip(self, out, in_, reads=(), writes=(), tag=""):
        return self.s.op("dve", lambda e, o=out, i=in_: e.reciprocal(out=o, in_=i),
                         reads=reads, writes=writes, tag=tag)

    def dump(self, name, src_ap, reads, cast=False):
        if not self.dbg:
            return
        shape = list(src_ap.shape)
        d = self.dout("dbg_" + name, shape, F32)
        q = "pool" if cast else "sp"
        self.dma(q, d, src_ap, reads=reads, tag="dump_" + name)
        self.dbg_views[name] = shape


def attn_c_phase(kb, h, L):
    pf, pb = L["pf"], L["pb"]
    konst = L["konst"]; idf = L["idf"]; idb = L["idb"]
    stcrP, stcP, statsT = L["stcr"], L["stc"], L["statsT"]
    trash2 = L["trash2"]; sshin = L["sshin"]; h1s = L["h1s"]; h1f = L["h1f"]
    m1tail, m2T, b1f = L["m1tail"], L["m2T"], L["b1f"]
    lgr, lgc, pc, pcolP = L["lgr"], L["lgc"], L["pc"], L["pcol"]
    ls_tok, ll_tok, dout, v_tok = L["ls_tok"], L["ll_tok"], L["dout"], L["v_tok"]
    o_tP, o_t2P, ohn_t = L["o_t"], L["o_t2"], L["ohn_t"]
    crl_sig, trec, normw_bc = L["crl_sig"], L["trec"], L["normw_bc"]
    ssh_scr, ohn_scr = L["ssh_scr"], L["ohn_scr"]
    kb.s.retag("copyA"); kb.s.retag("copyB")
    sshd = L["sshd"]

    def ssh_load(t):
        kb.dma("sp", sshd[:].rearrange("p (g t) -> p g t", g=8),
               ssh_scr[:, :, t * TT:(t + 1) * TT].rearrange("g p t -> p g t"),
               reads=[("sshd", None)] + [("ssh_scr", (t, go)) for go in range(8)],
               writes=[("sshd", None)], tag=f"sshld{h}_{t}")
    tensors = [("qf", ls_tok), ("kf", ll_tok), ("dout", dout), ("v_tok", v_tok)]
    for t in range(NT):
        sshin = sshd[:]
        ssh_n = ("sshd", None)
        ssh_load(t)
        for i in range(4):
            blk = t * 4 + i
            pp2 = i % 2
            stcr = stcrP[pp2]; stc = stcP[pp2]
            scr_n = f"stcr{pp2}"; stc_n = f"stc{pp2}"
            for ti, (tn, tb) in enumerate(tensors):
                xsl = tb[:, blk * 256:(blk + 1) * 256]
                kb.red(stcr[:, ti:ti + 1], xsl, ALU.add, reads=[(tn, None)],
                       writes=[(scr_n, ti)], tag=f"csum{h}_{blk}_{ti}")
                kb.red(stcr[:, 4 + ti:5 + ti], xsl, ALU.add, absval=True, reads=[(tn, None)],
                       writes=[(scr_n, 4 + ti)], tag=f"cabs{h}_{blk}_{ti}")
                kb.act(trash2[:, 0:256], xsl, AF.Square, accum_out=stcr[:, 8 + ti:9 + ti],
                       reads=[(tn, None)], writes=[("trash", None), (scr_n, 8 + ti)],
                       tag=f"csq{h}_{blk}_{ti}")
            stc_v = stc[:].rearrange("p (t s) -> p t s", s=4)
            raw_r = [(scr_n, j) for j in range(12)]
            kb.ts(stc_v[:, :, 0], stcr[:, 0:4], 1.0 / 256, reads=raw_r,
                  writes=[(stc_n, 0)], tag=f"cmean{h}_{blk}")
            kb.ts(stcr[:, 12:16], stcr[:, 8:12], 1.0 / 256, reads=raw_r,
                  writes=[(scr_n, 12)], tag=f"cex2{h}_{blk}")
            kb.tt(stc_v[:, :, 1], stc_v[:, :, 0], stc_v[:, :, 0], ALU.mult,
                  reads=[(stc_n, 0)], writes=[(stc_n, 1)], tag=f"cmsq{h}_{blk}")
            kb.tt(stc_v[:, :, 1], stcr[:, 12:16], stc_v[:, :, 1], ALU.subtract,
                  reads=[(scr_n, 12), (stc_n, 1)], writes=[(stc_n, 1)], tag=f"cvar{h}_{blk}")
            kb.ts(stc_v[:, :, 2], stcr[:, 4:8], 1.0 / 256, reads=raw_r,
                  writes=[(stc_n, 2)], tag=f"cam{h}_{blk}")
            kb.act(stc_v[:, :, 3], stcr[:, 8:12], AF.Sqrt, reads=raw_r,
                   writes=[(stc_n, 3)], tag=f"cl2{h}_{blk}")
            kb.tr(pf[5][0:16, 0:128], stc[:], idf[:],
                  reads=[(stc_n, kk) for kk in range(4)] + [("idf", None)],
                  writes=[("pf5", None)], tag=f"cstT{h}_{blk}")
            kb.cp(statsT[0:16, i * 128:(i + 1) * 128], pf[5][0:16, 0:128],
                  eng="dve", reads=[("pf5", None)], writes=[("statsT", i)],
                  tag=f"cstTe{h}_{blk}")
        for go in range(8):
            bank = pf[go % 2]
            bres = f"pf{go % 2}"
            kb.mm(bank[:], m1tail[0:16, go * 128:(go + 1) * 128], statsT[0:16, 0:512],
                  True, True,
                  reads=[("m1tail", None)] + [("statsT", ii) for ii in range(4)],
                  writes=[(bres, None)], tag=f"gmm{h}_{t}_{go}")
            r2 = go % 2
            hsl = h1s[:, r2 * 512:(r2 + 1) * 512]
            kb.tt(hsl, bank[:], sshin[:, go * 512:(go + 1) * 512], ALU.add,
                  reads=[(bres, None), ("copyA", None), ("h1s", r2)], writes=[("h1s", r2)],
                  tag=f"gadd{h}_{t}_{go}")
            kb.act(h1f[:, go * 512:(go + 1) * 512], hsl, AF.Gelu_apprx_tanh,
                   bias=b1f[:, go:go + 1],
                   reads=[("h1s", r2), ("b1f", None)], writes=[("copyB", go)],
                   tag=f"gact{h}_{t}_{go}")
        for kt in range(8):
            kb.mm(pf[2][0:4, 0:512], m2T[:, kt * 4:(kt + 1) * 4],
                  h1f[:, kt * 512:(kt + 1) * 512], start=(kt == 0), stop=(kt == 7),
                  reads=[("m2T", None), ("copyB", kt)], writes=[("pf2", None)],
                  tag=f"glog{h}_{t}_{kt}")
        kb.act(lgr[:], pf[2][0:4, 0:512], AF.Copy, reads=[("pf2", None)],
               writes=[("lgr", None)], tag=f"glge{h}_{t}")
        for i in range(4):
            kb.tr(pf[2][:, 128 + i * 4: 128 + (i + 1) * 4], lgr[0:4, i * 128:(i + 1) * 128],
                  idf[0:4, 0:4], reads=[("lgr", None), ("idf", None)],
                  writes=[("pf2", None)], tag=f"glgT{h}_{t}_{i}")
        kb.cp(lgc[:], pf[2][:, 128:144], eng="dve", reads=[("pf2", None)],
              writes=[("lgc", None)], tag=f"glgc{h}_{t}")

        for i in range(4):
            blk = t * 4 + i
            pp2 = i % 2
            pcol = pcolP[pp2]; o_t = o_tP[pp2]; o_t2 = o_t2P[pp2]
            pcl_n = f"pcol{pp2}"; ot_n = f"o_t{pp2}"; ot2_n = f"o_t2{pp2}"
            lsl = lgc[:, i * 4:(i + 1) * 4]
            kb.tt(lsl, lsl, L["m2b_bc"][:], ALU.add,
                  reads=[("lgc", None), ("m2b_bc", None)], writes=[("lgc", None)],
                  tag=f"sb{h}_{blk}")
            kb.red(pcol[:], lsl, ALU.max, reads=[("lgc", None)], writes=[(pcl_n, None)],
                   tag=f"smax{h}_{blk}")
            kb.ts(lsl, lsl, pcol[:], op0=ALU.subtract, reads=[("lgc", None), (pcl_n, None)],
                  writes=[("lgc", None)], tag=f"ssub{h}_{blk}")
            psl = pc[:, i * 4:(i + 1) * 4]
            kb.act(psl, lsl, AF.Exp, scale=trec[:],
                   reads=[("lgc", None), ("trec", None)], writes=[("pc", i)],
                   tag=f"sexp{h}_{blk}")
            kb.red(pcol[:], psl, ALU.add, reads=[("pc", i)], writes=[(pcl_n, None)],
                   tag=f"ssum{h}_{blk}")
            kb.recip(pcol[:], pcol[:], reads=[(pcl_n, None)], writes=[(pcl_n, None)])
            kb.ts(psl, psl, pcol[:], op0=ALU.mult, reads=[("pc", i), (pcl_n, None)],
                  writes=[("pc", i)], tag=f"sdiv{h}_{blk}")
            kb.ts(psl, psl, 0.02, op0=ALU.max, reads=[("pc", i)], writes=[("pc", i)],
                  tag=f"sflr{h}_{blk}")
            kb.red(pcol[:], psl, ALU.add, reads=[("pc", i)], writes=[(pcl_n, None)],
                   tag=f"ssum2{h}_{blk}")
            kb.recip(pcol[:], pcol[:], reads=[(pcl_n, None)], writes=[(pcl_n, None)])
            kb.ts(psl, psl, pcol[:], op0=ALU.mult, reads=[("pc", i), (pcl_n, None)],
                  writes=[("pc", i)], tag=f"sdiv2{h}_{blk}")
            kb.tt(psl[:, 0:1], psl[:, 0:1], crl_sig[:, h:h + 1], ALU.add,
                  reads=[("pc", i), ("crl_sig", None)], writes=[("pc", i)],
                  tag=f"sp0{h}_{blk}")
            kb.ts(o_t[:], ls_tok[:, blk * 256:(blk + 1) * 256], psl[:, 0:1],
                  op0=ALU.mult, reads=[("qf", None), ("pc", i), (ot_n, None)],
                  writes=[(ot_n, None)], tag=f"cb0{h}_{blk}")
            for bi, (bn, bb) in enumerate((("kf", ll_tok), ("dout", dout), ("v_tok", v_tok))):
                kb.stt(o_t[:], bb[:, blk * 256:(blk + 1) * 256], psl[:, bi + 1: bi + 2],
                       o_t[:], ALU.mult, ALU.add,
                       reads=[(bn, None), ("pc", i), (ot_n, None)], writes=[(ot_n, None)],
                       tag=f"cb{bi + 1}{h}_{blk}")
            kb.act(trash2[:, 256:512], o_t[:], AF.Square, accum_out=pcol[:],
                   reads=[(ot_n, None)], writes=[("trash", None), (pcl_n, None)],
                   tag=f"crms{h}_{blk}")
            kb.act(pcol[:], pcol[:], AF.Sqrt, scale=1.0 / 256, bias=konst[:, 0:1],
                   reads=[(pcl_n, None), ("konst", None)], writes=[(pcl_n, None)],
                   tag=f"crs{h}_{blk}")
            kb.recip(pcol[:], pcol[:], reads=[(pcl_n, None)], writes=[(pcl_n, None)])
            kb.stt(o_t2[:], o_t[:], pcol[:], normw_bc[:], ALU.mult, ALU.mult,
                   reads=[(ot_n, None), (pcl_n, None), ("normw_bc", None), (ot2_n, None)],
                   writes=[(ot2_n, None)], tag=f"cnw{h}_{blk}")
            for g in range(2):
                kb.tr(pb[1][:, g * 128:(g + 1) * 128], o_t2[:, g * 128:(g + 1) * 128],
                      idb[:], reads=[(ot2_n, None), ("idb", None)], writes=[("pb1", None)],
                      tag=f"coT{h}_{blk}_{g}")
            for g in range(2):
                kb.cp(ohn_t[:, g * 512 + i * 128: g * 512 + (i + 1) * 128],
                      pb[1][:, g * 128:(g + 1) * 128], eng="dve",
                      reads=[("pb1", None)], writes=[("ohn_t", (g, i))],
                      tag=f"coTe{h}_{blk}_{g}")
        for g in range(2):
            kb.dma("sp", ohn_scr[h * 2 + g, :, t * TT:(t + 1) * TT],
                   ohn_t[:, g * 512:(g + 1) * 512],
                   reads=[("ohn_t", (g, 0)), ("ohn_t", (g, 1)), ("ohn_t", (g, 2)), ("ohn_t", (g, 3))],
                   writes=[("ohn_scr", (h * 2 + g, t))], tag=f"ohnst{h}_{t}_{g}")


def attn_tail(kb, L):
    pf = L["pf"]
    xr, xa_scr, ohn_scr = L["xr"], L["xa_scr"], L["ohn_scr"]
    owT_sb, ohnin, xin = L["owT_sb"], L["ohnin"], L["xin"]
    win = L["win"]
    kb.dma("pool", owT_sb[:].rearrange("p (kt o) -> p kt o", kt=8),
           win["owT"].rearrange("(kt p) o -> p kt o", p=128), writes=[("lnxf", "ow")],
           tag="ldow")
    for t in range(NT):
        kb.dma("sp", ohnin[:].rearrange("p (kt c) -> p kt c", kt=8),
               ohn_scr[:, :, t * TT:(t + 1) * TT].rearrange("g p t -> p g t"),
               reads=[("lnxf", "ohin")] + [("ohn_scr", (kt, t)) for kt in range(8)],
               writes=[("lnxf", "ohin")], tag=f"ohld{t}")
        for i in range(4):
            blk = t * 4 + i
            xsl = xin[:, i * 1024:(i + 1) * 1024]
            kb.dma("sp", xsl, xr[blk], reads=[("xin", i)], writes=[("xin", i)],
                   tag=f"tx{blk}")
            for half in range(2):
                bank = pf[3 + half]
                bres = f"pf{3 + half}"
                for kt in range(8):
                    kb.mm(bank[:], ohnin[:, kt * 512 + i * 128: kt * 512 + (i + 1) * 128],
                          owT_sb[:, kt * 1024 + half * 512: kt * 1024 + (half + 1) * 512],
                          start=(kt == 0), stop=(kt == 7),
                          reads=[("lnxf", "ohin"), ("lnxf", "ow")], writes=[(bres, None)],
                          tag=f"tmm{blk}_{half}_{kt}")
                kb.tt(xsl[:, half * 512:(half + 1) * 512], xsl[:, half * 512:(half + 1) * 512],
                      bank[:], ALU.add,
                      reads=[("xin", i), (bres, None)], writes=[("xin", i)],
                      tag=f"tadd{blk}_{half}")
            kb.dma("sp", xa_scr.rearrange("(b p) d -> b p d", p=128)[blk], xsl,
                   reads=[("xin", i)], writes=[("xa_scr", blk)], tag=f"txs{blk}")


def ffn_phase(kb, G, tag):
    R = G["R"]
    """One FFN application: x_src -> x_dst. G holds buffers/weights APs."""
    pf, pb = G["pf"], G["pb"]
    idf, idb, konst = G["idf"], G["idb"], G["konst"]
    ones_row, ones_col_b = G["ones_row"], G["ones_col_b"]
    outf, xt, lnt = G["outf"], G["xt"], G["lnt"]
    obP, h1tP, h2tP, h3tP, twP = G["ob"], G["h1t"], G["h2t"], G["h3t"], G["tw"]
    cn1, cn2, cn3, cn4 = G["cn1"], G["cn2"], G["cn3"], G["cn4"]
    cb1f, cb2f, cb3f, cb4 = G["cb1f"], G["cb2f"], G["cb3f"], G["cb4"]
    gw1, gw2, gw3 = G["gw1"], G["gw2"], G["gw3"]
    gb1, gb2, gb3, arr = G["gb1"], G["gb2"], G["gb3"], G["arr"]
    gsc, gcf, statf = G["gsc"], G["gcf"], G["statf"]
    x_src, x_dst = G["x_src"], G["x_dst"]
    xsr = x_src.rearrange("(b p) d -> b p d", p=128)
    xdr = x_dst.rearrange("(b p) d -> b p d", p=128)
    outf_v = outf[:].rearrange("p (g t) -> p g t", g=8)
    T_ = T

    for blk in range(NBLK):
        r = blk % 4
        r2_ = blk % 2
        xv = xt[:, r * 1024:(r + 1) * 1024]
        kb.dma("sp", xv, xsr[blk], reads=[(R("xt"), r), (G["x_src_name"], blk)],
               writes=[(R("xt"), r)], tag=f"f{tag}lx{blk}")
        sc = statf[:, r * 8:(r + 1) * 8]
        kb.act(G["trash"][:], xv, AF.Square, accum_out=sc[:, 0:1],
               reads=[(R("xt"), r)], writes=[(R("trash"), None), (R("statf"), (r, 0))], tag=f"f{tag}sq{blk}")
        kb.act(G["trash"][:], xv, AF.Copy, accum_out=sc[:, 1:2],
               reads=[(R("xt"), r)], writes=[(R("trash"), None), (R("statf"), (r, 1))], tag=f"f{tag}sm{blk}")
        kb.ts(sc[:, 2:3], sc[:, 1:2], 1.0 / D, reads=[(R("statf"), (r, 1))], writes=[(R("statf"), (r, 2))])
        kb.ts(sc[:, 3:4], sc[:, 0:1], 1.0 / D, reads=[(R("statf"), (r, 0))], writes=[(R("statf"), (r, 3))])
        kb.tt(sc[:, 4:5], sc[:, 2:3], sc[:, 2:3], ALU.mult, reads=[(R("statf"), (r, 2))], writes=[(R("statf"), (r, 4))])
        kb.tt(sc[:, 5:6], sc[:, 3:4], sc[:, 4:5], ALU.subtract,
              reads=[(R("statf"), (r, 3)), (R("statf"), (r, 4))], writes=[(R("statf"), (r, 5))])
        kb.act(sc[:, 6:7], sc[:, 5:6], AF.Sqrt, bias=konst[:, 0:1],
               reads=[(R("statf"), (r, 5)), ("konst", None)], writes=[(R("statf"), (r, 6))], tag=f"f{tag}rt{blk}")
        kb.recip(sc[:, 7:8], sc[:, 6:7], reads=[(R("statf"), (r, 6))], writes=[(R("statf"), (r, 7))])
        lv = lnt[:, r2_ * 1024:(r2_ + 1) * 1024]
        kb.ts(lv, xv, sc[:, 2:3], sc[:, 7:8], ALU.subtract, ALU.mult,
              reads=[(R("xt"), r), (R("statf"), (r, 2)), (R("statf"), (r, 7)), (R("lnt"), r2_)],
              writes=[(R("lnt"), r2_)], tag=f"f{tag}ln{blk}")
        for half in range(2):
            kb.mm(pf[4 + half][0:1, 0:512], ones_col_b[:], lv[:, half * 512:(half + 1) * 512],
                  start=(blk == 0), stop=(blk == NBLK - 1),
                  reads=[("ones_col_b", None), (R("lnt"), r2_)], writes=[(f"pf{4 + half}", None)],
                  tag=f"f{tag}gc{blk}_{half}")
        for half in range(2):
            bank = pb[half]
            for g4 in range(4):
                g = half * 4 + g4
                kb.tr(bank[:, g4 * 128:(g4 + 1) * 128], lv[:, g * 128:(g + 1) * 128], idb[:],
                      reads=[(R("lnt"), r2_), ("idb", None)], writes=[(f"pb{half}", None)],
                      tag=f"f{tag}tr{blk}_{g}")
            kb.cp(outf_v[:, half * 4:(half + 1) * 4, blk * 128:(blk + 1) * 128],
                  bank[:, 0:512].rearrange("p (g t) -> p g t", g=4), eng="dve",
                  reads=[(f"pb{half}", None)], writes=[(R("outf"), (blk, half))],
                  tag=f"f{tag}ev{blk}_{half}")
    for half in range(2):
        kb.act(gsc[0:1, half * 512:(half + 1) * 512], pf[4 + half][0:1, 0:512], AF.Copy,
               scale=1.0 / T_, reads=[(f"pf{4 + half}", None)], writes=[(R("gsc"), ("gc", half))],
               tag=f"f{tag}gcr{half}")
    def rowmat(src_row_keys, src_row, n_in, w_sb, w_name, n_out, out_key, bias_row, bias_key, fn):
        nk = n_in // 128
        for kt in range(nk):
            kb.tr(pf[3][:, kt:kt + 1], src_row[0:1, kt * 128:(kt + 1) * 128], idf[0:1, 0:1],
                  reads=[(R("gsc"), k2) for k2 in src_row_keys] + [("idf", None)],
                  writes=[("pf3", None)], tag=f"f{tag}gT{out_key}_{kt}")
        kb.act(gcf[:, 0:nk], pf[3][:, 0:nk], AF.Copy, reads=[("pf3", None)],
               writes=[(R("gcf"), None)], tag=f"f{tag}gTe{out_key}")
        for kt in range(nk):
            kb.mm(pf[3][0:1, 0:n_out], gcf[:, kt:kt + 1], w_sb[:, kt * n_out:(kt + 1) * n_out],
                  start=(kt == 0), stop=(kt == nk - 1),
                  reads=[(R("gcf"), None), (w_name, None)], writes=[("pf3", None)],
                  tag=f"f{tag}gm{out_key}_{kt}")
        dst = G["grow"][0:1, 0:n_out]
        kb.tt(dst, pf[3][0:1, 0:n_out], bias_row[0:1, 0:n_out], ALU.add,
              reads=[("pf3", None), (bias_key, None), (R("grow"), None)], writes=[(R("grow"), None)],
              tag=f"f{tag}gb{out_key}")
        kb.act(gsc[0:1, 1024 + out_key[1] * 512: 1024 + out_key[1] * 512 + n_out],
               G["grow"][0:1, 0:n_out], fn,
               reads=[(R("grow"), None)], writes=[(R("gsc"), out_key)], tag=f"f{tag}ga{out_key}")
    rowmat([("gc", 0), ("gc", 1)], gsc[0:1, 0:1024], 1024, gw1, R("gw1"), 512, ("g1", 0), G["gb1"], R("gb1"), AF.Gelu)
    rowmat([("g1", 0)], gsc[0:1, 1024:1536], 512, gw2, R("gw2"), 256, ("g2", 1), G["gb2"], R("gb2"), AF.Gelu)
    rowmat([("g2", 1)], gsc[0:1, 1536:1792], 256, gw3, R("gw3"), 1, ("g3", 2), G["gb3"], R("gb3"), AF.Sigmoid)
    gfv = gsc[0:1, 2048:2049]
    c1 = gsc[0:1, 2560:2561]
    c2 = gsc[0:1, 2561:2562]
    t0_ = gsc[0:1, 2562:2563]
    kb.ts(c1, arr[0:1, 0:1], 0.6, reads=[(R("arr"), None)], writes=[(R("gsc"), "c1")])
    kb.tt(t0_, arr[0:1, 0:1], gfv, ALU.mult, reads=[(R("arr"), None), (R("gsc"), ("g3", 2))],
          writes=[(R("gsc"), "t0")])
    kb.ts(t0_, t0_, 1.4, reads=[(R("gsc"), "t0")], writes=[(R("gsc"), "t0")])
    kb.stt(c2, arr[0:1, 0:1], -1.0, t0_, ALU.mult, ALU.add,
           reads=[(R("arr"), None), (R("gsc"), "t0")], writes=[(R("gsc"), "c2")])
    kb.ts(c2, c2, 0.1, op0=ALU.add, reads=[(R("gsc"), "c2")], writes=[(R("gsc"), "c2")])

    def ffn_tile(t):
        fp = t % 2
        ob = obP[fp]; h1t = h1tP[fp]; h2t = h2tP[fp]; h3t = h3tP[fp]; tw = twP[fp]
        ob_n = R(f"ob{fp}"); h1_n = R(f"h1t{fp}"); h2_n = R(f"h2t{fp}")
        h3_n = R(f"h3t{fp}"); tw_n = R(f"tw{fp}")
        bh1 = (pf[0], "pf0", pf[1], "pf1") if fp == 0 else (pf[4], "pf4", pf[5], "pf5")
        bh2 = (pf[2], "pf2") if fp == 0 else (pf[3], "pf3")
        bh3 = (pf[0], "pf0") if fp == 0 else (pf[4], "pf4")
        bt4 = (pf[1], "pf1") if fp == 0 else (pf[5], "pf5")
        for cyc in range(3):
            for go in range(8):
                kb.act(ob[:, go * 512:(go + 1) * 512], outf_v[:, go, t * TT:(t + 1) * TT],
                       AF.Copy,
                       reads=[(R("outf"), (t * 4 + i2, go // 4)) for i2 in range(4)],
                       writes=[(ob_n, go)], tag=f"f{tag}ob{t}_{cyc}_{go}")
            yield
            for go in range(8):
                bank = bh1[0] if go % 2 == 0 else bh1[2]
                bres = bh1[1] if go % 2 == 0 else bh1[3]
                for kt in range(8):
                    kb.mm(bank[:], cn1[:, kt * 1024 + go * 128: kt * 1024 + (go + 1) * 128],
                          ob[:, kt * 512:(kt + 1) * 512], start=(kt == 0), stop=(kt == 7),
                          reads=[(R("cn1"), None), (ob_n, kt)], writes=[(bres, None)],
                          tag=f"f{tag}h1m{t}_{cyc}_{go}_{kt}")
                kb.act(h1t[:, go * 512:(go + 1) * 512], bank[:], AF.Gelu,
                       bias=cb1f[:, go:go + 1],
                       reads=[(bres, None), (R("cb1f"), None)], writes=[(h1_n, go)],
                       tag=f"f{tag}h1a{t}_{cyc}_{go}")
            yield
            for go in range(4):
                bank, bres = bh2
                for kt in range(8):
                    kb.mm(bank[:], cn2[:, kt * 512 + go * 128: kt * 512 + (go + 1) * 128],
                          h1t[:, kt * 512:(kt + 1) * 512], start=(kt == 0), stop=(kt == 7),
                          reads=[(R("cn2"), None), (h1_n, kt)], writes=[(bres, None)],
                          tag=f"f{tag}h2m{t}_{cyc}_{go}_{kt}")
                kb.act(h2t[:, go * 512:(go + 1) * 512], bank[:], AF.Gelu,
                       bias=cb2f[:, go:go + 1],
                       reads=[(bres, None), (R("cb2f"), None)], writes=[(h2_n, go)],
                       tag=f"f{tag}h2a{t}_{cyc}_{go}")
            yield
            for go in range(2):
                bank, bres = bh3
                for kt in range(4):
                    kb.mm(bank[:], cn3[:, kt * 256 + go * 128: kt * 256 + (go + 1) * 128],
                          h2t[:, kt * 512:(kt + 1) * 512], start=(kt == 0), stop=(kt == 3),
                          reads=[(R("cn3"), None), (h2_n, kt)], writes=[(bres, None)],
                          tag=f"f{tag}h3m{t}_{cyc}_{go}_{kt}")
                kb.act(h3t[:, go * 512:(go + 1) * 512], bank[:], AF.Gelu,
                       bias=cb3f[:, go:go + 1],
                       reads=[(bres, None), (R("cb3f"), None)], writes=[(h3_n, go)],
                       tag=f"f{tag}h3a{t}_{cyc}_{go}")
            yield
            for kt in range(2):
                kb.mm(bt4[0][0:1, 0:512], cn4[:, kt:kt + 1], h3t[:, kt * 512:(kt + 1) * 512],
                      start=(kt == 0), stop=(kt == 1),
                      reads=[(R("cn4"), None), (h3_n, kt)], writes=[(bt4[1], None)],
                      tag=f"f{tag}t4m{t}_{cyc}_{kt}")
            tfrow = gsc[0:1, 2563 + fp * 520: 3075 + fp * 520]
            kb.act(tfrow, bt4[0][0:1, 0:512], AF.Sigmoid, bias=cb4[0:1, 0:1],
                   reads=[(bt4[1], None), (R("cb4"), None), (R("gsc"), ("tf", fp))], writes=[(R("gsc"), ("tf", fp))],
                   tag=f"f{tag}t4a{t}_{cyc}")
            kb.ts(tfrow, tfrow, c1, c2, ALU.mult, ALU.add,
                  reads=[(R("gsc"), ("tf", fp)), (R("gsc"), "c1"), (R("gsc"), "c2")],
                  writes=[(R("gsc"), ("tf", fp))],
                  tag=f"f{tag}cpl{t}_{cyc}")
            kb.mm(bt4[0][:, 0:512], ones_row[:], tfrow, True, True,
                  reads=[("ones_row", None), (R("gsc"), ("tf", fp))], writes=[(bt4[1], None)],
                  tag=f"f{tag}cbc{t}_{cyc}")
            yield
            for g in range(8):
                gsrc = (g + 4) % 8
                sign = 1.0 if g in (0, 1, 6, 7) else -1.0
                kb.stt(tw[:], bt4[0][:, 0:512], sign, ob[:, gsrc * 512:(gsrc + 1) * 512],
                       ALU.mult, ALU.mult,
                       reads=[(bt4[1], None), (ob_n, gsrc), (tw_n, None)], writes=[(tw_n, None)],
                       tag=f"f{tag}tw{t}_{cyc}_{g}")
                kb.tt(outf_v[:, g, t * TT:(t + 1) * TT],
                      outf_v[:, g, t * TT:(t + 1) * TT], tw[:], ALU.add,
                      reads=[(tw_n, None)] + [(R("outf"), (t * 4 + i2, g // 4)) for i2 in range(4)],
                      writes=[(R("outf"), (t * 4 + i2, g // 4)) for i2 in range(4)],
                      tag=f"f{tag}up{t}_{cyc}_{g}")
        yield
        for i in range(4):
            blk = t * 4 + i
            r = blk % 4
            xv = xt[:, r * 1024:(r + 1) * 1024]
            kb.dma("sp", xv, xsr[blk], reads=[(R("xt"), r), (G["x_src_name"], blk)],
               writes=[(R("xt"), r)], tag=f"f{tag}rx{blk}")
            for half in range(2):
                bank = (bh1[0] if half == 0 else bh1[2])
                bres = (bh1[1] if half == 0 else bh1[3])
                for g4 in range(4):
                    g = half * 4 + g4
                    kb.tr(bank[:, g4 * 128:(g4 + 1) * 128],
                          outf_v[:, g, t * TT + i * 128: t * TT + (i + 1) * 128], idf[:],
                          reads=[(R("outf"), (t * 4, g // 4)), (R("outf"), (t * 4 + i, g // 4)), ("idf", None)],
                          writes=[(bres, None)], tag=f"f{tag}rT{blk}_{g}")
                kb.tt(xv[:, half * 512:(half + 1) * 512], xv[:, half * 512:(half + 1) * 512],
                      bank[:], ALU.add, reads=[(R("xt"), r), (bres, None)], writes=[(R("xt"), r)],
                      tag=f"f{tag}ra{blk}_{half}")
            kb.dma("sp", xdr[blk], xv, reads=[(R("xt"), r)], writes=[(G["x_dst_name"], blk)],
                   tag=f"f{tag}rs{blk}")
            yield


    weave(ffn_tile(0), ffn_tile(1))
    weave(ffn_tile(2), ffn_tile(3))

def attnb_phase(kb, B, n_chunks=NBLK):
    pf, pb = B["pf"], B["pb"]
    idf, idb, konst = B["idf"], B["idb"], B["konst"]
    ones_row = B["ones_row"]
    rkT, Wst, Wbf, WwT, WrT = B["rkT"], B["Wst"], B["Wbf"], B["WwT"], B["WrT"]
    MT, lgcol, gpcol, gwcol, gccol, acol = B["MT"], B["lgcol"], B["gpcol"], B["gwcol"], B["gccol"], B["acol"]
    rdbf, rdT, SMT, scrf = B["rdbf"], B["rdT"], B["SMT"], B["scrf"]
    x_src, x_dst = B["x_src"], B["x_dst"]
    xsr = x_src.rearrange("(b p) d -> b p d", p=128)
    xdr = x_dst.rearrange("(b p) d -> b p d", p=128)
    rkT_v = rkT[:].rearrange("p (g t) -> p g t", g=8)
    def chunk_front(c):
        pp = c % 2
        xt = B["xt"][c % 4]; lnt32 = B["lnt32"][pp]; lnbf = B["lnbf"][pp]
        vt = B["vt"][pp]; vgw = B["vgw"][pp]; wkt = B["wkt"][pp]
        statb = B["statb"] if pp == 0 else B["statb2"]
        sb_n = "statb" if pp == 0 else "statb2"
        xt_n = f"xtb{c % 4}"; ln_n = f"lnt32{pp}"; lb_n = f"lnbf{pp}"
        vt_n = f"vt{pp}"; vg_n = f"vgw{pp}"; wk_n = f"wkt{pp}"
        vbank, vb_n = (pf[0], "pf0") if pp == 0 else (pf[3], "pf3")
        kb.dma("sp", xt[:], xsr[c], reads=[(xt_n, None), (B["x_src_name"], c)],
               writes=[(xt_n, None)], tag=f"blx{c}")
        yield
        sc = statb
        kb.act(B["trash"][:], xt[:], AF.Square, accum_out=sc[:, 0:1],
               reads=[(xt_n, None)], writes=[("trashb", None), (sb_n, 0)], tag=f"bsq{c}")
        kb.act(B["trash"][:], xt[:], AF.Copy, accum_out=sc[:, 1:2],
               reads=[(xt_n, None)], writes=[("trashb", None), (sb_n, 1)], tag=f"bsm{c}")
        kb.ts(sc[:, 2:3], sc[:, 1:2], 1.0 / D, reads=[(sb_n, 1)], writes=[(sb_n, 2)])
        kb.ts(sc[:, 3:4], sc[:, 0:1], 1.0 / D, reads=[(sb_n, 0)], writes=[(sb_n, 3)])
        kb.tt(sc[:, 4:5], sc[:, 2:3], sc[:, 2:3], ALU.mult, reads=[(sb_n, 2)], writes=[(sb_n, 4)])
        kb.tt(sc[:, 5:6], sc[:, 3:4], sc[:, 4:5], ALU.subtract,
              reads=[(sb_n, 3), (sb_n, 4)], writes=[(sb_n, 5)])
        kb.act(sc[:, 6:7], sc[:, 5:6], AF.Sqrt, bias=konst[:, 0:1],
               reads=[(sb_n, 5), ("konst", None)], writes=[(sb_n, 6)], tag=f"brt{c}")
        kb.recip(sc[:, 7:8], sc[:, 6:7], reads=[(sb_n, 6)], writes=[(sb_n, 7)])
        yield
        kb.ts(lnt32[:], xt[:], sc[:, 2:3], sc[:, 7:8], ALU.subtract, ALU.mult,
              reads=[(xt_n, None), (sb_n, 2), (sb_n, 7), (ln_n, None)],
              writes=[(ln_n, None)], tag=f"bln{c}")
        kb.act(lnbf[:], lnt32[:], AF.Copy, reads=[(ln_n, None)], writes=[(lb_n, None)],
               tag=f"blnb{c}")
        yield
        for half in range(2):
            bank = pb[half]
            for g4 in range(4):
                g = half * 4 + g4
                kb.tr(bank[:, g4 * 128:(g4 + 1) * 128], lnbf[:, g * 128:(g + 1) * 128], idb[:],
                      reads=[(lb_n, None), ("idb", None)], writes=[(f"pb{half}", None)],
                      tag=f"btr{c}_{g}")
            kb.cp(rkT_v[:, half * 4:(half + 1) * 4, 1 + c * 128: 1 + (c + 1) * 128],
                  bank[:, 0:512].rearrange("p (g t) -> p g t", g=4), eng="dve",
                  reads=[(f"pb{half}", None)], writes=[("rkT", (c, half))],
                  tag=f"bev{c}_{half}")
        yield
        rk_reads = [("rkT", (c, 0)), ("rkT", (c, 1))]
        for half in range(2) if "B" in ATTNB_PARTS else []:
            for kt in range(8):
                kb.mm(vbank[:], rkT_v[:, kt, 1 + c * 128: 1 + (c + 1) * 128],
                      WwT[:, kt * 1024 + half * 512: kt * 1024 + (half + 1) * 512],
                      start=(kt == 0), stop=(kt == 7),
                      reads=rk_reads + [("WwT", None)], writes=[(vb_n, None)],
                      tag=f"bvm{c}_{half}_{kt}")
            kb.act(vt[:, half * 512:(half + 1) * 512], vbank[:], AF.Copy,
                   reads=[(vb_n, None)], writes=[(vt_n, half)], tag=f"bve{c}_{half}")
            kb.ts(vgw[:, half * 512:(half + 1) * 512], vbank[:], gwcol[:], op0=ALU.mult,
                  reads=[(vb_n, None), ("gwcol", None)], writes=[(vg_n, half)],
                  tag=f"bvg{c}_{half}")
        yield
        for half in range(2) if "C" in ATTNB_PARTS else []:
            bank = pb[half]
            for g4 in range(4):
                g = half * 4 + g4
                kb.tr(bank[:, g4 * 128:(g4 + 1) * 128], rkT_v[:, g, c * 128: (c + 1) * 128],
                      idb[:],
                      reads=[("rkT", (c, half)), ("rkT", (c - 1, half)), ("idb", None)],
                      writes=[(f"pb{half}", None)], tag=f"bwk{c}_{g}")
            kb.cp(wkt[:, half * 512:(half + 1) * 512],
                  bank[:, 0:512], eng="dve",
                  reads=[(f"pb{half}", None)], writes=[(wk_n, half)],
                  tag=f"bwke{c}_{half}")
        yield

    def chunk_tail(c):
        pp = c % 2
        xt = B["xt"][c % 4]; lnt32 = B["lnt32"][pp]
        vt = B["vt"][pp]; vgw = B["vgw"][pp]; wkt = B["wkt"][pp]
        xt_n = f"xtb{c % 4}"; ln_n = f"lnt32{pp}"
        vt_n = f"vt{pp}"; vg_n = f"vgw{pp}"; wk_n = f"wkt{pp}"
        rk_reads = [("rkT", (c, 0)), ("rkT", (c, 1))]
        for half in range(2) if "D" in ATTNB_PARTS else []:
            bank = pf[1 + half]
            for kt in range(8):
                kb.mm(bank[:], rkT_v[:, kt, 1 + c * 128: 1 + (c + 1) * 128],
                      Wbf[:, kt * 1024 + half * 512: kt * 1024 + (half + 1) * 512],
                      start=(kt == 0), stop=(kt == 7),
                      reads=rk_reads + [("Wbf", None)], writes=[(f"pf{1 + half}", None)],
                      tag=f"bim{c}_{half}_{kt}")
        yield
        for kt in range(8) if "E" in ATTNB_PARTS else []:
            kb.mm(pf[3][:, 0:128], rkT_v[:, kt, c * 128:(c + 1) * 128],
                  rkT_v[:, kt, 1 + c * 128: 1 + (c + 1) * 128],
                  start=(kt == 0), stop=(kt == 7),
                  reads=rk_reads + [("rkT", (c - 1, 0)), ("rkT", (c - 1, 1))],
                  writes=[("pf3", None)], tag=f"bsm{c}_{kt}")
        if "E" in ATTNB_PARTS:
            kb.tt(SMT[:], pf[3][:, 0:128], MT[:], ALU.mult,
                  reads=[("pf3", None), ("MT", None)], writes=[("SMT", None)], tag=f"bsmt{c}")
        yield
        for half in range(2) if "E" in ATTNB_PARTS else []:
            kb.mm(pf[4 + half][:], SMT[:], vt[:, half * 512:(half + 1) * 512], True, True,
                  reads=[("SMT", None), (vt_n, half)], writes=[(f"pf{4 + half}", None)],
                  tag=f"bia{c}_{half}")
        yield
        for half in range(2) if "E" in ATTNB_PARTS and "D" in ATTNB_PARTS else []:
            kb.act(scrf[:, 0:512], pf[4 + half][:], AF.Copy,
                   reads=[(f"pf{4 + half}", None), ("scrf", None)], writes=[("scrf", None)],
                   tag=f"bre{c}_{half}")
            kb.stt(rdbf[:, half * 512:(half + 1) * 512], pf[1 + half][:], gpcol[:],
                   scrf[:, 0:512], ALU.mult, ALU.add,
                   reads=[(f"pf{1 + half}", None), ("gpcol", None), ("scrf", None)],
                   writes=[("rdbf", half)], tag=f"brd{c}_{half}")
        yield
        for half in range(2) if "F" in ATTNB_PARTS else []:
            bank = pb[half]
            for g4 in range(4):
                g = half * 4 + g4
                kb.tr(bank[:, g4 * 128:(g4 + 1) * 128], rdbf[:, g * 128:(g + 1) * 128], idb[:],
                      reads=[("rdbf", 0), ("rdbf", 1), ("idb", None)],
                      writes=[(f"pb{half}", None)], tag=f"brT{c}_{g}")
            kb.cp(rdT[:, half * 512:(half + 1) * 512], bank[:, 0:512], eng="dve",
                  reads=[(f"pb{half}", None)], writes=[("rdT", half)], tag=f"brTe{c}_{half}")
        yield
        for half in range(2) if "F" in ATTNB_PARTS else []:
            bank = pf[1 + half]
            for kt in range(8):
                kb.mm(bank[:], rdT[:, kt * 128:(kt + 1) * 128],
                      WrT[:, kt * 1024 + half * 512: kt * 1024 + (half + 1) * 512],
                      start=(kt == 0), stop=(kt == 7),
                      reads=[("rdT", 0), ("rdT", 1), ("WrT", None)],
                      writes=[(f"pf{1 + half}", None)], tag=f"brom{c}_{half}_{kt}")
            kb.stt(scrf[:, 512:1024], bank[:], acol[:], lnt32[:, half * 512:(half + 1) * 512],
                   ALU.mult, ALU.add,
                   reads=[(f"pf{1 + half}", None), ("acol", None), (ln_n, None), ("scrf", None)],
                   writes=[("scrf", None)], tag=f"bxu{c}_{half}")
            kb.tt(xt[:, half * 512:(half + 1) * 512], xt[:, half * 512:(half + 1) * 512],
                  scrf[:, 512:1024], ALU.add,
                  reads=[(xt_n, None), ("scrf", None)], writes=[(xt_n, None)],
                  tag=f"bxa{c}_{half}")
        yield
        kb.dma("sp", xdr[c], xt[:], reads=[(xt_n, None)], writes=[(B["x_dst_name"], c)],
               tag=f"bxs{c}")
        yield
        for jt in range(8) if "G" in ATTNB_PARTS else []:
            for half in range(2):
                bank = pf[4 + half]
                kb.mm(bank[:], wkt[:, jt * 128:(jt + 1) * 128],
                      vgw[:, half * 512:(half + 1) * 512], True, True,
                      reads=[(wk_n, jt // 4), (vg_n, half)], writes=[(f"pf{4 + half}", None)],
                      tag=f"bwu{c}_{jt}_{half}")
                dst = Wst[:, jt * 1024 + half * 512: jt * 1024 + (half + 1) * 512]
                kb.stt(dst, dst, gccol[:], bank[:], ALU.mult, ALU.add,
                       reads=[("Wst", (jt, half)), ("gccol", None), (f"pf{4 + half}", None)],
                       writes=[("Wst", (jt, half))], tag=f"bws{c}_{jt}_{half}")
                kb.cp(Wbf[:, jt * 1024 + half * 512: jt * 1024 + (half + 1) * 512], dst,
                      eng="act", reads=[("Wst", (jt, half)), ("Wbf", (jt, half))],
                      writes=[("Wbf", (jt, half))], tag=f"bwb{c}_{jt}_{half}")


    weave(chunk_front(0))
    for c in range(n_chunks):
        weave(chunk_tail(c), chunk_front(c + 1) if c + 1 < n_chunks else None)

def build_program(dbg=False, stop_after=None):
    nc = bass.Bass("TRN2", target_bir_lowering=False,
                   detect_race_conditions=EMIT_DRAINS)
    kb = KB(nc, dbg=dbg)
    s = kb.s

    x_hbm = kb.din("x", [T, D])
    xr = x_hbm.rearrange("(b p) d -> b p d", p=128)
    consts = prep_consts()
    cin = {k: kb.din("c_" + k, list(v.shape)) for k, v in consts.items()}
    wshapes = {k: list(v.shape) for k, v in prep_weights(_dummy_ins()).items()}
    win = {k: kb.din("w_" + k, v) for k, v in wshapes.items()}
    out_hbm = kb.dout("out", [T, D])
    outr = out_hbm.rearrange("(b p) d -> b p d", p=128)
    ssh_scr = kb.dint("ssh_scr", [8, 128, T], BF16)
    ohn_scr = kb.dint("ohn_scr", [8, 128, T], BF16)
    xa_scr = kb.dint("xa_scr", [T, D], F32)
    xb_scr = kb.dint("xb_scr", [T, D], F32)

    idf = kb.sbuf("idf", [128, 128], F32)
    idb = kb.sbuf("idb", [128, 128], BF16)
    ones_row = kb.sbuf("ones_row", [1, 128], F32)
    ones_row_b = kb.sbuf("ones_row_b", [1, 128], BF16)
    ones_col_b = kb.sbuf("ones_col_b", [128, 1], BF16)
    konst = kb.sbuf("konst", [128, 8], F32)
    maskSL = kb.sbuf("maskSL", [128, 128], BF16)
    maskIU = kb.sbuf("maskIU", [128, 128], BF16)
    lnxf = kb.sbuf("lnxf", [128, 8 * T], BF16)
    m1main = kb.sbuf("m1main", [128, 8 * 1024], BF16)
    bwT = kb.sbuf("bwT", [128, 8 * 4], BF16)
    beta_cols = kb.sbuf("beta_cols", [128, NBLK * 4], F32)
    xin = kb.sbuf("xin", [128, 4 * 1024], F32)
    lntok = kb.sbuf("lntok", [128, 4 * 1024], BF16)
    trash = kb.sbuf("trash", [128, 1024], BF16)
    statc = kb.sbuf("statc", [128, 4 * 8], F32)
    ssh_t = kb.sbuf("ssh_t", [128, 2 * 512], BF16)
    beta_rows = None

    pf = [kb.psum(f"pf{i}", [128, 512], F32) for i in range(6)]
    pb = [kb.psum(f"pb{i}", [128, 1024], BF16) for i in range(2)]

    kb.dma("sp", idf[:], cin["identity"], writes=[("idf", None)])
    kb.dma("pool", idb[:], cin["identity"], writes=[("idb", None)])
    kb.dma("sp", ones_row[:], cin["ones_row"], writes=[("ones_row", None)])
    kb.dma("sp", konst[:], cin["konst"], writes=[("konst", None)])
    kb.dma("pool", ones_row_b[:], cin["ones_row"], writes=[("ones_row_b", None)])
    kb.dma("pool", ones_col_b[:], cin["ones_col"], writes=[("ones_col_b", None)])
    kb.dma("pool", maskSL[:], cin["maskSL"], writes=[("maskSL", None)])
    kb.dma("pool", maskIU[:], cin["maskIU"], writes=[("maskIU", None)])
    kb.dma("pool", m1main[:].rearrange("p (kt o) -> p kt o", kt=8),
           win["m1main"].rearrange("(kt p) o -> p kt o", p=128),
           writes=[("m1main", None)])
    kb.dma("pool", bwT[:].rearrange("p (kt h) -> p kt h", kt=8),
           win["bwT"].rearrange("(kt p) h -> p kt h", p=128),
           writes=[("bwT", None)])

    RL = 4
    for blk in range(NBLK):
        r = blk % RL
        xv = xin[:, r * 1024:(r + 1) * 1024]
        kb.dma("sp", xv, xr[blk], reads=[("xin", r)], writes=[("xin", r)], tag=f"ldx{blk}")
        sc = statc[:, r * 8:(r + 1) * 8]
        kb.act(trash[:], xv, AF.Square, accum_out=sc[:, 0:1],
               reads=[("xin", r)], writes=[("trash", None), ("statc", (r, 0))], tag=f"sq{blk}")
        kb.act(trash[:], xv, AF.Copy, accum_out=sc[:, 1:2],
               reads=[("xin", r)], writes=[("trash", None), ("statc", (r, 1))], tag=f"sum{blk}")
        kb.ts(sc[:, 2:3], sc[:, 1:2], 1.0 / D, reads=[("statc", (r, 1))],
              writes=[("statc", (r, 2))])
        kb.ts(sc[:, 3:4], sc[:, 0:1], 1.0 / D, reads=[("statc", (r, 0))],
              writes=[("statc", (r, 3))])
        kb.tt(sc[:, 4:5], sc[:, 2:3], sc[:, 2:3], ALU.mult,
              reads=[("statc", (r, 2))], writes=[("statc", (r, 4))])
        kb.tt(sc[:, 5:6], sc[:, 3:4], sc[:, 4:5], ALU.subtract,
              reads=[("statc", (r, 3)), ("statc", (r, 4))], writes=[("statc", (r, 5))])
        kb.act(sc[:, 6:7], sc[:, 5:6], AF.Sqrt, bias=konst[:, 0:1],
               reads=[("statc", (r, 5)), ("konst", None)], writes=[("statc", (r, 6))])
        kb.recip(sc[:, 7:8], sc[:, 6:7], reads=[("statc", (r, 6))],
                 writes=[("statc", (r, 7))])
        lv = lntok[:, r * 1024:(r + 1) * 1024]
        kb.ts(lv, xv, sc[:, 2:3], sc[:, 7:8], ALU.subtract, ALU.mult,
              reads=[("xin", r), ("statc", (r, 2)), ("statc", (r, 7)), ("lntok", r)],
              writes=[("lntok", r)], tag=f"ln{blk}")
        for half in range(2):
            bank = pb[half]
            for g4 in range(4):
                g = half * 4 + g4
                kb.tr(bank[:, g4 * 128:(g4 + 1) * 128], lv[:, g * 128:(g + 1) * 128], idb[:],
                      reads=[("lntok", r), ("idb", None)], writes=[(f"pb{half}", None)],
                      tag=f"trln{blk}_{g}")
            lnxf_v = lnxf[:].rearrange("p (g t) -> p g t", g=8)
            kb.cp(lnxf_v[:, half * 4:(half + 1) * 4, blk * 128:(blk + 1) * 128],
                  bank[:, 0:512].rearrange("p (g t) -> p g t", g=4),
                  eng="dve",
                  reads=[(f"pb{half}", None)], writes=[("lnxf", (blk, half))],
                  tag=f"evln{blk}_{half}")

    kb.dump("lnxf", lnxf[:], [("lnxf", None)], cast=True)

    kb.s.retag("xin")
    beta_rows = xin[0:4, 0:T]
    for t in range(NT):
        bank = pf[t % 2]
        for kt in range(8):
            kb.mm(bank[0:4, 0:512], bwT[:, kt * 4:(kt + 1) * 4],
                  lnxf[:].rearrange("p (g t) -> p g t", g=8)[:, kt, t * TT:(t + 1) * TT],
                  start=(kt == 0), stop=(kt == 7),
                  reads=[("bwT", None)] + [("lnxf", (t * 4 + i, h)) for i in range(4) for h in range(2)],
                  writes=[(f"pf{t % 2}", None)], tag=f"betamm{t}_{kt}")
        kb.act(beta_rows[:, t * TT:(t + 1) * TT], bank[0:4, 0:512], AF.Sigmoid,
               reads=[(f"pf{t % 2}", None)], writes=[("xin", ("beta", t))], tag=f"betasig{t}")
        kb.ts(beta_rows[:, t * TT:(t + 1) * TT], beta_rows[:, t * TT:(t + 1) * TT],
              1e-6, op0=ALU.max, reads=[("xin", ("beta", t))], writes=[("xin", ("beta", t))])
    for q in range(4):
        bank = pf[2]
        for i in range(4):
            blk = q * 4 + i
            kb.tr(bank[:, i * 4:(i + 1) * 4], beta_rows[0:4, blk * 128:(blk + 1) * 128],
                  idf[0:4, 0:4],
                  reads=[("xin", ("beta", blk // 4)), ("idf", None)], writes=[("pf2", None)],
                  tag=f"trbeta{blk}")
        kb.cp(beta_cols[:, q * 16:(q + 1) * 16], bank[:, 0:16], eng="dve",
              reads=[("pf2", None)], writes=[("beta_cols", q)], tag=f"evbeta{q}")
    kb.dump("beta_cols", beta_cols[:], [("beta_cols", None)])

    lnxf_v = lnxf[:].rearrange("p (g t) -> p g t", g=8)
    for t in range(NT):
        ln_reads = [("lnxf", (t * 4 + i, h)) for i in range(4) for h in range(2)]
        for go in range(8):
            bank = pf[go % 2]
            for kt in range(8):
                kb.mm(bank[:], m1main[:, kt * 1024 + go * 128: kt * 1024 + (go + 1) * 128],
                      lnxf_v[:, kt, t * TT:(t + 1) * TT],
                      start=(kt == 0), stop=(kt == 7),
                      reads=[("m1main", None)] + ln_reads,
                      writes=[(f"pf{go % 2}", None)], tag=f"sshmm{t}_{go}_{kt}")
            rr = (t * 8 + go) % 2
            sshv = ssh_t[:, rr * 512:(rr + 1) * 512]
            kb.act(sshv, bank[:], AF.Copy,
                   reads=[(f"pf{go % 2}", None), ("ssh_t", rr)], writes=[("ssh_t", rr)],
                   tag=f"sshev{t}_{go}")
            kb.dma("sp", ssh_scr[go, :, t * TT:(t + 1) * TT], sshv,
                   reads=[("ssh_t", rr)], writes=[("ssh_scr", (t, go))], tag=f"sshst{t}_{go}")

    if stop_after == "phaseA":
        return nc, kb

    wq = kb.sbuf("wq", [128, 2048], BF16)
    wk = kb.sbuf("wk", [128, 2048], BF16)
    wv = kb.sbuf("wv", [128, 2048], BF16)
    qcw = kb.sbuf("qcw", [128, 32], F32)
    kcw = kb.sbuf("kcw", [128, 32], F32)
    vcw = kb.sbuf("vcw", [128, 32], F32)
    m1tail = kb.sbuf("m1tail", [16, 1024], BF16)
    m2T = kb.sbuf("m2T", [128, 32], BF16)
    b1f = kb.sbuf("b1f", [128, 8], F32)
    normw_bc = kb.sbuf("normw_bc", [128, 256], F32)
    crl_sig = kb.sbuf("crl_sig", [128, 4], F32)
    trec = kb.sbuf("trec", [128, 1], F32)
    srow = kb.sbuf("srow", [1, 8], F32)
    nbeta_cols = kb.sbuf("nbeta_cols", [128, NBLK * 4], F32)
    pre = kb.sbuf("pre", [128, 2 * 516], BF16)
    qf = kb.sbuf("qf", [128, 2 * 2048], BF16)
    kf = kb.sbuf("kf", [128, 2 * 2048], BF16)
    vfp = kb.sbuf("vfp", [128, 2 * 2112], BF16)
    kn_tok = kb.sbuf("kn_tok", [128, 4096], BF16)
    kbn_tok = kb.sbuf("kbn_tok", [128, 4096], BF16)
    v_tok = kb.sbuf("v_tok", [128, 4096], BF16)
    vb_tok = kb.sbuf("vb_tok", [128, 4096], BF16)
    copyA = kb.sbuf("copyA", [128, 4096], BF16)
    copyB = kb.sbuf("copyB", [128, 17 * 256], BF16)
    dout = kb.sbuf("dout", [128, 4096], BF16)
    ls_tok = qf
    ll_tok = kf
    S32 = kb.sbuf("S32", [128, 512], F32)
    Sbf = kb.sbuf("Sbf", [128, 512], BF16)
    dpar = []
    for pp in range(2):
        d = {}
        for nm, w_ in (("Y32", 128), ("Ybf", 128), ("Pt", 128), ("PTt", 128),
                       ("Q0", 128), ("QT0", 128), ("Q1", 128), ("QT1", 128),
                       ("attnT", 128), ("wT_sb", 256), ("u_sb", 256), ("uadj", 256)):
            dt_ = F32 if nm == "Y32" else BF16
            d[nm] = kb.sbuf(f"{nm}_{pp}", [128, w_], dt_)
        dpar.append(d)
    sq = trash
    rq = kb.sbuf("rq", [1, 512], F32)
    stcr = [kb.sbuf(f"stcr{p}", [128, 16], F32) for p in range(2)]
    stc = [kb.sbuf(f"stc{p}", [128, 16], F32) for p in range(2)]
    statsT = kb.sbuf("statsT", [16, 512], BF16)
    h1s = kb.sbuf("h1s", [128, 1024], BF16)
    sshd = kb.sbuf("sshd", [128, 4096], BF16)
    h1f = copyB
    sshin = None
    lgr = kb.sbuf("lgr", [4, 512], F32)
    lgc = kb.sbuf("lgc", [128, 16], F32)
    pc = kb.sbuf("pc", [128, 16], F32)
    pcol = [kb.sbuf(f"pcol{p}", [128, 1], F32) for p in range(2)]
    o_t = [kb.sbuf(f"o_t{p}", [128, 256], BF16) for p in range(2)]
    o_t2 = [kb.sbuf(f"o_t2{p}", [128, 256], BF16) for p in range(2)]
    ohn_t = kb.sbuf("ohn_t", [128, 1024], BF16)
    trash2 = trash

    kb.dma("pool", m1tail[:].rearrange("s (kt o) -> s kt o", kt=8),
           win["m1tail"].rearrange("s (kt o) -> s kt o", kt=8), writes=[("m1tail", None)])
    kb.dma("pool", m2T[:].rearrange("p (kt f) -> p kt f", kt=8),
           win["m2T"].rearrange("(kt p) f -> p kt f", p=128), writes=[("m2T", None)])
    kb.dma("sp", b1f[:], win["b1f"], writes=[("b1f", None)])
    for nm, wkey in (("qcw", "qcw"), ("kcw", "kcw"), ("vcw", "vcw")):
        kb.dma("sp", kb.sb[nm][:], win[wkey].rearrange("p g j -> p (g j)"),
               writes=[(nm, None)])
    nwrow = xin[0:1, 2048:2304]
    kb.s.retag("xin")
    kb.dma("sp", nwrow, win["normw_row"], reads=[("xin", "nw")], writes=[("xin", "nw")])
    kb.mm(pf[5][:, 0:256], ones_row[:], nwrow, True, True,
          reads=[("ones_row", None), ("xin", "nw")], writes=[("pf5", None)], tag="nwbc")
    kb.cp(normw_bc[:], pf[5][:, 0:256], eng="dve", reads=[("pf5", None)],
          writes=[("normw_bc", None)])
    crl_row = kb.sbuf("crl_row", [1, 4], F32)
    kb.dma("sp", crl_row[:], win["crl_row"], writes=[("crl_row", None)])
    kb.act(crl_row[:], crl_row[:], AF.Sigmoid, reads=[("crl_row", None)],
           writes=[("crl_row", None)])
    kb.mm(pf[5][:, 256:260], ones_row[:], crl_row[:], True, True,
          reads=[("ones_row", None), ("crl_row", None)], writes=[("pf5", None)], tag="crlbc")
    kb.cp(crl_sig[:], pf[5][:, 256:260], eng="dve", reads=[("pf5", None)],
          writes=[("crl_sig", None)])
    ltrow = kb.sbuf("ltrow", [1, 4], F32)
    kb.dma("sp", ltrow[:, 0:1], win["ltemp"], writes=[("ltrow", 0)])
    kb.act(ltrow[:, 1:2], ltrow[:, 0:1], AF.Abs, reads=[("ltrow", 0)], writes=[("ltrow", 1)])
    kb.act(ltrow[:, 1:2], ltrow[:, 1:2], AF.Exp, scale=-1.0, reads=[("ltrow", 1)], writes=[("ltrow", 1)])
    kb.act(ltrow[:, 1:2], ltrow[:, 1:2], AF.Ln, bias=konst[0:1, 2:3], reads=[("ltrow", 1), ("konst", None)], writes=[("ltrow", 1)])
    kb.act(ltrow[:, 2:3], ltrow[:, 0:1], AF.Relu, reads=[("ltrow", 0)], writes=[("ltrow", 2)])
    kb.tt(ltrow[:, 3:4], ltrow[:, 1:2], ltrow[:, 2:3], ALU.add,
          reads=[("ltrow", 1), ("ltrow", 2)], writes=[("ltrow", 3)])
    kb.recip(ltrow[:, 3:4], ltrow[:, 3:4], reads=[("ltrow", 3)], writes=[("ltrow", 3)])
    kb.mm(pf[5][:, 260:261], ones_row[:], ltrow[:, 3:4], True, True,
          reads=[("ones_row", None), ("ltrow", 3)], writes=[("pf5", None)], tag="trbc")
    kb.cp(trec[:], pf[5][:, 260:261], eng="dve", reads=[("pf5", None)], writes=[("trec", None)])
    m2b_bc = kb.sbuf("m2b_bc", [128, 4], F32)
    m2brow = kb.sbuf("m2brow", [1, 4], F32)
    kb.dma("sp", m2brow[:], win["m2b_row"], writes=[("m2brow", None)])
    kb.mm(pf[5][:, 261:265], ones_row[:], m2brow[:], True, True,
          reads=[("ones_row", None), ("m2brow", None)], writes=[("pf5", None)], tag="m2bc")
    kb.cp(m2b_bc[:], pf[5][:, 261:265], eng="dve", reads=[("pf5", None)],
          writes=[("m2b_bc", None)])
    kb.ts(nbeta_cols[:], beta_cols[:], -1.0, reads=[("beta_cols", None)],
          writes=[("nbeta_cols", None)])

    qf_v = qf[:].rearrange("p (g t) -> p g t", g=2)
    kf_v = kf[:].rearrange("p (g t) -> p g t", g=2)
    vfp_v = vfp[:].rearrange("p (g t) -> p g t", g=2)

    def load_head_weights(h):
        for nm, wsrc in (("wq", "qwT"), ("wk", "kwT"), ("wv", "vwT")):
            dst = kb.sb[nm][:].rearrange("p (kt c) -> p kt c", kt=8)
            srcv = win[wsrc].rearrange("(kt p) c -> p kt c", p=128)[:, :, h * 256:(h + 1) * 256]
            kb.dma("pool", dst, srcv, reads=[(nm, None)], writes=[(nm, None)],
                   tag=f"ldw_{nm}{h}")

    def proj_conv(h, tensor, wname, cwname, dstv, dst_name, pad):
        for g in range(2):
            for t in range(NT):
                bank = pf[t % 2]
                bank_res = f"pf{t % 2}"
                w_sb = kb.sb[wname]
                for kt in range(8):
                    kb.mm(bank[:], w_sb[:, kt * 256 + g * 128: kt * 256 + (g + 1) * 128],
                          lnxf_v[:, kt, t * TT:(t + 1) * TT],
                          start=(kt == 0), stop=(kt == 7),
                          reads=[(wname, None)] + [("lnxf", (t * 4 + i2, hh)) for i2 in range(4) for hh in range(2)],
                          writes=[(bank_res, None)], tag=f"pj{tensor}{h}{g}{t}")
                e = (g * NT + t) % 2
                pe_sl = pre[:, e * 516:(e + 1) * 516]
                kb.act(pe_sl[:, 4:516], bank[:], AF.Copy,
                       reads=[(bank_res, None), ("pre", e)], writes=[("pre", e)],
                       tag=f"pcv{tensor}{h}{g}{t}")
                if t == 0:
                    kb.memset(pe_sl[:, 0:4], 0.0, reads=[("pre", e)], writes=[("pre", e)])
                else:
                    ep = (g * NT + t - 1) % 2
                    kb.cp(pe_sl[:, 0:4], pre[:, ep * 516 + 512: ep * 516 + 516],
                          eng="dve", reads=[("pre", ep), ("pre", e)], writes=[("pre", e)])
                dsl = dstv[:, g, pad + t * TT: pad + (t + 1) * TT]
                cw = kb.sb[cwname]
                gg = h * 2 + g
                kb.ts(dsl, pe_sl[:, 1:513], cw[:, gg * 4 + 0: gg * 4 + 1],
                      op0=ALU.mult,
                      reads=[("pre", e), (cwname, None), (dst_name, (g, t))],
                      writes=[(dst_name, (g, t))], tag=f"cv0{tensor}{h}{g}{t}")
                for j in range(1, 4):
                    kb.stt(dsl, pe_sl[:, 1 + j: 513 + j], cw[:, gg * 4 + j: gg * 4 + j + 1],
                           dsl, ALU.mult, ALU.add,
                           reads=[("pre", e), (cwname, None), (dst_name, (g, t))],
                           writes=[(dst_name, (g, t))], tag=f"cv{j}{tensor}{h}{g}{t}")
                kb.act(dsl, dsl, AF.Silu, reads=[(dst_name, (g, t))],
                       writes=[(dst_name, (g, t))], tag=f"silu{tensor}{h}{g}{t}")

    def l2norm(h, srcv, src_name, pad):
        for t in range(NT):
            r = 0
            for g in range(2):
                kb.act(sq[:, g * 512:(g + 1) * 512], srcv[:, g, pad + t * TT: pad + (t + 1) * TT],
                       AF.Square, reads=[(src_name, (g, t))], writes=[("trash", None)],
                       tag=f"l2sq{src_name}{h}{t}{g}")
            for g in range(2):
                kb.mm(pf[2][0:1, 0:512], ones_col_b[:], sq[:, g * 512:(g + 1) * 512],
                      start=(g == 0), stop=(g == 1),
                      reads=[("ones_col_b", None), ("trash", None)],
                      writes=[("pf2", None)], tag=f"l2mm{src_name}{h}{t}{g}")
            rsl = rq[:, r * 512:(r + 1) * 512]
            kb.act(rsl, pf[2][0:1, 0:512], AF.Sqrt, bias=konst[0:1, 1:2],
                   reads=[("pf2", None), ("konst", None), ("rq", r)], writes=[("rq", r)],
                   tag=f"l2rt{src_name}{h}{t}")
            kb.recip(rsl, rsl, reads=[("rq", r)], writes=[("rq", r)])
            kb.mm(pf[2][:, 512 - 512:512] if False else pf[2][:, 0:512],
                  ones_row[:], rsl, True, True,
                  reads=[("ones_row", None), ("rq", r)], writes=[("pf2", None)],
                  tag=f"l2bc{src_name}{h}{t}")
            for g in range(2):
                dsl = srcv[:, g, pad + t * TT: pad + (t + 1) * TT]
                kb.tt(dsl, dsl, pf[2][:, 0:512], ALU.mult,
                      reads=[(src_name, (g, t)), ("pf2", None)],
                      writes=[(src_name, (g, t))], tag=f"l2ap{src_name}{h}{t}{g}")

    def transpose_blocks(h, srcv, src_name, src_col0, dsts, blk_range=range(NBLK),
                         col_of_blk=None, tag=""):
        W_src = srcv.shape[2]
        for blk in blk_range:
            bank = pb[blk % 2]
            bank_res = f"pb{blk % 2}"
            c0 = col_of_blk(blk) if col_of_blk else (src_col0 + blk * 128)
            W = min(128, W_src - c0)
            for g in range(2):
                kb.tr(bank[0:W, g * 128:(g + 1) * 128], srcv[:, g, c0:c0 + W], idb[:],
                      reads=[(src_name, None), ("idb", None)], writes=[(bank_res, None)],
                      tag=f"tr{tag}{h}_{blk}_{g}")
            for (dst, dst_name, kind, scol_fn) in dsts:
                dsl = dst[0:W, blk * 256:(blk + 1) * 256]
                if kind == "copy":
                    kb.cp(dsl, bank[0:W, 0:256], eng="dve",
                          reads=[(bank_res, None)], writes=[(dst_name, blk)],
                          tag=f"ev{tag}{dst_name}{h}_{blk}")
                else:
                    kb.ts(dsl, bank[0:W, 0:256], scol_fn(blk), op0=ALU.mult,
                          reads=[(bank_res, None), ("beta_cols", None)],
                          writes=[(dst_name, blk)], tag=f"ev{tag}{dst_name}{h}_{blk}")

    def delta_env(c):
        pp = c % 2
        D_ = dpar[pp]
        def RN(b):
            return f"{b}_{pp}"
        bA, bB, bC = (("pf0", pf[0]), ("pf1", pf[1]), ("pf2", pf[2])) if pp == 0 \
            else (("pf3", pf[3]), ("pf4", pf[4]), ("pf5", pf[5]))
        pbank = ("pb0", pb[0]) if pp == 0 else ("pb1", pb[1])
        return D_, RN, bA, bB, bC, pbank

    def delta_pre(h, c):
        D_, RN, bA, bB, bC, pbank = delta_env(c)
        Ptp, PTtp, Ybfp = D_["Pt"], D_["PTt"], D_["Ybf"]
        u_p, wTp = D_["u_sb"], D_["wT_sb"]
        nbcol = nbeta_cols[:, (c * 4 + h):(c * 4 + h) + 1]
        ccols = slice(c * 128, (c + 1) * 128)
        for g in range(2):
            kb.mm(bA[1][:, 0:128], kf_v[:, g, ccols], kf_v[:, g, ccols],
                  start=(g == 0), stop=(g == 1),
                  reads=[("kf", None)], writes=[(bA[0], None)], tag=f"dM{h}_{c}")
        yield
        kb.stt(Ptp[:], bA[1][:, 0:128], nbcol, maskSL[:], ALU.mult, ALU.mult,
               reads=[(bA[0], None), ("nbeta_cols", None), ("maskSL", None)],
               writes=[(RN("Pt"), None)], tag=f"dP{h}_{c}")
        yield
        kb.tr(pbank[1][:, 0:128], Ptp[:], idb[:], reads=[(RN("Pt"), None), ("idb", None)],
              writes=[(pbank[0], None)], tag=f"dPT{h}_{c}")
        yield
        kb.cp(PTtp[:], pbank[1][:, 0:128], eng="dve", reads=[(pbank[0], None)],
              writes=[(RN("PTt"), None)], tag=f"dPTe{h}_{c}")
        yield
        kb.tt(Ybfp[:], idf[:], PTtp[:], ALU.add, reads=[("idf", None), (RN("PTt"), None)],
              writes=[(RN("Ybf"), None)], tag=f"dY0{h}_{c}")
        yield
        qcur, qtcur = (RN("Pt"), RN("PTt"))
        bufs = [(RN("Q0"), RN("QT0")), (RN("Q1"), RN("QT1"))]
        for i in range(1, 7):
            qn, qtn = bufs[i % 2]
            kb.mm(bA[1][:, 128:256], kb.sb[qtcur][:], kb.sb[qcur][:], True, True,
                  reads=[(qtcur, None), (qcur, None)], writes=[(bA[0], None)],
                  tag=f"dQ{h}_{c}_{i}")
            kb.mm(bB[1][:, 0:128], kb.sb[qcur][:], kb.sb[qtcur][:], True, True,
                  reads=[(qcur, None), (qtcur, None)], writes=[(bB[0], None)],
                  tag=f"dQT{h}_{c}_{i}")
            yield
            kb.cp(kb.sb[qn][:], bA[1][:, 128:256], eng="dve",
                  reads=[(bA[0], None)], writes=[(qn, None)], tag=f"dQe{h}_{c}_{i}")
            kb.cp(kb.sb[qtn][:], bB[1][:, 0:128], eng="act",
                  reads=[(bB[0], None)], writes=[(qtn, None)], tag=f"dQTe{h}_{c}_{i}")
            yield
            kb.mm(bC[1][:, 0:128], kb.sb[qn][:], Ybfp[:], True, True,
                  reads=[(qn, None), (RN("Ybf"), None)], writes=[(bC[0], None)],
                  tag=f"dYm{h}_{c}_{i}")
            yield
            kb.tt(Ybfp[:], Ybfp[:], bC[1][:, 0:128], ALU.add,
                  reads=[(RN("Ybf"), None), (bC[0], None)], writes=[(RN("Ybf"), None)],
                  tag=f"dYa{h}_{c}_{i}")
            yield
            qcur, qtcur = qn, qtn
        kb.mm(bA[1][:, 0:256], Ybfp[:], vb_tok[:, c * 256:(c + 1) * 256], True, True,
              reads=[(RN("Ybf"), None), ("v_tok_b", c)], writes=[(bA[0], None)],
              tag=f"du{h}_{c}")
        yield
        kb.act(u_p[:], bA[1][:, 0:256], AF.Copy, reads=[(bA[0], None)],
               writes=[(RN("u_sb"), None)], tag=f"due{h}_{c}")
        yield
        for g in range(2):
            kb.mm(bB[1][:, g * 128:(g + 1) * 128],
                  kbn_tok[:, c * 256 + g * 128: c * 256 + (g + 1) * 128], Ybfp[:],
                  True, True,
                  reads=[("kbn_tok", c), (RN("Ybf"), None)], writes=[(bB[0], None)],
                  tag=f"dwT{h}_{c}_{g}")
        yield
        kb.cp(wTp[:], bB[1][:, 0:256], eng="dve", reads=[(bB[0], None)],
              writes=[(RN("wT_sb"), None)], tag=f"dwTe{h}_{c}")
        yield

    def delta_tail(h, c):
        D_, RN, bA, bB, bC, pbank = delta_env(c)
        wTp, u_p, uadjp, attnTp = D_["wT_sb"], D_["u_sb"], D_["uadj"], D_["attnT"]
        ccols = slice(c * 128, (c + 1) * 128)
        for g in range(2):
            kb.mm(bC[1][:, 0:256], wTp[:, g * 128:(g + 1) * 128],
                  Sbf[:, g * 256:(g + 1) * 256], start=(g == 0), stop=(g == 1),
                  reads=[(RN("wT_sb"), None), ("Sbf", None)], writes=[(bC[0], None)],
                  tag=f"dwS{h}_{c}_{g}")
        yield
        kb.stt(uadjp[:], bC[1][:, 0:256], -1.0, u_p[:], ALU.mult, ALU.add,
               reads=[(bC[0], None), (RN("u_sb"), None)], writes=[(RN("uadj"), None)],
               tag=f"duadj{h}_{c}")
        yield
        for g in range(2):
            kb.mm(bA[1][:, 0:128], kf_v[:, g, ccols], qf_v[:, g, ccols],
                  start=(g == 0), stop=(g == 1),
                  reads=[("kf", None), ("qf", None)], writes=[(bA[0], None)],
                  tag=f"dat{h}_{c}_{g}")
        yield
        kb.tt(attnTp[:], bA[1][:, 0:128], maskIU[:], ALU.mult,
              reads=[(bA[0], None), ("maskIU", None)], writes=[(RN("attnT"), None)],
              tag=f"datm{h}_{c}")
        yield
        for g in range(2):
            kb.mm(bB[1][:, 0:256], qf_v[:, g, ccols], Sbf[:, g * 256:(g + 1) * 256],
                  start=(g == 0), stop=False,
                  reads=[("qf", None), ("Sbf", None)], writes=[(bB[0], None)],
                  tag=f"dqS{h}_{c}_{g}")
        yield
        kb.mm(bB[1][:, 0:256], attnTp[:], uadjp[:], start=False, stop=True,
              reads=[(RN("attnT"), None), (RN("uadj"), None)], writes=[(bB[0], None)],
              tag=f"dou{h}_{c}")
        yield
        kb.act(dout[:, c * 256:(c + 1) * 256], bB[1][:, 0:256], AF.Copy,
               reads=[(bB[0], None)], writes=[("dout", c)], tag=f"doute{h}_{c}")
        yield
        for g in range(2):
            kb.mm(bC[1][:, 0:256] if g == 0 else bC[1][:, 256:512],
                  kn_tok[:, c * 256 + g * 128: c * 256 + (g + 1) * 128], uadjp[:],
                  True, True,
                  reads=[("kn_tok", c), (RN("uadj"), None)], writes=[(bC[0], None)],
                  tag=f"dSu{h}_{c}_{g}")
        yield
        kb.tt(S32[:], S32[:], bC[1][:], ALU.add,
              reads=[("S32", None), (bC[0], None)], writes=[("S32", None)],
              tag=f"dSa{h}_{c}")
        yield
        kb.act(Sbf[:], S32[:], AF.Copy, reads=[("S32", None)], writes=[("Sbf", None)],
               tag=f"dSb{h}_{c}")
        yield

    def toep_ld_quarter(h, toep_win, q, tag):
        buf = q % 2
        stage = sshd[0:127, :].bitcast(F32)
        for piece in range(2):
            ch0 = h * 256 + q * 64 + piece * 32
            srcp = toep_win[:, ch0 * 64:(ch0 + 32) * 64]
            kb.dma("sp", stage, srcp,
                   reads=[("sshd", None)], writes=[("sshd", None)],
                   tag=f"ldtoep{tag}{h}_{q}_{piece}")
            kb.cp(m1main[0:127, buf * 4096 + piece * 2048: buf * 4096 + (piece + 1) * 2048],
                  stage, eng="dve",
                  reads=[("sshd", None), ("m1main", buf)], writes=[("m1main", buf)],
                  tag=f"cstoep{tag}{h}_{q}_{piece}")

    def fir(h, toep_win, dst, dst_name, tag, prefetched=False):
        def ld_quarter(q):
            toep_ld_quarter(h, toep_win, q, tag)
        if not prefetched:
            ld_quarter(0)
            ld_quarter(1)
        for q in range(4):
            buf = q % 2
            if q + 2 < 4:
                pass
            for ci in range(64):
                cih = q * 64 + ci
                bank = pf[(cih // 32) % 2]
                bank_res = f"pf{(cih // 32) % 2}"
                cslot = cih % 32
                lhsT = m1main[0:127, buf * 4096 + ci * 64: buf * 4096 + (ci + 1) * 64]
                rA = copyA[0:127, :].rearrange("s (m c) -> s m c", c=256)[:, :, cih:cih + 1]
                rB = copyB[0:127, :].rearrange("s (m c) -> s m c", c=256)[:, 1:17, cih:cih + 1]
                kb.mm(bank[0:64, cslot * 16:(cslot + 1) * 16], lhsT, rA, True, True,
                      reads=[("m1main", buf), ("copyA", None)], writes=[(bank_res, cslot // 8)],
                      tag=f"fA{tag}{h}_{cih}")
                kb.mm(bank[64:128, cslot * 16:(cslot + 1) * 16], lhsT, rB, True, True,
                      reads=[("m1main", buf), ("copyB", None)], writes=[(bank_res, cslot // 8)],
                      tile_position=(0, 64), tag=f"fB{tag}{h}_{cih}")
                if cslot == 31:
                    c0 = cih - 31
                    dstap = dst[:, :].rearrange("p (m c) -> p m c", c=256)[:, :, c0:c0 + 32]
                    srcap = bank[:, 0:512].rearrange("p (c m) -> p c m", m=16)
                    kb.cp(dstap, srcap.rearrange("p c m -> p m c"), eng="dve",
                          reads=[(bank_res, 0), (bank_res, 1), (bank_res, 2), (bank_res, 3)],
                          writes=[(dst_name, cih // 32)], tag=f"fE{tag}{h}_{cih}")
            if q + 2 < 4:
                ld_quarter(q + 2)

    for h in range(H):
        for rn in ("qf", "kf", "vfp", "kn_tok", "kbn_tok", "v_tok", "v_tok_b",
                   "copyA", "copyB", "dout"):
            kb.s.retag(rn)
        load_head_weights(h)
        proj_conv(h, "q", "wq", "qcw", qf_v, "qf", 0)
        proj_conv(h, "k", "wk", "kcw", kf_v, "kf", 0)
        proj_conv(h, "v", "wv", "vcw", vfp_v, "vfp", 64)
        for g in range(2):
            kb.memset(vfp_v[:, g, 0:64], 0.0, reads=[], writes=[("vfp", (g, "pad"))],
                      tag=f"vpad{h}{g}")
        l2norm(h, qf_v, "qf", 0)
        l2norm(h, kf_v, "kf", 0)
        transpose_blocks(h, kf_v, "kf", 0,
                         [(kn_tok, "kn_tok", "copy", None),
                          (kbn_tok, "kbn_tok", "ts", lambda blk: beta_cols[:, blk * 4 + h: blk * 4 + h + 1])],
                         tag="kn")
        transpose_blocks(h, vfp_v, "vfp", 64,
                         [(v_tok, "v_tok", "copy", None),
                          (vb_tok, "v_tok_b", "ts", lambda blk: beta_cols[:, blk * 4 + h: blk * 4 + h + 1])],
                         col_of_blk=lambda blk: 64 + blk * 128, tag="vt")
        transpose_blocks(h, vfp_v, "vfp", 0, [(copyA, "copyA", "copy", None)],
                         col_of_blk=lambda blk: blk * 128 + 1, tag="ca")
        transpose_blocks(h, vfp_v, "vfp", 0, [(copyB, "copyB", "copy", None)],
                         blk_range=range(1, 17),
                         col_of_blk=lambda blk: blk * 128 - 63, tag="cb")
        kb.memset(S32[:], 0.0, reads=[], writes=[("S32", None)], tag=f"S0{h}")
        kb.memset(Sbf[:], 0.0, reads=[], writes=[("Sbf", None)], tag=f"S0b{h}")
        toep_ld_quarter(h, win["toep_l"], 0, "l")
        toep_ld_quarter(h, win["toep_l"], 1, "l")
        weave(delta_pre(h, 0))
        for c in range(NBLK):
            weave(delta_tail(h, c),
                  delta_pre(h, c + 1) if c + 1 < NBLK else None)
        kb.s.retag("qf"); kb.s.retag("kf")
        fir(h, win["toep_l"], ll_tok, "kf", "l", prefetched=True)
        fir(h, win["toep_s"], ls_tok, "qf", "s")
        if dbg and stop_after == "head0" and h == 0:
            kb.dump("qf", qf[:], [("qf", None)], cast=True)
            kb.dump("kf", kf[:], [("kf", None)], cast=True)
            kb.dump("vfp", vfp[:], [("vfp", None)], cast=True)
            kb.dump("v_tok", v_tok[:], [("v_tok", None)], cast=True)
            kb.dump("kn_tok", kn_tok[:], [("kn_tok", None)], cast=True)
            kb.dump("dout", dout[:], [("dout", None)], cast=True)
            kb.dump("ll_tok", ll_tok[:], [("kf", None)], cast=True)
            kb.dump("ls_tok", ls_tok[:], [("qf", None)], cast=True)
        attn_c_phase(kb, h, locals())
        if dbg and stop_after == "head0" and h == 0:
            return nc, kb

    kb.s.retag("lnxf")
    kb.s.retag("xin")
    owT_sb = lnxf[:, 0:8192]
    ohnin = lnxf[:, 8192:12288]
    attn_tail(kb, locals())
    if stop_after == "attna":
        return nc, kb

    kb.free_from("lnxf")

    def alloc_ffn(sfx):
        R = lambda b: b + sfx
        Gd = {"R": R}
        Gd["outf"] = kb.sbuf(R("outf"), [128, 8 * T], F32)
        Gd["cn1"] = kb.sbuf(R("cn1"), [128, 8 * 1024], BF16)
        Gd["cn2"] = kb.sbuf(R("cn2"), [128, 8 * 512], BF16)
        Gd["cn3"] = kb.sbuf(R("cn3"), [128, 4 * 256], BF16)
        Gd["cn4"] = kb.sbuf(R("cn4"), [128, 2], BF16)
        Gd["cb1f"] = kb.sbuf(R("cb1f"), [128, 8], F32)
        Gd["cb2f"] = kb.sbuf(R("cb2f"), [128, 4], F32)
        Gd["cb3f"] = kb.sbuf(R("cb3f"), [128, 2], F32)
        Gd["cb4"] = kb.sbuf(R("cb4"), [1, 1], F32)
        Gd["gw1"] = kb.sbuf(R("gw1"), [128, 8 * 512], BF16)
        Gd["gw2"] = kb.sbuf(R("gw2"), [128, 4 * 256], BF16)
        Gd["gw3"] = kb.sbuf(R("gw3"), [128, 2], BF16)
        Gd["gb1"] = kb.sbuf(R("gb1"), [1, 512], F32)
        Gd["gb2"] = kb.sbuf(R("gb2"), [1, 256], F32)
        Gd["gb3"] = kb.sbuf(R("gb3"), [1, 1], F32)
        Gd["arr"] = kb.sbuf(R("arr"), [1, 1], F32)
        Gd["xt"] = kb.sbuf(R("xt"), [128, 4 * 1024], F32)
        Gd["lnt"] = kb.sbuf(R("lnt"), [128, 2 * 1024], BF16)
        Gd["trash"] = kb.sbuf(R("trash"), [128, 1024], BF16)
        Gd["statf"] = kb.sbuf(R("statf"), [128, 32], F32)
        Gd["ob"] = [kb.sbuf(R(f"ob{p}"), [128, 8 * 512], BF16) for p in range(2)]
        Gd["h1t"] = [kb.sbuf(R(f"h1t{p}"), [128, 8 * 512], BF16) for p in range(2)]
        Gd["h2t"] = [kb.sbuf(R(f"h2t{p}"), [128, 4 * 512], BF16) for p in range(2)]
        Gd["h3t"] = [kb.sbuf(R(f"h3t{p}"), [128, 2 * 512], BF16) for p in range(2)]
        Gd["gsc"] = kb.sbuf(R("gsc"), [1, 3640], F32)
        Gd["gcf"] = kb.sbuf(R("gcf"), [128, 8], BF16)
        Gd["grow"] = kb.sbuf(R("grow"), [1, 512], F32)
        Gd["tw"] = [kb.sbuf(R(f"tw{p}"), [128, 512], BF16) for p in range(2)]
        kb.dma("pool", Gd["cn1"][:].rearrange("p (kt o) -> p kt o", kt=8),
               win["cn_w1"].rearrange("(kt p) o -> p kt o", p=128), writes=[(R("cn1"), None)])
        kb.dma("pool", Gd["cn2"][:].rearrange("p (kt o) -> p kt o", kt=8),
               win["cn_w2"].rearrange("(kt p) o -> p kt o", p=128), writes=[(R("cn2"), None)])
        kb.dma("pool", Gd["cn3"][:].rearrange("p (kt o) -> p kt o", kt=4),
               win["cn_w3"].rearrange("(kt p) o -> p kt o", p=128), writes=[(R("cn3"), None)])
        kb.dma("pool", Gd["cn4"][:], win["cn_w4"], writes=[(R("cn4"), None)])
        kb.dma("pool", Gd["gw1"][:].rearrange("p (kt o) -> p kt o", kt=8),
               win["gc_w1"].rearrange("(kt p) o -> p kt o", p=128), writes=[(R("gw1"), None)])
        kb.dma("pool", Gd["gw2"][:].rearrange("p (kt o) -> p kt o", kt=4),
               win["gc_w2"].rearrange("(kt p) o -> p kt o", p=128), writes=[(R("gw2"), None)])
        kb.dma("pool", Gd["gw3"][:], win["gc_w3"], writes=[(R("gw3"), None)])
        kb.dma("sp", Gd["cb1f"][:], win["cb1f"], writes=[(R("cb1f"), None)])
        kb.dma("sp", Gd["cb2f"][:], win["cb2f"], writes=[(R("cb2f"), None)])
        kb.dma("sp", Gd["cb3f"][:], win["cb3f"], writes=[(R("cb3f"), None)])
        kb.dma("sp", Gd["cb4"][:], win["cb4"], writes=[(R("cb4"), None)])
        kb.dma("sp", Gd["gb1"][:], win["gb1_row"], writes=[(R("gb1"), None)])
        kb.dma("sp", Gd["gb2"][:], win["gb2_row"], writes=[(R("gb2"), None)])
        kb.dma("sp", Gd["gb3"][:], win["gb3"], writes=[(R("gb3"), None)])
        kb.dma("sp", Gd["arr"][:], win["ar"], writes=[(R("arr"), None)])
        Gd.update(pf=pf, pb=pb, idf=idf, idb=idb, konst=konst, ones_row=ones_row,
                  ones_col_b=ones_col_b)
        return Gd

    G = alloc_ffn("_f1")
    G["x_src"] = xa_scr
    G["x_src_name"] = "xa_scr"
    G["x_dst"] = xb_scr
    G["x_dst_name"] = "xb_scr"
    ffn_phase(kb, G, "1")
    if stop_after == "ffn1":
        return nc, kb

    kb.free_from("outf_f1")

    Bd = {}
    Bd["rkT"] = kb.sbuf("rkT", [128, 8 * 2049], BF16)
    Bd["Wst"] = kb.sbuf("Wst", [128, 8 * 1024], F32)
    Bd["Wbf"] = kb.sbuf("Wbf", [128, 8 * 1024], BF16)
    Bd["WwT"] = kb.sbuf("WwT", [128, 8 * 1024], BF16)
    Bd["WrT"] = kb.sbuf("WrT", [128, 8 * 1024], BF16)
    Bd["MT"] = kb.sbuf("MT", [128, 128], BF16)
    Bd["DTs"] = kb.sbuf("DTs", [128, 128], F32)
    Bd["CZs"] = kb.sbuf("CZs", [128, 128], F32)
    Bd["iop"] = kb.sbuf("iop", [128, 1], F32)
    Bd["ior"] = kb.sbuf("ior", [128, 1], F32)
    Bd["lgcol"] = kb.sbuf("lgcol", [128, 1], F32)
    Bd["gpcol"] = kb.sbuf("gpcol", [128, 1], F32)
    Bd["gwcol"] = kb.sbuf("gwcol", [128, 1], F32)
    Bd["gccol"] = kb.sbuf("gccol", [128, 1], F32)
    Bd["acol"] = kb.sbuf("acol", [128, 1], F32)
    Bd["brow"] = kb.sbuf("brow", [1, 8], F32)
    Bd["xt"] = [kb.sbuf(f"xtb{p}", [128, 1024], F32) for p in range(4)]
    Bd["lnt32"] = [kb.sbuf(f"lnt32{p}", [128, 1024], F32) for p in range(2)]
    Bd["lnbf"] = [kb.sbuf(f"lnbf{p}", [128, 1024], BF16) for p in range(2)]
    Bd["vt"] = [kb.sbuf(f"vt{p}", [128, 1024], BF16) for p in range(2)]
    Bd["vgw"] = [kb.sbuf(f"vgw{p}", [128, 1024], BF16) for p in range(2)]
    Bd["wkt"] = [kb.sbuf(f"wkt{p}", [128, 1024], BF16) for p in range(2)]
    Bd["statb2"] = kb.sbuf("statb2", [128, 8], F32)
    Bd["rdbf"] = kb.sbuf("rdbf", [128, 1024], BF16)
    Bd["rdT"] = kb.sbuf("rdT", [128, 1024], BF16)
    Bd["SMT"] = kb.sbuf("SMT", [128, 128], BF16)
    Bd["scrf"] = kb.sbuf("scrf", [128, 1024], F32)
    Bd["trash"] = kb.sbuf("trashb", [128, 1024], BF16)
    Bd["statb"] = kb.sbuf("statb", [128, 8], F32)
    Bd.update(pf=pf, pb=pb, idf=idf, idb=idb, konst=konst, ones_row=ones_row)
    Bd["x_src"] = xb_scr
    Bd["x_src_name"] = "xb_scr"
    Bd["x_dst"] = xa_scr
    Bd["x_dst_name"] = "xa2"
    kb.dma("pool", Bd["WwT"][:].rearrange("p (kt o) -> p kt o", kt=8),
           win["WwT"].rearrange("(kt p) o -> p kt o", p=128), writes=[("WwT", None)])
    kb.dma("pool", Bd["WrT"][:].rearrange("p (kt o) -> p kt o", kt=8),
           win["WrT"].rearrange("(kt p) o -> p kt o", p=128), writes=[("WrT", None)])
    kb.dma("sp", Bd["DTs"][:], cin["DT"], writes=[("DTs", None)])
    kb.dma("sp", Bd["CZs"][:], cin["CZ"], writes=[("CZs", None)])
    kb.dma("sp", Bd["iop"][:], cin["iota_p"], writes=[("iop", None)])
    kb.dma("sp", Bd["ior"][:], cin["iota_rev"], writes=[("ior", None)])
    kb.dma("sp", Bd["brow"][0:1, 0:1], win["decay"], writes=[("brow", 0)])
    kb.dma("sp", Bd["brow"][0:1, 1:2], win["log_alpha"], writes=[("brow", 1)])
    kb.memset(Bd["Wst"][:], 0.0, writes=[("Wst", (jt, half)) for jt in range(8) for half in range(2)], tag="bW0")
    kb.memset(Bd["Wbf"][:], 0.0, writes=[("Wbf", None)], tag="bWb0")
    for g in range(8):
        kb.memset(Bd["rkT"][:, g * 2049: g * 2049 + 1], 0.0,
                  writes=[("rkT", (-1, g // 4))], tag=f"brk0_{g}")
    kb.act(Bd["brow"][0:1, 2:3], Bd["brow"][0:1, 0:1], AF.Sigmoid,
           reads=[("brow", 0)], writes=[("brow", 2)], tag="bsig")
    kb.act(Bd["brow"][0:1, 3:4], Bd["brow"][0:1, 2:3], AF.Ln,
           reads=[("brow", 2)], writes=[("brow", 3)], tag="bln0")
    kb.act(Bd["brow"][0:1, 4:5], Bd["brow"][0:1, 1:2], AF.Exp,
           reads=[("brow", 1)], writes=[("brow", 4)], tag="bexp")
    kb.mm(pf[0][:, 0:1], ones_row[:], Bd["brow"][0:1, 3:4], True, True,
          reads=[("ones_row", None), ("brow", 3)], writes=[("pf0", None)], tag="blgb")
    kb.cp(Bd["lgcol"][:], pf[0][:, 0:1], eng="dve", reads=[("pf0", None)],
          writes=[("lgcol", None)])
    kb.mm(pf[0][:, 1:2], ones_row[:], Bd["brow"][0:1, 4:5], True, True,
          reads=[("ones_row", None), ("brow", 4)], writes=[("pf0", None)], tag="bacb")
    kb.cp(Bd["acol"][:], pf[0][:, 1:2], eng="dve", reads=[("pf0", None)],
          writes=[("acol", None)])
    kb.act(Bd["gpcol"][:], Bd["iop"][:], AF.Exp, scale=Bd["lgcol"][:],
           reads=[("iop", None), ("lgcol", None)], writes=[("gpcol", None)], tag="bgp")
    kb.act(Bd["gwcol"][:], Bd["ior"][:], AF.Exp, scale=Bd["lgcol"][:],
           reads=[("ior", None), ("lgcol", None)], writes=[("gwcol", None)], tag="bgw")
    kb.ts(Bd["scrf"][:, 0:1], Bd["lgcol"][:], 128.0, reads=[("lgcol", None)],
          writes=[("scrf", None)], tag="bgc1")
    kb.act(Bd["gccol"][:], Bd["scrf"][:, 0:1], AF.Exp,
           reads=[("scrf", None)], writes=[("gccol", None)], tag="bgc2")
    kb.act(Bd["scrf"][:, 0:128], Bd["DTs"][:], AF.Exp, scale=Bd["lgcol"][:],
           reads=[("DTs", None), ("lgcol", None), ("scrf", None)], writes=[("scrf", None)], tag="bmt1")
    kb.tt(Bd["MT"][:], Bd["scrf"][:, 0:128], Bd["CZs"][:], ALU.mult,
          reads=[("scrf", None), ("CZs", None)], writes=[("MT", None)], tag="bmt2")
    nck = NBLK
    if stop_after and stop_after.startswith("attnb") and len(stop_after) > 5:
        nck = int(stop_after[5:])
    attnb_phase(kb, Bd, nck)
    if stop_after and stop_after.startswith("attnb"):
        if kb.dbg:
            kb.dump("Wst", Bd["Wst"][:], [("Wst", (jt, hh)) for jt in range(8) for hh in range(2)])
            kb.dump("rdbf", Bd["rdbf"][:], [("rdbf", 0), ("rdbf", 1)], cast=True)
        return nc, kb

    kb.free_from("rkT")
    G2 = alloc_ffn("_f2")
    G2["x_src"] = xa_scr
    G2["x_src_name"] = "xa2"
    G2["x_dst"] = out_hbm
    G2["x_dst_name"] = "out"
    ffn_phase(kb, G2, "2")
    return nc, kb


def _dummy_ins():
    d = {}
    d["L0_attn_a_q_w"] = np.zeros((D, D), np.float32)
    d["L0_attn_a_k_w"] = np.zeros((D, D), np.float32)
    d["L0_attn_a_v_w"] = np.zeros((D, D), np.float32)
    d["L0_attn_a_b_w"] = np.zeros((H, D), np.float32)
    d["L0_attn_a_o_w"] = np.zeros((D, D), np.float32)
    d["L0_attn_a_mlp1_w"] = np.zeros((1024, D + 16), np.float32)
    d["L0_attn_a_mlp1_b"] = np.zeros((1024,), np.float32)
    d["L0_attn_a_mlp2_w"] = np.zeros((4, 1024), np.float32)
    d["L0_attn_a_mlp2_b"] = np.zeros((4,), np.float32)
    d["L0_attn_a_q_cw"] = np.zeros((D, 4), np.float32)
    d["L0_attn_a_k_cw"] = np.zeros((D, 4), np.float32)
    d["L0_attn_a_v_cw"] = np.zeros((D, 4), np.float32)
    d["L0_attn_a_fir_short_w"] = np.zeros((H, DV, 5), np.float32)
    d["L0_attn_a_fir_long_w"] = np.zeros((H, DV, 64), np.float32)
    d["L0_attn_a_logit_temp"] = np.zeros((1,), np.float32)
    d["L0_attn_a_conv_res_logit"] = np.zeros((H,), np.float32)
    d["L0_attn_a_norm_w"] = np.zeros((DV,), np.float32)
    d["L0_ffn_cn_w1"] = np.zeros((D, D), np.float32)
    d["L0_ffn_cn_w2"] = np.zeros((D, 512), np.float32)
    d["L0_ffn_cn_w3"] = np.zeros((512, 256), np.float32)
    d["L0_ffn_cn_w4"] = np.zeros((256, 1), np.float32)
    d["L0_ffn_cn_b1"] = np.zeros((D,), np.float32)
    d["L0_ffn_cn_b2"] = np.zeros((512,), np.float32)
    d["L0_ffn_cn_b3"] = np.zeros((256,), np.float32)
    d["L0_ffn_cn_b4"] = np.zeros((1,), np.float32)
    d["L0_ffn_gc_w1"] = np.zeros((D, 512), np.float32)
    d["L0_ffn_gc_w2"] = np.zeros((512, 256), np.float32)
    d["L0_ffn_gc_w3"] = np.zeros((256, 1), np.float32)
    d["L0_ffn_gc_b1"] = np.zeros((512,), np.float32)
    d["L0_ffn_gc_b2"] = np.zeros((256,), np.float32)
    d["L0_ffn_gc_b3"] = np.zeros((1,), np.float32)
    d["L0_ffn_adaptive_range"] = np.zeros((1,), np.float32)
    d["L0_attn_b_W_write"] = np.zeros((D, D), np.float32)
    d["L0_attn_b_W_read"] = np.zeros((D, D), np.float32)
    d["L0_attn_b_decay"] = np.zeros((), np.float32)
    d["L0_attn_b_log_alpha"] = np.zeros((), np.float32)
    return d


def finish(nc, kb):
    sems = {e: nc.semaphore("sem_" + e).__enter__() for e in ENGINES}
    dma_sems = {q: [nc.semaphore(f"dma_{q}{i}").__enter__() for i in range(Sched.NDMA)]
                for q in ("sp", "pool")}
    run = kb.s.emit_all(sems, dma_sems)
    with nc.Block() as block:
        @block.tensor
        def _(t):
            run("pe", t)

        @block.scalar
        def _(sc):
            run("act", sc)

        @block.vector
        def _(v):
            run("dve", v)

        @block.gpsimd
        def _(g):
            run("pool", g)

        @block.sync
        def _(sy):
            run("sp", sy)
            for q in ("sp", "pool"):
                for slot in range(Sched.NDMA):
                    fin = kb.s.dma_final.get((q, slot))
                    if fin:
                        sy.wait_ge(dma_sems[q][slot], 16 * fin)
    kb.es.close()
    return nc


def make_in_map(ins, b, consts, weights):
    m = {"x": np.ascontiguousarray(ins["L0_attn_a_hidden_states"][b]).astype(np.float32)}
    for k, v in consts.items():
        m["c_" + k] = v
    for k, v in weights.items():
        m["w_" + k] = np.ascontiguousarray(v).astype(np.float32)
    return m



def kernel(**inputs):
    ins = {k: np.asarray(v) for k, v in inputs.items()}
    consts = prep_consts()
    weights = prep_weights(ins)
    nc, kb = build_program(dbg=False, stop_after=None)
    finish(nc, kb)
    in_maps = [make_in_map(ins, b, consts, weights) for b in range(8)]
    res = run_bass_kernel_spmd(nc, in_maps, core_ids=list(range(8)))
    out = np.stack([np.asarray(res.results[b]["out"], dtype=np.float32)
                    for b in range(8)], axis=0)
    return out


# version 19 (final)
# speedup vs baseline: 1.2230x; 1.2230x over previous
import math
import numpy as np
import ml_dtypes
import concourse.bass as bass
import concourse.mybir as mybir
from concourse.bass_utils import run_bass_kernel_spmd
from contextlib import ExitStack

F32 = mybir.dt.float32
BF16 = mybir.dt.bfloat16
AF = mybir.ActivationFunctionType
ALU = mybir.AluOpType
AX = mybir.AxisListType

T = 2048
D = 1024
H, DK, DV = 4, 256, 256
NBLK = 16
CH = 128
NT = 4
TT = 512

ENGINES = ("pe", "act", "dve", "pool", "sp")
ATTNB_PARTS = set("ABCDEFG")
EMIT_DRAINS = True


def weave(*gens):
    gens = [g for g in gens if g is not None]
    while gens:
        for g in list(gens):
            try:
                next(g)
            except StopIteration:
                gens.remove(g)



class Op:
    __slots__ = ("eng", "emit", "order_i", "mark", "is_dma", "dma_idx", "waits",
                 "tag", "spos", "drain_pos")

    def __init__(self, eng, emit, is_dma=False, tag=""):
        self.eng = eng
        self.emit = emit
        self.mark = False
        self.is_dma = is_dma
        self.dma_idx = -1
        self.waits = []
        self.order_i = -1
        self.spos = -1
        self.drain_pos = -1
        self.tag = tag


class Sched:

    def __init__(self, nc):
        self.nc = nc
        self.ops = {e: [] for e in ENGINES}
        self.order = []
        self.res = {}
        self.carry = {}
        self._dma_counts = {}

    def _st(self, name, key):
        d = self.res.setdefault(name, {})
        return d.setdefault(key, {"w": None, "r": []})

    def _dep_ops(self, o, reads, writes):
        deps = []
        for (name, key) in reads:
            keys = list(self.res.get(name, {})) if key is None else [key, None]
            psum = name.startswith("pf") or name.startswith("pb")
            for k in keys:
                st = self.res.get(name, {}).get(k)
                if st and st["w"] is not None:
                    deps.append(st["w"])
                if psum and st:
                    deps.extend(r for r in st["r"] if r.eng != o.eng)
            deps.extend(self.carry.get(name, ()))
        for (name, key) in writes:
            keys = list(self.res.get(name, {})) if key is None else [key, None]
            for k in keys:
                st = self.res.get(name, {}).get(k)
                if st:
                    if st["w"] is not None:
                        deps.append(st["w"])
                    deps.extend(st["r"])
            deps.extend(self.carry.get(name, ()))
        return deps

    def _commit(self, o, reads, writes):
        for (name, key) in reads:
            self._st(name, key)["r"].append(o)
        for (name, key) in writes:
            if key is None:
                d = self.res.setdefault(name, {})
                d.clear()
                self.carry.pop(name, None)
            st = self._st(name, key)
            st["w"] = o
            st["r"] = []

    def op(self, eng, emit, reads=(), writes=(), is_dma=False, tag=""):
        o = Op(eng, emit, is_dma=is_dma, tag=tag)
        deps = self._dep_ops(o, reads, writes)
        best = {}
        for d in deps:
            if d is o:
                continue
            if not d.is_dma and d.eng == eng:
                if eng != "pe":
                    o.drain_pos = max(o.drain_pos, d.spos)
                continue
            stream = ("dma", d.eng, d.dma_idx % self.NDMA) if d.is_dma else ("eng", d.eng)
            cur = best.get(stream)
            if cur is None or d.order_i > cur.order_i:
                best[stream] = d
        o.waits = list(best.values())
        o.spos = len(self.ops[eng])
        if is_dma:
            o.dma_idx = self._dma_counts.get(eng, 0)
            self._dma_counts[eng] = o.dma_idx + 1
        o.order_i = len(self.order)
        self.order.append(o)
        self.ops[eng].append(o)
        self._commit(o, reads, writes)
        return o

    @staticmethod
    def _best_per_stream(ops):
        best = {}
        for d in ops:
            stream = ("dma", d.eng, d.dma_idx % Sched.NDMA) if d.is_dma else ("eng", d.eng)
            cur = best.get(stream)
            if cur is None or d.order_i > cur.order_i:
                best[stream] = d
        return list(best.values())

    def retag(self, name):
        ops = []
        for st in self.res.get(name, {}).values():
            if st["w"] is not None:
                ops.append(st["w"])
            ops.extend(st["r"])
        ops.extend(self.carry.get(name, ()))
        self.res[name] = {}
        self.carry[name] = self._best_per_stream(ops)

    NDMA = 8

    def finalize_marks(self):
        for o in self.order:
            for d in o.waits:
                if not d.is_dma:
                    d.mark = True
        self.mnum = {}
        self.dma_slot = {}
        self.dma_final = {}
        for e in ENGINES:
            n = 0
            for o in self.ops[e]:
                if o.is_dma:
                    slot = o.dma_idx % self.NDMA
                    use = o.dma_idx // self.NDMA + 1
                    self.dma_slot[id(o)] = (slot, use)
                    self.dma_final[(e, slot)] = use
                elif o.mark:
                    n += 1
                self.mnum[id(o)] = n if not o.is_dma else None

    def emit_all(self, sems, dma_sems):
        self.finalize_marks()
        seen = {e: {} for e in ENGINES}

        def run_stream(e, engine):
            sn = seen[e]
            drained_upto = -1
            for o in self.ops[e]:
                if EMIT_DRAINS and o.drain_pos >= 0 and o.drain_pos > drained_upto:
                    engine.drain()
                    drained_upto = o.spos - 1
                for d in o.waits:
                    if d.is_dma:
                        slot, use = self.dma_slot[id(d)]
                        key = ("dma", d.eng, slot)
                        val = 16 * use
                        sem = dma_sems[d.eng][slot]
                    else:
                        key = ("eng", d.eng)
                        val = self.mnum[id(d)]
                        sem = sems[d.eng]
                    if sn.get(key, 0) >= val:
                        continue
                    engine.wait_ge(sem, val)
                    sn[key] = val
                if o.is_dma:
                    slot, use = self.dma_slot[id(o)]
                    key = ("dma", o.eng, slot)
                    if use >= 2 and sn.get(key, 0) < 16 * (use - 1):
                        engine.wait_ge(dma_sems[o.eng][slot], 16 * (use - 1))
                        sn[key] = 16 * (use - 1)
                    inst = o.emit(engine)
                    inst.then_inc(dma_sems[o.eng][slot], 16)
                else:
                    inst = o.emit(engine)
                    if o.mark:
                        if inst is None:
                            raise RuntimeError(f"marked op has no instruction: {o.tag}")
                        inst.then_inc(sems[o.eng], 1)
        return run_stream



def build_toeplitz(filt, K):
    C = filt.shape[0]
    out = np.zeros((127, C, 64), np.float32)
    s = np.arange(127)[:, None]
    i = np.arange(64)[None, :]
    j = s - i + K - 64
    valid = (j >= 0) & (j < K)
    js = np.clip(j, 0, K - 1)
    for c in range(C):
        out[:, c, :] = np.where(valid, filt[c][js], 0.0)
    return out.reshape(127, C * 64)


def prep_consts():
    c = {}
    c["identity"] = np.eye(128, dtype=np.float32)
    c["ones_row"] = np.ones((1, 128), np.float32)
    c["ones_col"] = np.ones((128, 1), np.float32)
    c["maskSL"] = np.tril(np.ones((128, 128), np.float32), -1)
    c["maskIU"] = np.triu(np.ones((128, 128), np.float32), 0)
    p = np.arange(128, dtype=np.float32)
    c["iota_p"] = p[:, None].copy()
    c["iota_rev"] = (127.0 - p)[:, None].copy()
    DT = np.clip(p[None, :] - 1 - p[:, None], 0, None)
    c["DT"] = DT.astype(np.float32)
    c["CZ"] = (p[None, :] > p[:, None]).astype(np.float32)
    k = np.zeros((128, 8), np.float32)
    k[:, 0] = 1e-5; k[:, 1] = 1e-6; k[:, 2] = 1.0; k[:, 3] = 0.02
    c["konst"] = k
    return c


def prep_weights(ins):
    w = {}
    w["qwT"] = np.ascontiguousarray(ins["L0_attn_a_q_w"].T)
    w["kwT"] = np.ascontiguousarray(ins["L0_attn_a_k_w"].T)
    w["vwT"] = np.ascontiguousarray(ins["L0_attn_a_v_w"].T)
    w["bwT"] = np.ascontiguousarray(ins["L0_attn_a_b_w"].T)
    w["owT"] = np.ascontiguousarray(ins["L0_attn_a_o_w"].T)
    w["m1main"] = np.ascontiguousarray(ins["L0_attn_a_mlp1_w"][:, :D].T)
    w["m1tail"] = np.ascontiguousarray(ins["L0_attn_a_mlp1_w"][:, D:].T)
    w["m2T"] = np.ascontiguousarray(ins["L0_attn_a_mlp2_w"].T)
    for nm, key in (("qcw", "L0_attn_a_q_cw"), ("kcw", "L0_attn_a_k_cw"), ("vcw", "L0_attn_a_v_cw")):
        w[nm] = np.ascontiguousarray(ins[key].reshape(8, 128, 4).transpose(1, 0, 2))
    w["toep_s"] = build_toeplitz(ins["L0_attn_a_fir_short_w"].reshape(H * DV, 5), 5)
    w["toep_l"] = build_toeplitz(ins["L0_attn_a_fir_long_w"].reshape(H * DV, 64), 64)
    w["b1f"] = np.ascontiguousarray(ins["L0_attn_a_mlp1_b"].reshape(8, 128).T)
    w["normw_row"] = ins["L0_attn_a_norm_w"][None, :].copy()
    w["m2b_row"] = ins["L0_attn_a_mlp2_b"][None, :].copy()
    w["crl_row"] = ins["L0_attn_a_conv_res_logit"][None, :].copy()
    w["ltemp"] = ins["L0_attn_a_logit_temp"].reshape(1, 1).copy()
    w["cn_w1"] = ins["L0_ffn_cn_w1"]
    w["cn_w2"] = ins["L0_ffn_cn_w2"]
    w["cn_w3"] = ins["L0_ffn_cn_w3"]
    w["cn_w4"] = np.ascontiguousarray(ins["L0_ffn_cn_w4"].reshape(2, 128).T)
    w["cb1f"] = np.ascontiguousarray(ins["L0_ffn_cn_b1"].reshape(8, 128).T)
    w["cb2f"] = np.ascontiguousarray(ins["L0_ffn_cn_b2"].reshape(4, 128).T)
    w["cb3f"] = np.ascontiguousarray(ins["L0_ffn_cn_b3"].reshape(2, 128).T)
    w["cb4"] = ins["L0_ffn_cn_b4"].reshape(1, 1).copy()
    w["gc_w1"] = ins["L0_ffn_gc_w1"]
    w["gc_w2"] = ins["L0_ffn_gc_w2"]
    w["gc_w3"] = np.ascontiguousarray(ins["L0_ffn_gc_w3"].reshape(2, 128).T)
    w["gb1_row"] = ins["L0_ffn_gc_b1"][None, :].copy()
    w["gb2_row"] = ins["L0_ffn_gc_b2"][None, :].copy()
    w["gb3"] = ins["L0_ffn_gc_b3"].reshape(1, 1).copy()
    w["ar"] = ins["L0_ffn_adaptive_range"].reshape(1, 1).copy()
    w["WwT"] = np.ascontiguousarray(ins["L0_attn_b_W_write"].T)
    w["WrT"] = np.ascontiguousarray(ins["L0_attn_b_W_read"].T)
    w["decay"] = np.asarray(ins["L0_attn_b_decay"]).reshape(1, 1).copy()
    w["log_alpha"] = np.asarray(ins["L0_attn_b_log_alpha"]).reshape(1, 1).copy()
    return w



class KB:
    def __init__(self, nc, dbg=False):
        self.nc = nc
        self.s = Sched(nc)
        self.es = ExitStack()
        self.sb = {}
        self.dbg = dbg
        self.dbg_views = {}

    def sbuf(self, name, shape, dt):
        cm = self.nc.sbuf_tensor(name, shape, dt)
        t = cm.__enter__()
        self.sb[name] = t
        self._cms = getattr(self, "_cms", {})
        self._cms[name] = cm
        self._order = getattr(self, "_order", [])
        self._order.append(name)
        pend = getattr(self, "_pending_carry", None)
        if pend:
            self.s.carry.setdefault(name, []).extend(pend)
        return t

    def free(self, *names):
        pend = getattr(self, "_pending_carry", None)
        if pend is None:
            pend = self._pending_carry = []
        for name in names:
            for st in self.s.res.get(name, {}).values():
                if st["w"] is not None:
                    pend.append(st["w"])
                pend.extend(st["r"])
            pend.extend(self.s.carry.get(name, ()))
            self.s.res.pop(name, None)
            self.s.carry.pop(name, None)
            self._cms[name].__exit__(None, None, None)
            del self.sb[name]
            self._order.remove(name)
        self._pending_carry = pend = self.s._best_per_stream(pend)

    def free_from(self, marker):
        i = self._order.index(marker)
        for name in reversed(self._order[i:]):
            self.free(name)

    def psum(self, name, shape, dt):
        t = self.es.enter_context(self.nc.psum_tensor(name, shape, dt))
        self.sb[name] = t
        return t

    def din(self, name, shape, dt=F32):
        return self.nc.dram_tensor(name, shape, dt, kind="ExternalInput").ap()

    def dout(self, name, shape, dt=F32):
        return self.nc.dram_tensor(name, shape, dt, kind="ExternalOutput").ap()

    def dint(self, name, shape, dt=F32):
        kind = "ExternalOutput" if self.dbg else "Internal"
        return self.nc.dram_tensor(name, shape, dt, kind=kind).ap()

    def dma(self, q, out_ap, in_ap, reads=(), writes=(), tag=""):
        return self.s.op(q, lambda e, o=out_ap, i=in_ap: e.dma_start(out=o, in_=i),
                         reads=reads, writes=writes, is_dma=True, tag=tag)

    def mm(self, out_ap, lhsT, rhs, start, stop, reads=(), writes=(), tag="", tile_position=None):
        return self.s.op("pe",
                         lambda e, o=out_ap, l=lhsT, r=rhs, st=start, sp=stop, tp=tile_position:
                             e.matmul(o, l, r, start=st, stop=sp, tile_position=tp),
                         reads=reads, writes=writes, tag=tag)

    def tr(self, out_ap, in_ap, ident, reads=(), writes=(), tag=""):
        return self.s.op("pe",
                         lambda e, o=out_ap, i=in_ap, d=ident: e.transpose(o, i, d),
                         reads=reads, writes=writes, tag=tag)

    def act(self, out_ap, in_ap, func, bias=None, scale=None, accum_out=None,
            reads=(), writes=(), tag=""):
        kw = {}
        if bias is not None:
            kw["bias"] = bias
        if scale is not None:
            kw["scale"] = scale
        if accum_out is not None:
            kw["accum_out"] = accum_out
        return self.s.op("act",
                         lambda e, o=out_ap, i=in_ap, f=func, kw=kw:
                             e.activation(out=o, in_=i, func=f, **kw),
                         reads=reads, writes=writes, tag=tag)

    def tt(self, out, in0, in1, op, eng="dve", reads=(), writes=(), tag=""):
        return self.s.op(eng,
                         lambda e, o=out, a=in0, b=in1, p=op: e.tensor_tensor(out=o, in0=a, in1=b, op=p),
                         reads=reads, writes=writes, tag=tag)

    def ts(self, out, in0, s1, s2=None, op0=ALU.mult, op1=None, eng="dve",
           reads=(), writes=(), tag=""):
        def f(e, o=out, a=in0, s1=s1, s2=s2, p0=op0, p1=op1):
            kw = dict(out=o, in0=a, scalar1=s1, scalar2=s2, op0=p0)
            if p1 is not None:
                kw["op1"] = p1
            return e.tensor_scalar(**kw)
        return self.s.op(eng, f, reads=reads, writes=writes, tag=tag)

    def stt(self, out, in0, scalar, in1, op0, op1, eng="dve", reads=(), writes=(), tag=""):
        return self.s.op(eng,
                         lambda e, o=out, a=in0, s=scalar, b=in1, p0=op0, p1=op1:
                             e.scalar_tensor_tensor(out=o, in0=a, scalar=s, in1=b, op0=p0, op1=p1),
                         reads=reads, writes=writes, tag=tag)

    def red(self, out, in_, op, absval=False, eng="dve", reads=(), writes=(), tag=""):
        return self.s.op(eng,
                         lambda e, o=out, i=in_, p=op, av=absval:
                             e.tensor_reduce(out=o, in_=i, axis=AX.X, op=p, apply_absolute_value=av),
                         reads=reads, writes=writes, tag=tag)

    def cp(self, out, in_, eng="dve", reads=(), writes=(), tag=""):
        if eng == "act":
            return self.s.op(eng, lambda e, o=out, i=in_: e.copy(out=o, in_=i),
                             reads=reads, writes=writes, tag=tag)
        return self.s.op(eng, lambda e, o=out, i=in_: e.tensor_copy(out=o, in_=i),
                         reads=reads, writes=writes, tag=tag)

    def memset(self, ap, val, eng="dve", reads=(), writes=(), tag=""):
        return self.s.op(eng, lambda e, a=ap, v=val: e.memset(a, v),
                         reads=reads, writes=writes, tag=tag)

    def recip(self, out, in_, reads=(), writes=(), tag=""):
        return self.s.op("dve", lambda e, o=out, i=in_: e.reciprocal(out=o, in_=i),
                         reads=reads, writes=writes, tag=tag)

    def dump(self, name, src_ap, reads, cast=False):
        if not self.dbg:
            return
        shape = list(src_ap.shape)
        d = self.dout("dbg_" + name, shape, F32)
        q = "pool" if cast else "sp"
        self.dma(q, d, src_ap, reads=reads, tag="dump_" + name)
        self.dbg_views[name] = shape


def attn_c_phase(kb, h, L):
    pf, pb = L["pf"], L["pb"]
    konst = L["konst"]; idf = L["idf"]; idb = L["idb"]
    stcrP, stcP, statsT = L["stcr"], L["stc"], L["statsT"]
    trash2 = L["trash2"]; sshin = L["sshin"]; h1s = L["h1s"]; h1f = L["h1f"]
    m1tail, m2T, b1f = L["m1tail"], L["m2T"], L["b1f"]
    lgr, lgc, pc, pcolP = L["lgr"], L["lgc"], L["pc"], L["pcol"]
    ls_tok, ll_tok, dout, v_tok = L["ls_tok"], L["ll_tok"], L["dout"], L["v_tok"]
    o_tP, o_t2P, ohn_t = L["o_t"], L["o_t2"], L["ohn_t"]
    crl_sig, trec, normw_bc = L["crl_sig"], L["trec"], L["normw_bc"]
    ssh_scr, ohn_scr = L["ssh_scr"], L["ohn_scr"]
    kb.s.retag("copyA"); kb.s.retag("copyB")
    sshd = L["sshd"]

    def ssh_load(t):
        kb.dma("sp", sshd[:].rearrange("p (g t) -> p g t", g=8),
               ssh_scr[:, :, t * TT:(t + 1) * TT].rearrange("g p t -> p g t"),
               reads=[("sshd", None)] + [("ssh_scr", (t, go)) for go in range(8)],
               writes=[("sshd", None)], tag=f"sshld{h}_{t}")
    tensors = [("qf", ls_tok), ("kf", ll_tok), ("dout", dout), ("v_tok", v_tok)]
    for t in range(NT):
        sshin = sshd[:]
        ssh_n = ("sshd", None)
        ssh_load(t)
        for i in range(4):
            blk = t * 4 + i
            pp2 = i % 2
            stcr = stcrP[pp2]; stc = stcP[pp2]
            scr_n = f"stcr{pp2}"; stc_n = f"stc{pp2}"
            for ti, (tn, tb) in enumerate(tensors[:2]):
                xsl = tb[:, blk * 256:(blk + 1) * 256]
                kb.red(stcr[:, ti:ti + 1], xsl, ALU.add, reads=[(tn, None)],
                       writes=[(scr_n, ti)], tag=f"csum{h}_{blk}_{ti}")
                kb.red(stcr[:, 4 + ti:5 + ti], xsl, ALU.add, absval=True, reads=[(tn, None)],
                       writes=[(scr_n, 4 + ti)], tag=f"cabs{h}_{blk}_{ti}")
                kb.act(trash2[:, 0:256], xsl, AF.Square, accum_out=stcr[:, 8 + ti:9 + ti],
                       reads=[(tn, None)], writes=[("trash", None), (scr_n, 8 + ti)],
                       tag=f"csq{h}_{blk}_{ti}")
            xin_c1 = L["xin"]
            c1r = [("xin", ("c1", blk, ti2, j2)) for ti2 in range(2) for j2 in range(3)]
            kb.cp(stcr[:, 2:4], xin_c1[:, blk * 8: blk * 8 + 4: 3], eng="dve",
                  reads=c1r, writes=[(scr_n, 2), (scr_n, 3)], tag=f"c1cp0{h}_{blk}")
            kb.cp(stcr[:, 6:8], xin_c1[:, blk * 8 + 1: blk * 8 + 5: 3], eng="dve",
                  reads=c1r, writes=[(scr_n, 6), (scr_n, 7)], tag=f"c1cp1{h}_{blk}")
            kb.cp(stcr[:, 10:12], xin_c1[:, blk * 8 + 2: blk * 8 + 6: 3], eng="dve",
                  reads=c1r, writes=[(scr_n, 10), (scr_n, 11)], tag=f"c1cp2{h}_{blk}")
            stc_v = stc[:].rearrange("p (t s) -> p t s", s=4)
            raw_r = [(scr_n, j) for j in range(12)]
            kb.ts(stc_v[:, :, 0], stcr[:, 0:4], 1.0 / 256, reads=raw_r,
                  writes=[(stc_n, 0)], tag=f"cmean{h}_{blk}")
            kb.ts(stcr[:, 12:16], stcr[:, 8:12], 1.0 / 256, reads=raw_r,
                  writes=[(scr_n, 12)], tag=f"cex2{h}_{blk}")
            kb.tt(stc_v[:, :, 1], stc_v[:, :, 0], stc_v[:, :, 0], ALU.mult,
                  reads=[(stc_n, 0)], writes=[(stc_n, 1)], tag=f"cmsq{h}_{blk}")
            kb.tt(stc_v[:, :, 1], stcr[:, 12:16], stc_v[:, :, 1], ALU.subtract,
                  reads=[(scr_n, 12), (stc_n, 1)], writes=[(stc_n, 1)], tag=f"cvar{h}_{blk}")
            kb.ts(stc_v[:, :, 2], stcr[:, 4:8], 1.0 / 256, reads=raw_r,
                  writes=[(stc_n, 2)], tag=f"cam{h}_{blk}")
            kb.act(stc_v[:, :, 3], stcr[:, 8:12], AF.Sqrt, reads=raw_r,
                   writes=[(stc_n, 3)], tag=f"cl2{h}_{blk}")
            kb.tr(pf[5][0:16, 0:128], stc[:], idf[:],
                  reads=[(stc_n, kk) for kk in range(4)] + [("idf", None)],
                  writes=[("pf5", None)], tag=f"cstT{h}_{blk}")
            kb.cp(statsT[0:16, i * 128:(i + 1) * 128], pf[5][0:16, 0:128],
                  eng="dve", reads=[("pf5", None)], writes=[("statsT", i)],
                  tag=f"cstTe{h}_{blk}")
        for go in range(8):
            bank = pf[go % 2]
            bres = f"pf{go % 2}"
            kb.mm(bank[:], m1tail[0:16, go * 128:(go + 1) * 128], statsT[0:16, 0:512],
                  True, True,
                  reads=[("m1tail", None)] + [("statsT", ii) for ii in range(4)],
                  writes=[(bres, None)], tag=f"gmm{h}_{t}_{go}")
            r2 = go % 2
            hsl = h1s[:, r2 * 512:(r2 + 1) * 512]
            kb.tt(hsl, bank[:], sshin[:, go * 512:(go + 1) * 512], ALU.add,
                  reads=[(bres, None), ("copyA", None), ("h1s", r2)], writes=[("h1s", r2)],
                  tag=f"gadd{h}_{t}_{go}")
            kb.act(h1f[:, go * 512:(go + 1) * 512], hsl, AF.Gelu_apprx_tanh,
                   bias=b1f[:, go:go + 1],
                   reads=[("h1s", r2), ("b1f", None)], writes=[("copyB", go)],
                   tag=f"gact{h}_{t}_{go}")
        for kt in range(8):
            kb.mm(pf[2][0:4, 0:512], m2T[:, kt * 4:(kt + 1) * 4],
                  h1f[:, kt * 512:(kt + 1) * 512], start=(kt == 0), stop=(kt == 7),
                  reads=[("m2T", None), ("copyB", kt)], writes=[("pf2", None)],
                  tag=f"glog{h}_{t}_{kt}")
        kb.act(lgr[:], pf[2][0:4, 0:512], AF.Copy, reads=[("pf2", None)],
               writes=[("lgr", None)], tag=f"glge{h}_{t}")
        for i in range(4):
            kb.tr(pf[2][:, 128 + i * 4: 128 + (i + 1) * 4], lgr[0:4, i * 128:(i + 1) * 128],
                  idf[0:4, 0:4], reads=[("lgr", None), ("idf", None)],
                  writes=[("pf2", None)], tag=f"glgT{h}_{t}_{i}")
        kb.cp(lgc[:], pf[2][:, 128:144], eng="dve", reads=[("pf2", None)],
              writes=[("lgc", None)], tag=f"glgc{h}_{t}")

        for i in range(4):
            blk = t * 4 + i
            pp2 = i % 2
            pcol = pcolP[pp2]; o_t = o_tP[pp2]; o_t2 = o_t2P[pp2]
            pcl_n = f"pcol{pp2}"; ot_n = f"o_t{pp2}"; ot2_n = f"o_t2{pp2}"
            lsl = lgc[:, i * 4:(i + 1) * 4]
            kb.tt(lsl, lsl, L["m2b_bc"][:], ALU.add,
                  reads=[("lgc", None), ("m2b_bc", None)], writes=[("lgc", None)],
                  tag=f"sb{h}_{blk}")
            kb.red(pcol[:], lsl, ALU.max, reads=[("lgc", None)], writes=[(pcl_n, None)],
                   tag=f"smax{h}_{blk}")
            kb.ts(lsl, lsl, pcol[:], op0=ALU.subtract, reads=[("lgc", None), (pcl_n, None)],
                  writes=[("lgc", None)], tag=f"ssub{h}_{blk}")
            psl = pc[:, i * 4:(i + 1) * 4]
            kb.act(psl, lsl, AF.Exp, scale=trec[:],
                   reads=[("lgc", None), ("trec", None)], writes=[("pc", i)],
                   tag=f"sexp{h}_{blk}")
            kb.red(pcol[:], psl, ALU.add, reads=[("pc", i)], writes=[(pcl_n, None)],
                   tag=f"ssum{h}_{blk}")
            kb.recip(pcol[:], pcol[:], reads=[(pcl_n, None)], writes=[(pcl_n, None)])
            kb.ts(psl, psl, pcol[:], op0=ALU.mult, reads=[("pc", i), (pcl_n, None)],
                  writes=[("pc", i)], tag=f"sdiv{h}_{blk}")
            kb.ts(psl, psl, 0.02, op0=ALU.max, reads=[("pc", i)], writes=[("pc", i)],
                  tag=f"sflr{h}_{blk}")
            kb.red(pcol[:], psl, ALU.add, reads=[("pc", i)], writes=[(pcl_n, None)],
                   tag=f"ssum2{h}_{blk}")
            kb.recip(pcol[:], pcol[:], reads=[(pcl_n, None)], writes=[(pcl_n, None)])
            kb.ts(psl, psl, pcol[:], op0=ALU.mult, reads=[("pc", i), (pcl_n, None)],
                  writes=[("pc", i)], tag=f"sdiv2{h}_{blk}")
            kb.tt(psl[:, 0:1], psl[:, 0:1], crl_sig[:, h:h + 1], ALU.add,
                  reads=[("pc", i), ("crl_sig", None)], writes=[("pc", i)],
                  tag=f"sp0{h}_{blk}")
            kb.ts(o_t[:], ls_tok[:, blk * 256:(blk + 1) * 256], psl[:, 0:1],
                  op0=ALU.mult, reads=[("qf", None), ("pc", i), (ot_n, None)],
                  writes=[(ot_n, None)], tag=f"cb0{h}_{blk}")
            for bi, (bn, bb) in enumerate((("kf", ll_tok), ("dout", dout), ("v_tok", v_tok))):
                kb.stt(o_t[:], bb[:, blk * 256:(blk + 1) * 256], psl[:, bi + 1: bi + 2],
                       o_t[:], ALU.mult, ALU.add,
                       reads=[(bn, None), ("pc", i), (ot_n, None)], writes=[(ot_n, None)],
                       tag=f"cb{bi + 1}{h}_{blk}")
            kb.act(trash2[:, 256:512], o_t[:], AF.Square, accum_out=pcol[:],
                   reads=[(ot_n, None)], writes=[("trash", None), (pcl_n, None)],
                   tag=f"crms{h}_{blk}")
            kb.act(pcol[:], pcol[:], AF.Sqrt, scale=1.0 / 256, bias=konst[:, 0:1],
                   reads=[(pcl_n, None), ("konst", None)], writes=[(pcl_n, None)],
                   tag=f"crs{h}_{blk}")
            kb.recip(pcol[:], pcol[:], reads=[(pcl_n, None)], writes=[(pcl_n, None)])
            kb.stt(o_t2[:], o_t[:], pcol[:], normw_bc[:], ALU.mult, ALU.mult,
                   reads=[(ot_n, None), (pcl_n, None), ("normw_bc", None), (ot2_n, None)],
                   writes=[(ot2_n, None)], tag=f"cnw{h}_{blk}")
            for g in range(2):
                kb.tr(pb[1][:, g * 128:(g + 1) * 128], o_t2[:, g * 128:(g + 1) * 128],
                      idb[:], reads=[(ot2_n, None), ("idb", None)], writes=[("pb1", None)],
                      tag=f"coT{h}_{blk}_{g}")
            for g in range(2):
                kb.cp(ohn_t[:, g * 512 + i * 128: g * 512 + (i + 1) * 128],
                      pb[1][:, g * 128:(g + 1) * 128], eng="dve",
                      reads=[("pb1", None)], writes=[("ohn_t", (g, i))],
                      tag=f"coTe{h}_{blk}_{g}")
        for g in range(2):
            kb.dma("sp", ohn_scr[h * 2 + g, :, t * TT:(t + 1) * TT],
                   ohn_t[:, g * 512:(g + 1) * 512],
                   reads=[("ohn_t", (g, 0)), ("ohn_t", (g, 1)), ("ohn_t", (g, 2)), ("ohn_t", (g, 3))],
                   writes=[("ohn_scr", (h * 2 + g, t))], tag=f"ohnst{h}_{t}_{g}")


def attn_tail(kb, L):
    pf = L["pf"]
    xr, xa_scr, ohn_scr = L["xr"], L["xa_scr"], L["ohn_scr"]
    owT_sb, ohnin, xin = L["owT_sb"], L["ohnin"], L["xin"]
    win = L["win"]
    kb.s.retag("xin")
    kb.dma("pool", owT_sb[:].rearrange("p (kt o) -> p kt o", kt=8),
           win["owT"].rearrange("(kt p) o -> p kt o", p=128), writes=[("lnxf", "ow")],
           tag="ldow")
    for t in range(NT):
        kb.dma("sp", ohnin[:].rearrange("p (kt c) -> p kt c", kt=8),
               ohn_scr[:, :, t * TT:(t + 1) * TT].rearrange("g p t -> p g t"),
               reads=[("lnxf", "ohin")] + [("ohn_scr", (kt, t)) for kt in range(8)],
               writes=[("lnxf", "ohin")], tag=f"ohld{t}")
        for i in range(4):
            blk = t * 4 + i
            xsl = xin[:, i * 1024:(i + 1) * 1024]
            kb.dma("sp", xsl, xr[blk], reads=[("xin", i)], writes=[("xin", i)],
                   tag=f"tx{blk}")
            for half in range(2):
                bank = pf[3 + half]
                bres = f"pf{3 + half}"
                for kt in range(8):
                    kb.mm(bank[:], ohnin[:, kt * 512 + i * 128: kt * 512 + (i + 1) * 128],
                          owT_sb[:, kt * 1024 + half * 512: kt * 1024 + (half + 1) * 512],
                          start=(kt == 0), stop=(kt == 7),
                          reads=[("lnxf", "ohin"), ("lnxf", "ow")], writes=[(bres, None)],
                          tag=f"tmm{blk}_{half}_{kt}")
                kb.tt(xsl[:, half * 512:(half + 1) * 512], xsl[:, half * 512:(half + 1) * 512],
                      bank[:], ALU.add,
                      reads=[("xin", i), (bres, None)], writes=[("xin", i)],
                      tag=f"tadd{blk}_{half}")
            kb.dma("sp", xa_scr.rearrange("(b p) d -> b p d", p=128)[blk], xsl,
                   reads=[("xin", i)], writes=[("xa_scr", blk)], tag=f"txs{blk}")


def ffn_phase(kb, G, tag):
    R = G["R"]
    """One FFN application: x_src -> x_dst. G holds buffers/weights APs."""
    pf, pb = G["pf"], G["pb"]
    idf, idb, konst = G["idf"], G["idb"], G["konst"]
    ones_row, ones_col_b = G["ones_row"], G["ones_col_b"]
    outf, xt, lnt = G["outf"], G["xt"], G["lnt"]
    obP, h1tP, h2tP, h3tP, twP = G["ob"], G["h1t"], G["h2t"], G["h3t"], G["tw"]
    cn1, cn2, cn3, cn4 = G["cn1"], G["cn2"], G["cn3"], G["cn4"]
    cb1f, cb2f, cb3f, cb4 = G["cb1f"], G["cb2f"], G["cb3f"], G["cb4"]
    gw1, gw2, gw3 = G["gw1"], G["gw2"], G["gw3"]
    gb1, gb2, gb3, arr = G["gb1"], G["gb2"], G["gb3"], G["arr"]
    gsc, gcf, statf = G["gsc"], G["gcf"], G["statf"]
    x_src, x_dst = G["x_src"], G["x_dst"]
    xsr = x_src.rearrange("(b p) d -> b p d", p=128)
    xdr = x_dst.rearrange("(b p) d -> b p d", p=128)
    outf_v = outf[:].rearrange("p (g t) -> p g t", g=8)
    T_ = T

    for blk in range(NBLK):
        r = blk % 4
        r2_ = blk % 2
        xv = xt[:, r * 1024:(r + 1) * 1024]
        kb.dma("sp", xv, xsr[blk], reads=[(R("xt"), r), (G["x_src_name"], blk)],
               writes=[(R("xt"), r)], tag=f"f{tag}lx{blk}")
        sc = statf[:, r * 8:(r + 1) * 8]
        kb.act(G["trash"][:], xv, AF.Square, accum_out=sc[:, 0:1],
               reads=[(R("xt"), r)], writes=[(R("trash"), None), (R("statf"), (r, 0))], tag=f"f{tag}sq{blk}")
        kb.act(G["trash"][:], xv, AF.Copy, accum_out=sc[:, 1:2],
               reads=[(R("xt"), r)], writes=[(R("trash"), None), (R("statf"), (r, 1))], tag=f"f{tag}sm{blk}")
        kb.ts(sc[:, 2:3], sc[:, 1:2], 1.0 / D, reads=[(R("statf"), (r, 1))], writes=[(R("statf"), (r, 2))])
        kb.ts(sc[:, 3:4], sc[:, 0:1], 1.0 / D, reads=[(R("statf"), (r, 0))], writes=[(R("statf"), (r, 3))])
        kb.tt(sc[:, 4:5], sc[:, 2:3], sc[:, 2:3], ALU.mult, reads=[(R("statf"), (r, 2))], writes=[(R("statf"), (r, 4))])
        kb.tt(sc[:, 5:6], sc[:, 3:4], sc[:, 4:5], ALU.subtract,
              reads=[(R("statf"), (r, 3)), (R("statf"), (r, 4))], writes=[(R("statf"), (r, 5))])
        kb.act(sc[:, 6:7], sc[:, 5:6], AF.Sqrt, bias=konst[:, 0:1],
               reads=[(R("statf"), (r, 5)), ("konst", None)], writes=[(R("statf"), (r, 6))], tag=f"f{tag}rt{blk}")
        kb.recip(sc[:, 7:8], sc[:, 6:7], reads=[(R("statf"), (r, 6))], writes=[(R("statf"), (r, 7))])
        lv = lnt[:, r2_ * 1024:(r2_ + 1) * 1024]
        kb.ts(lv, xv, sc[:, 2:3], sc[:, 7:8], ALU.subtract, ALU.mult,
              reads=[(R("xt"), r), (R("statf"), (r, 2)), (R("statf"), (r, 7)), (R("lnt"), r2_)],
              writes=[(R("lnt"), r2_)], tag=f"f{tag}ln{blk}")
        for half in range(2):
            kb.mm(pf[4 + half][0:1, 0:512], ones_col_b[:], lv[:, half * 512:(half + 1) * 512],
                  start=(blk == 0), stop=(blk == NBLK - 1),
                  reads=[("ones_col_b", None), (R("lnt"), r2_)], writes=[(f"pf{4 + half}", None)],
                  tag=f"f{tag}gc{blk}_{half}")
        for half in range(2):
            bank = pb[half]
            for g4 in range(4):
                g = half * 4 + g4
                kb.tr(bank[:, g4 * 128:(g4 + 1) * 128], lv[:, g * 128:(g + 1) * 128], idb[:],
                      reads=[(R("lnt"), r2_), ("idb", None)], writes=[(f"pb{half}", None)],
                      tag=f"f{tag}tr{blk}_{g}")
            kb.cp(outf_v[:, half * 4:(half + 1) * 4, blk * 128:(blk + 1) * 128],
                  bank[:, 0:512].rearrange("p (g t) -> p g t", g=4), eng="dve",
                  reads=[(f"pb{half}", None)], writes=[(R("outf"), (blk, half))],
                  tag=f"f{tag}ev{blk}_{half}")
    for half in range(2):
        kb.act(gsc[0:1, half * 512:(half + 1) * 512], pf[4 + half][0:1, 0:512], AF.Copy,
               scale=1.0 / T_, reads=[(f"pf{4 + half}", None)], writes=[(R("gsc"), ("gc", half))],
               tag=f"f{tag}gcr{half}")
    def rowmat(src_row_keys, src_row, n_in, w_sb, w_name, n_out, out_key, bias_row, bias_key, fn):
        nk = n_in // 128
        for kt in range(nk):
            kb.tr(pf[3][:, kt:kt + 1], src_row[0:1, kt * 128:(kt + 1) * 128], idf[0:1, 0:1],
                  reads=[(R("gsc"), k2) for k2 in src_row_keys] + [("idf", None)],
                  writes=[("pf3", None)], tag=f"f{tag}gT{out_key}_{kt}")
        kb.act(gcf[:, 0:nk], pf[3][:, 0:nk], AF.Copy, reads=[("pf3", None)],
               writes=[(R("gcf"), None)], tag=f"f{tag}gTe{out_key}")
        for kt in range(nk):
            kb.mm(pf[3][0:1, 0:n_out], gcf[:, kt:kt + 1], w_sb[:, kt * n_out:(kt + 1) * n_out],
                  start=(kt == 0), stop=(kt == nk - 1),
                  reads=[(R("gcf"), None), (w_name, None)], writes=[("pf3", None)],
                  tag=f"f{tag}gm{out_key}_{kt}")
        dst = G["grow"][0:1, 0:n_out]
        kb.tt(dst, pf[3][0:1, 0:n_out], bias_row[0:1, 0:n_out], ALU.add,
              reads=[("pf3", None), (bias_key, None), (R("grow"), None)], writes=[(R("grow"), None)],
              tag=f"f{tag}gb{out_key}")
        kb.act(gsc[0:1, 1024 + out_key[1] * 512: 1024 + out_key[1] * 512 + n_out],
               G["grow"][0:1, 0:n_out], fn,
               reads=[(R("grow"), None)], writes=[(R("gsc"), out_key)], tag=f"f{tag}ga{out_key}")
    rowmat([("gc", 0), ("gc", 1)], gsc[0:1, 0:1024], 1024, gw1, R("gw1"), 512, ("g1", 0), G["gb1"], R("gb1"), AF.Gelu)
    rowmat([("g1", 0)], gsc[0:1, 1024:1536], 512, gw2, R("gw2"), 256, ("g2", 1), G["gb2"], R("gb2"), AF.Gelu)
    rowmat([("g2", 1)], gsc[0:1, 1536:1792], 256, gw3, R("gw3"), 1, ("g3", 2), G["gb3"], R("gb3"), AF.Sigmoid)
    gfv = gsc[0:1, 2048:2049]
    c1 = gsc[0:1, 2560:2561]
    c2 = gsc[0:1, 2561:2562]
    t0_ = gsc[0:1, 2562:2563]
    kb.ts(c1, arr[0:1, 0:1], 0.6, reads=[(R("arr"), None)], writes=[(R("gsc"), "c1")])
    kb.tt(t0_, arr[0:1, 0:1], gfv, ALU.mult, reads=[(R("arr"), None), (R("gsc"), ("g3", 2))],
          writes=[(R("gsc"), "t0")])
    kb.ts(t0_, t0_, 1.4, reads=[(R("gsc"), "t0")], writes=[(R("gsc"), "t0")])
    kb.stt(c2, arr[0:1, 0:1], -1.0, t0_, ALU.mult, ALU.add,
           reads=[(R("arr"), None), (R("gsc"), "t0")], writes=[(R("gsc"), "c2")])
    kb.ts(c2, c2, 0.1, op0=ALU.add, reads=[(R("gsc"), "c2")], writes=[(R("gsc"), "c2")])

    def ffn_tile(t):
        fp = t % 2
        ob = obP[fp]; h1t = h1tP[fp]; h2t = h2tP[fp]; h3t = h3tP[fp]; tw = twP[fp]
        ob_n = R(f"ob{fp}"); h1_n = R(f"h1t{fp}"); h2_n = R(f"h2t{fp}")
        h3_n = R(f"h3t{fp}"); tw_n = R(f"tw{fp}")
        bh1 = (pf[0], "pf0", pf[1], "pf1") if fp == 0 else (pf[4], "pf4", pf[5], "pf5")
        bh2 = (pf[2], "pf2") if fp == 0 else (pf[3], "pf3")
        bh3 = (pf[0], "pf0") if fp == 0 else (pf[4], "pf4")
        bt4 = (pf[1], "pf1") if fp == 0 else (pf[5], "pf5")
        for cyc in range(3):
            for go in range(8):
                kb.act(ob[:, go * 512:(go + 1) * 512], outf_v[:, go, t * TT:(t + 1) * TT],
                       AF.Copy,
                       reads=[(R("outf"), (t * 4 + i2, go // 4)) for i2 in range(4)],
                       writes=[(ob_n, go)], tag=f"f{tag}ob{t}_{cyc}_{go}")
            yield
            for go in range(8):
                bank = bh1[0] if go % 2 == 0 else bh1[2]
                bres = bh1[1] if go % 2 == 0 else bh1[3]
                for kt in range(8):
                    kb.mm(bank[:], cn1[:, kt * 1024 + go * 128: kt * 1024 + (go + 1) * 128],
                          ob[:, kt * 512:(kt + 1) * 512], start=(kt == 0), stop=(kt == 7),
                          reads=[(R("cn1"), None), (ob_n, kt)], writes=[(bres, None)],
                          tag=f"f{tag}h1m{t}_{cyc}_{go}_{kt}")
                kb.act(h1t[:, go * 512:(go + 1) * 512], bank[:], AF.Gelu,
                       bias=cb1f[:, go:go + 1],
                       reads=[(bres, None), (R("cb1f"), None)], writes=[(h1_n, go)],
                       tag=f"f{tag}h1a{t}_{cyc}_{go}")
            yield
            for go in range(4):
                bank, bres = bh2
                for kt in range(8):
                    kb.mm(bank[:], cn2[:, kt * 512 + go * 128: kt * 512 + (go + 1) * 128],
                          h1t[:, kt * 512:(kt + 1) * 512], start=(kt == 0), stop=(kt == 7),
                          reads=[(R("cn2"), None), (h1_n, kt)], writes=[(bres, None)],
                          tag=f"f{tag}h2m{t}_{cyc}_{go}_{kt}")
                kb.act(h2t[:, go * 512:(go + 1) * 512], bank[:], AF.Gelu,
                       bias=cb2f[:, go:go + 1],
                       reads=[(bres, None), (R("cb2f"), None)], writes=[(h2_n, go)],
                       tag=f"f{tag}h2a{t}_{cyc}_{go}")
            yield
            for go in range(2):
                bank, bres = bh3
                for kt in range(4):
                    kb.mm(bank[:], cn3[:, kt * 256 + go * 128: kt * 256 + (go + 1) * 128],
                          h2t[:, kt * 512:(kt + 1) * 512], start=(kt == 0), stop=(kt == 3),
                          reads=[(R("cn3"), None), (h2_n, kt)], writes=[(bres, None)],
                          tag=f"f{tag}h3m{t}_{cyc}_{go}_{kt}")
                kb.act(h3t[:, go * 512:(go + 1) * 512], bank[:], AF.Gelu,
                       bias=cb3f[:, go:go + 1],
                       reads=[(bres, None), (R("cb3f"), None)], writes=[(h3_n, go)],
                       tag=f"f{tag}h3a{t}_{cyc}_{go}")
            yield
            for kt in range(2):
                kb.mm(bt4[0][0:1, 0:512], cn4[:, kt:kt + 1], h3t[:, kt * 512:(kt + 1) * 512],
                      start=(kt == 0), stop=(kt == 1),
                      reads=[(R("cn4"), None), (h3_n, kt)], writes=[(bt4[1], None)],
                      tag=f"f{tag}t4m{t}_{cyc}_{kt}")
            tfrow = gsc[0:1, 2563 + fp * 520: 3075 + fp * 520]
            kb.act(tfrow, bt4[0][0:1, 0:512], AF.Sigmoid, bias=cb4[0:1, 0:1],
                   reads=[(bt4[1], None), (R("cb4"), None), (R("gsc"), ("tf", fp))], writes=[(R("gsc"), ("tf", fp))],
                   tag=f"f{tag}t4a{t}_{cyc}")
            kb.ts(tfrow, tfrow, c1, c2, ALU.mult, ALU.add,
                  reads=[(R("gsc"), ("tf", fp)), (R("gsc"), "c1"), (R("gsc"), "c2")],
                  writes=[(R("gsc"), ("tf", fp))],
                  tag=f"f{tag}cpl{t}_{cyc}")
            kb.mm(bt4[0][:, 0:512], ones_row[:], tfrow, True, True,
                  reads=[("ones_row", None), (R("gsc"), ("tf", fp))], writes=[(bt4[1], None)],
                  tag=f"f{tag}cbc{t}_{cyc}")
            yield
            for g in range(8):
                gsrc = (g + 4) % 8
                sign = 1.0 if g in (0, 1, 6, 7) else -1.0
                kb.stt(tw[:], bt4[0][:, 0:512], sign, ob[:, gsrc * 512:(gsrc + 1) * 512],
                       ALU.mult, ALU.mult,
                       reads=[(bt4[1], None), (ob_n, gsrc), (tw_n, None)], writes=[(tw_n, None)],
                       tag=f"f{tag}tw{t}_{cyc}_{g}")
                kb.tt(outf_v[:, g, t * TT:(t + 1) * TT],
                      outf_v[:, g, t * TT:(t + 1) * TT], tw[:], ALU.add,
                      reads=[(tw_n, None)] + [(R("outf"), (t * 4 + i2, g // 4)) for i2 in range(4)],
                      writes=[(R("outf"), (t * 4 + i2, g // 4)) for i2 in range(4)],
                      tag=f"f{tag}up{t}_{cyc}_{g}")
        yield
        for i in range(4):
            blk = t * 4 + i
            r = blk % 4
            xv = xt[:, r * 1024:(r + 1) * 1024]
            kb.dma("sp", xv, xsr[blk], reads=[(R("xt"), r), (G["x_src_name"], blk)],
               writes=[(R("xt"), r)], tag=f"f{tag}rx{blk}")
            for half in range(2):
                bank = (bh1[0] if half == 0 else bh1[2])
                bres = (bh1[1] if half == 0 else bh1[3])
                for g4 in range(4):
                    g = half * 4 + g4
                    kb.tr(bank[:, g4 * 128:(g4 + 1) * 128],
                          outf_v[:, g, t * TT + i * 128: t * TT + (i + 1) * 128], idf[:],
                          reads=[(R("outf"), (t * 4, g // 4)), (R("outf"), (t * 4 + i, g // 4)), ("idf", None)],
                          writes=[(bres, None)], tag=f"f{tag}rT{blk}_{g}")
                kb.tt(xv[:, half * 512:(half + 1) * 512], xv[:, half * 512:(half + 1) * 512],
                      bank[:], ALU.add, reads=[(R("xt"), r), (bres, None)], writes=[(R("xt"), r)],
                      tag=f"f{tag}ra{blk}_{half}")
            kb.dma("sp", xdr[blk], xv, reads=[(R("xt"), r)], writes=[(G["x_dst_name"], blk)],
                   tag=f"f{tag}rs{blk}")
            yield


    weave(ffn_tile(0), ffn_tile(1))
    weave(ffn_tile(2), ffn_tile(3))

def attnb_phase(kb, B, n_chunks=NBLK):
    pf, pb = B["pf"], B["pb"]
    idf, idb, konst = B["idf"], B["idb"], B["konst"]
    ones_row = B["ones_row"]
    rkT, Wst, Wbf, WwT, WrT = B["rkT"], B["Wst"], B["Wbf"], B["WwT"], B["WrT"]
    MT, lgcol, gpcol, gwcol, gccol, acol = B["MT"], B["lgcol"], B["gpcol"], B["gwcol"], B["gccol"], B["acol"]
    rdbf, rdT, SMT, scrf = B["rdbf"], B["rdT"], B["SMT"], B["scrf"]
    x_src, x_dst = B["x_src"], B["x_dst"]
    xsr = x_src.rearrange("(b p) d -> b p d", p=128)
    xdr = x_dst.rearrange("(b p) d -> b p d", p=128)
    rkT_v = rkT[:].rearrange("p (g t) -> p g t", g=8)
    def chunk_front(c):
        pp = c % 2
        xt = B["xt"][c % 4]; lnt32 = B["lnt32"][pp]; lnbf = B["lnbf"][pp]
        vt = B["vt"][pp]; vgw = B["vgw"][pp]; wkt = B["wkt"][pp]
        statb = B["statb"] if pp == 0 else B["statb2"]
        sb_n = "statb" if pp == 0 else "statb2"
        xt_n = f"xtb{c % 4}"; ln_n = f"lnt32{pp}"; lb_n = f"lnbf{pp}"
        vt_n = f"vt{pp}"; vg_n = f"vgw{pp}"; wk_n = f"wkt{pp}"
        vbank, vb_n = (pf[0], "pf0") if pp == 0 else (pf[3], "pf3")
        kb.dma("sp", xt[:], xsr[c], reads=[(xt_n, None), (B["x_src_name"], c)],
               writes=[(xt_n, None)], tag=f"blx{c}")
        yield
        sc = statb
        kb.act(B["trash"][:], xt[:], AF.Square, accum_out=sc[:, 0:1],
               reads=[(xt_n, None)], writes=[("trashb", None), (sb_n, 0)], tag=f"bsq{c}")
        kb.act(B["trash"][:], xt[:], AF.Copy, accum_out=sc[:, 1:2],
               reads=[(xt_n, None)], writes=[("trashb", None), (sb_n, 1)], tag=f"bsm{c}")
        kb.ts(sc[:, 2:3], sc[:, 1:2], 1.0 / D, reads=[(sb_n, 1)], writes=[(sb_n, 2)])
        kb.ts(sc[:, 3:4], sc[:, 0:1], 1.0 / D, reads=[(sb_n, 0)], writes=[(sb_n, 3)])
        kb.tt(sc[:, 4:5], sc[:, 2:3], sc[:, 2:3], ALU.mult, reads=[(sb_n, 2)], writes=[(sb_n, 4)])
        kb.tt(sc[:, 5:6], sc[:, 3:4], sc[:, 4:5], ALU.subtract,
              reads=[(sb_n, 3), (sb_n, 4)], writes=[(sb_n, 5)])
        kb.act(sc[:, 6:7], sc[:, 5:6], AF.Sqrt, bias=konst[:, 0:1],
               reads=[(sb_n, 5), ("konst", None)], writes=[(sb_n, 6)], tag=f"brt{c}")
        kb.recip(sc[:, 7:8], sc[:, 6:7], reads=[(sb_n, 6)], writes=[(sb_n, 7)])
        yield
        kb.ts(lnt32[:], xt[:], sc[:, 2:3], sc[:, 7:8], ALU.subtract, ALU.mult,
              reads=[(xt_n, None), (sb_n, 2), (sb_n, 7), (ln_n, None)],
              writes=[(ln_n, None)], tag=f"bln{c}")
        kb.act(lnbf[:], lnt32[:], AF.Copy, reads=[(ln_n, None)], writes=[(lb_n, None)],
               tag=f"blnb{c}")
        yield
        for half in range(2):
            bank = pb[half]
            for g4 in range(4):
                g = half * 4 + g4
                kb.tr(bank[:, g4 * 128:(g4 + 1) * 128], lnbf[:, g * 128:(g + 1) * 128], idb[:],
                      reads=[(lb_n, None), ("idb", None)], writes=[(f"pb{half}", None)],
                      tag=f"btr{c}_{g}")
            kb.cp(rkT_v[:, half * 4:(half + 1) * 4, 1 + c * 128: 1 + (c + 1) * 128],
                  bank[:, 0:512].rearrange("p (g t) -> p g t", g=4), eng="dve",
                  reads=[(f"pb{half}", None)], writes=[("rkT", (c, half))],
                  tag=f"bev{c}_{half}")
        yield
        rk_reads = [("rkT", (c, 0)), ("rkT", (c, 1))]
        for half in range(2) if "B" in ATTNB_PARTS else []:
            for kt in range(8):
                kb.mm(vbank[:], rkT_v[:, kt, 1 + c * 128: 1 + (c + 1) * 128],
                      WwT[:, kt * 1024 + half * 512: kt * 1024 + (half + 1) * 512],
                      start=(kt == 0), stop=(kt == 7),
                      reads=rk_reads + [("WwT", None)], writes=[(vb_n, None)],
                      tag=f"bvm{c}_{half}_{kt}")
            kb.act(vt[:, half * 512:(half + 1) * 512], vbank[:], AF.Copy,
                   reads=[(vb_n, None)], writes=[(vt_n, half)], tag=f"bve{c}_{half}")
            kb.ts(vgw[:, half * 512:(half + 1) * 512], vbank[:], gwcol[:], op0=ALU.mult,
                  reads=[(vb_n, None), ("gwcol", None)], writes=[(vg_n, half)],
                  tag=f"bvg{c}_{half}")
        yield
        for half in range(2) if "C" in ATTNB_PARTS else []:
            bank = pb[half]
            for g4 in range(4):
                g = half * 4 + g4
                kb.tr(bank[:, g4 * 128:(g4 + 1) * 128], rkT_v[:, g, c * 128: (c + 1) * 128],
                      idb[:],
                      reads=[("rkT", (c, half)), ("rkT", (c - 1, half)), ("idb", None)],
                      writes=[(f"pb{half}", None)], tag=f"bwk{c}_{g}")
            kb.cp(wkt[:, half * 512:(half + 1) * 512],
                  bank[:, 0:512], eng="dve",
                  reads=[(f"pb{half}", None)], writes=[(wk_n, half)],
                  tag=f"bwke{c}_{half}")
        yield

    def chunk_tail(c):
        pp = c % 2
        xt = B["xt"][c % 4]; lnt32 = B["lnt32"][pp]
        vt = B["vt"][pp]; vgw = B["vgw"][pp]; wkt = B["wkt"][pp]
        xt_n = f"xtb{c % 4}"; ln_n = f"lnt32{pp}"
        vt_n = f"vt{pp}"; vg_n = f"vgw{pp}"; wk_n = f"wkt{pp}"
        rk_reads = [("rkT", (c, 0)), ("rkT", (c, 1))]
        for half in range(2) if "D" in ATTNB_PARTS else []:
            bank = pf[1 + half]
            for kt in range(8):
                kb.mm(bank[:], rkT_v[:, kt, 1 + c * 128: 1 + (c + 1) * 128],
                      Wbf[:, kt * 1024 + half * 512: kt * 1024 + (half + 1) * 512],
                      start=(kt == 0), stop=(kt == 7),
                      reads=rk_reads + [("Wbf", None)], writes=[(f"pf{1 + half}", None)],
                      tag=f"bim{c}_{half}_{kt}")
        yield
        for kt in range(8) if "E" in ATTNB_PARTS else []:
            kb.mm(pf[3][:, 0:128], rkT_v[:, kt, c * 128:(c + 1) * 128],
                  rkT_v[:, kt, 1 + c * 128: 1 + (c + 1) * 128],
                  start=(kt == 0), stop=(kt == 7),
                  reads=rk_reads + [("rkT", (c - 1, 0)), ("rkT", (c - 1, 1))],
                  writes=[("pf3", None)], tag=f"bsm{c}_{kt}")
        if "E" in ATTNB_PARTS:
            kb.tt(SMT[:], pf[3][:, 0:128], MT[:], ALU.mult,
                  reads=[("pf3", None), ("MT", None)], writes=[("SMT", None)], tag=f"bsmt{c}")
        yield
        for half in range(2) if "E" in ATTNB_PARTS else []:
            kb.mm(pf[4 + half][:], SMT[:], vt[:, half * 512:(half + 1) * 512], True, True,
                  reads=[("SMT", None), (vt_n, half)], writes=[(f"pf{4 + half}", None)],
                  tag=f"bia{c}_{half}")
        yield
        for half in range(2) if "E" in ATTNB_PARTS and "D" in ATTNB_PARTS else []:
            kb.act(scrf[:, 0:512], pf[4 + half][:], AF.Copy,
                   reads=[(f"pf{4 + half}", None), ("scrf", None)], writes=[("scrf", None)],
                   tag=f"bre{c}_{half}")
            kb.stt(rdbf[:, half * 512:(half + 1) * 512], pf[1 + half][:], gpcol[:],
                   scrf[:, 0:512], ALU.mult, ALU.add,
                   reads=[(f"pf{1 + half}", None), ("gpcol", None), ("scrf", None)],
                   writes=[("rdbf", half)], tag=f"brd{c}_{half}")
        yield
        for half in range(2) if "F" in ATTNB_PARTS else []:
            bank = pb[half]
            for g4 in range(4):
                g = half * 4 + g4
                kb.tr(bank[:, g4 * 128:(g4 + 1) * 128], rdbf[:, g * 128:(g + 1) * 128], idb[:],
                      reads=[("rdbf", 0), ("rdbf", 1), ("idb", None)],
                      writes=[(f"pb{half}", None)], tag=f"brT{c}_{g}")
            kb.cp(rdT[:, half * 512:(half + 1) * 512], bank[:, 0:512], eng="dve",
                  reads=[(f"pb{half}", None)], writes=[("rdT", half)], tag=f"brTe{c}_{half}")
        yield
        for half in range(2) if "F" in ATTNB_PARTS else []:
            bank = pf[1 + half]
            for kt in range(8):
                kb.mm(bank[:], rdT[:, kt * 128:(kt + 1) * 128],
                      WrT[:, kt * 1024 + half * 512: kt * 1024 + (half + 1) * 512],
                      start=(kt == 0), stop=(kt == 7),
                      reads=[("rdT", 0), ("rdT", 1), ("WrT", None)],
                      writes=[(f"pf{1 + half}", None)], tag=f"brom{c}_{half}_{kt}")
            kb.stt(scrf[:, 512:1024], bank[:], acol[:], lnt32[:, half * 512:(half + 1) * 512],
                   ALU.mult, ALU.add,
                   reads=[(f"pf{1 + half}", None), ("acol", None), (ln_n, None), ("scrf", None)],
                   writes=[("scrf", None)], tag=f"bxu{c}_{half}")
            kb.tt(xt[:, half * 512:(half + 1) * 512], xt[:, half * 512:(half + 1) * 512],
                  scrf[:, 512:1024], ALU.add,
                  reads=[(xt_n, None), ("scrf", None)], writes=[(xt_n, None)],
                  tag=f"bxa{c}_{half}")
        yield
        kb.dma("sp", xdr[c], xt[:], reads=[(xt_n, None)], writes=[(B["x_dst_name"], c)],
               tag=f"bxs{c}")
        yield
        for jt in range(8) if "G" in ATTNB_PARTS else []:
            for half in range(2):
                bank = pf[4 + half]
                kb.mm(bank[:], wkt[:, jt * 128:(jt + 1) * 128],
                      vgw[:, half * 512:(half + 1) * 512], True, True,
                      reads=[(wk_n, jt // 4), (vg_n, half)], writes=[(f"pf{4 + half}", None)],
                      tag=f"bwu{c}_{jt}_{half}")
                dst = Wst[:, jt * 1024 + half * 512: jt * 1024 + (half + 1) * 512]
                kb.stt(dst, dst, gccol[:], bank[:], ALU.mult, ALU.add,
                       reads=[("Wst", (jt, half)), ("gccol", None), (f"pf{4 + half}", None)],
                       writes=[("Wst", (jt, half))], tag=f"bws{c}_{jt}_{half}")
                kb.cp(Wbf[:, jt * 1024 + half * 512: jt * 1024 + (half + 1) * 512], dst,
                      eng="act", reads=[("Wst", (jt, half)), ("Wbf", (jt, half))],
                      writes=[("Wbf", (jt, half))], tag=f"bwb{c}_{jt}_{half}")


    weave(chunk_front(0))
    for c in range(n_chunks):
        weave(chunk_tail(c), chunk_front(c + 1) if c + 1 < n_chunks else None)

def build_program(dbg=False, stop_after=None):
    nc = bass.Bass("TRN2", target_bir_lowering=False,
                   detect_race_conditions=EMIT_DRAINS)
    kb = KB(nc, dbg=dbg)
    s = kb.s

    x_hbm = kb.din("x", [T, D])
    xr = x_hbm.rearrange("(b p) d -> b p d", p=128)
    consts = prep_consts()
    cin = {k: kb.din("c_" + k, list(v.shape)) for k, v in consts.items()}
    wshapes = {k: list(v.shape) for k, v in prep_weights(_dummy_ins()).items()}
    win = {k: kb.din("w_" + k, v) for k, v in wshapes.items()}
    out_hbm = kb.dout("out", [T, D])
    outr = out_hbm.rearrange("(b p) d -> b p d", p=128)
    ssh_scr = kb.dint("ssh_scr", [8, 128, T], BF16)
    ohn_scr = kb.dint("ohn_scr", [8, 128, T], BF16)
    xa_scr = kb.dint("xa_scr", [T, D], F32)
    xb_scr = kb.dint("xb_scr", [T, D], F32)

    idf = kb.sbuf("idf", [128, 128], F32)
    idb = kb.sbuf("idb", [128, 128], BF16)
    ones_row = kb.sbuf("ones_row", [1, 128], F32)
    ones_row_b = kb.sbuf("ones_row_b", [1, 128], BF16)
    ones_col_b = kb.sbuf("ones_col_b", [128, 1], BF16)
    konst = kb.sbuf("konst", [128, 8], F32)
    maskSL = kb.sbuf("maskSL", [128, 128], BF16)
    maskIU = kb.sbuf("maskIU", [128, 128], BF16)
    lnxf = kb.sbuf("lnxf", [128, 8 * T], BF16)
    m1main = kb.sbuf("m1main", [128, 8 * 1024], BF16)
    bwT = kb.sbuf("bwT", [128, 8 * 4], BF16)
    beta_cols = kb.sbuf("beta_cols", [128, NBLK * 4], F32)
    xin = kb.sbuf("xin", [128, 4 * 1024], F32)
    lntok = kb.sbuf("lntok", [128, 4 * 1024], BF16)
    trash = kb.sbuf("trash", [128, 1024], BF16)
    statc = kb.sbuf("statc", [128, 4 * 8], F32)
    ssh_t = kb.sbuf("ssh_t", [128, 2 * 512], BF16)
    beta_rows = None

    pf = [kb.psum(f"pf{i}", [128, 512], F32) for i in range(6)]
    pb = [kb.psum(f"pb{i}", [128, 1024], BF16) for i in range(2)]

    kb.dma("sp", idf[:], cin["identity"], writes=[("idf", None)])
    kb.dma("pool", idb[:], cin["identity"], writes=[("idb", None)])
    kb.dma("sp", ones_row[:], cin["ones_row"], writes=[("ones_row", None)])
    kb.dma("sp", konst[:], cin["konst"], writes=[("konst", None)])
    kb.dma("pool", ones_row_b[:], cin["ones_row"], writes=[("ones_row_b", None)])
    kb.dma("pool", ones_col_b[:], cin["ones_col"], writes=[("ones_col_b", None)])
    kb.dma("pool", maskSL[:], cin["maskSL"], writes=[("maskSL", None)])
    kb.dma("pool", maskIU[:], cin["maskIU"], writes=[("maskIU", None)])
    kb.dma("pool", m1main[:].rearrange("p (kt o) -> p kt o", kt=8),
           win["m1main"].rearrange("(kt p) o -> p kt o", p=128),
           writes=[("m1main", None)])
    kb.dma("pool", bwT[:].rearrange("p (kt h) -> p kt h", kt=8),
           win["bwT"].rearrange("(kt p) h -> p kt h", p=128),
           writes=[("bwT", None)])

    RL = 4
    for blk in range(NBLK):
        r = blk % RL
        xv = xin[:, r * 1024:(r + 1) * 1024]
        kb.dma("sp", xv, xr[blk], reads=[("xin", r)], writes=[("xin", r)], tag=f"ldx{blk}")
        sc = statc[:, r * 8:(r + 1) * 8]
        kb.act(trash[:], xv, AF.Square, accum_out=sc[:, 0:1],
               reads=[("xin", r)], writes=[("trash", None), ("statc", (r, 0))], tag=f"sq{blk}")
        kb.act(trash[:], xv, AF.Copy, accum_out=sc[:, 1:2],
               reads=[("xin", r)], writes=[("trash", None), ("statc", (r, 1))], tag=f"sum{blk}")
        kb.ts(sc[:, 2:3], sc[:, 1:2], 1.0 / D, reads=[("statc", (r, 1))],
              writes=[("statc", (r, 2))])
        kb.ts(sc[:, 3:4], sc[:, 0:1], 1.0 / D, reads=[("statc", (r, 0))],
              writes=[("statc", (r, 3))])
        kb.tt(sc[:, 4:5], sc[:, 2:3], sc[:, 2:3], ALU.mult,
              reads=[("statc", (r, 2))], writes=[("statc", (r, 4))])
        kb.tt(sc[:, 5:6], sc[:, 3:4], sc[:, 4:5], ALU.subtract,
              reads=[("statc", (r, 3)), ("statc", (r, 4))], writes=[("statc", (r, 5))])
        kb.act(sc[:, 6:7], sc[:, 5:6], AF.Sqrt, bias=konst[:, 0:1],
               reads=[("statc", (r, 5)), ("konst", None)], writes=[("statc", (r, 6))])
        kb.recip(sc[:, 7:8], sc[:, 6:7], reads=[("statc", (r, 6))],
                 writes=[("statc", (r, 7))])
        lv = lntok[:, r * 1024:(r + 1) * 1024]
        kb.ts(lv, xv, sc[:, 2:3], sc[:, 7:8], ALU.subtract, ALU.mult,
              reads=[("xin", r), ("statc", (r, 2)), ("statc", (r, 7)), ("lntok", r)],
              writes=[("lntok", r)], tag=f"ln{blk}")
        for half in range(2):
            bank = pb[half]
            for g4 in range(4):
                g = half * 4 + g4
                kb.tr(bank[:, g4 * 128:(g4 + 1) * 128], lv[:, g * 128:(g + 1) * 128], idb[:],
                      reads=[("lntok", r), ("idb", None)], writes=[(f"pb{half}", None)],
                      tag=f"trln{blk}_{g}")
            lnxf_v = lnxf[:].rearrange("p (g t) -> p g t", g=8)
            kb.cp(lnxf_v[:, half * 4:(half + 1) * 4, blk * 128:(blk + 1) * 128],
                  bank[:, 0:512].rearrange("p (g t) -> p g t", g=4),
                  eng="dve",
                  reads=[(f"pb{half}", None)], writes=[("lnxf", (blk, half))],
                  tag=f"evln{blk}_{half}")

    kb.dump("lnxf", lnxf[:], [("lnxf", None)], cast=True)

    kb.s.retag("xin")
    beta_rows = xin[0:4, 0:T]
    for t in range(NT):
        bank = pf[t % 2]
        for kt in range(8):
            kb.mm(bank[0:4, 0:512], bwT[:, kt * 4:(kt + 1) * 4],
                  lnxf[:].rearrange("p (g t) -> p g t", g=8)[:, kt, t * TT:(t + 1) * TT],
                  start=(kt == 0), stop=(kt == 7),
                  reads=[("bwT", None)] + [("lnxf", (t * 4 + i, h)) for i in range(4) for h in range(2)],
                  writes=[(f"pf{t % 2}", None)], tag=f"betamm{t}_{kt}")
        kb.act(beta_rows[:, t * TT:(t + 1) * TT], bank[0:4, 0:512], AF.Sigmoid,
               reads=[(f"pf{t % 2}", None)], writes=[("xin", ("beta", t))], tag=f"betasig{t}")
        kb.ts(beta_rows[:, t * TT:(t + 1) * TT], beta_rows[:, t * TT:(t + 1) * TT],
              1e-6, op0=ALU.max, reads=[("xin", ("beta", t))], writes=[("xin", ("beta", t))])
    for q in range(4):
        bank = pf[2]
        for i in range(4):
            blk = q * 4 + i
            kb.tr(bank[:, i * 4:(i + 1) * 4], beta_rows[0:4, blk * 128:(blk + 1) * 128],
                  idf[0:4, 0:4],
                  reads=[("xin", ("beta", blk // 4)), ("idf", None)], writes=[("pf2", None)],
                  tag=f"trbeta{blk}")
        kb.cp(beta_cols[:, q * 16:(q + 1) * 16], bank[:, 0:16], eng="dve",
              reads=[("pf2", None)], writes=[("beta_cols", q)], tag=f"evbeta{q}")
    kb.dump("beta_cols", beta_cols[:], [("beta_cols", None)])

    lnxf_v = lnxf[:].rearrange("p (g t) -> p g t", g=8)
    for t in range(NT):
        ln_reads = [("lnxf", (t * 4 + i, h)) for i in range(4) for h in range(2)]
        for go in range(8):
            bank = pf[go % 2]
            for kt in range(8):
                kb.mm(bank[:], m1main[:, kt * 1024 + go * 128: kt * 1024 + (go + 1) * 128],
                      lnxf_v[:, kt, t * TT:(t + 1) * TT],
                      start=(kt == 0), stop=(kt == 7),
                      reads=[("m1main", None)] + ln_reads,
                      writes=[(f"pf{go % 2}", None)], tag=f"sshmm{t}_{go}_{kt}")
            rr = (t * 8 + go) % 2
            sshv = ssh_t[:, rr * 512:(rr + 1) * 512]
            kb.act(sshv, bank[:], AF.Copy,
                   reads=[(f"pf{go % 2}", None), ("ssh_t", rr)], writes=[("ssh_t", rr)],
                   tag=f"sshev{t}_{go}")
            kb.dma("sp", ssh_scr[go, :, t * TT:(t + 1) * TT], sshv,
                   reads=[("ssh_t", rr)], writes=[("ssh_scr", (t, go))], tag=f"sshst{t}_{go}")

    if stop_after == "phaseA":
        return nc, kb

    wq = kb.sbuf("wq", [128, 2048], BF16)
    wk = kb.sbuf("wk", [128, 2048], BF16)
    wv = kb.sbuf("wv", [128, 2048], BF16)
    qcw = kb.sbuf("qcw", [128, 32], F32)
    kcw = kb.sbuf("kcw", [128, 32], F32)
    vcw = kb.sbuf("vcw", [128, 32], F32)
    m1tail = kb.sbuf("m1tail", [16, 1024], BF16)
    m2T = kb.sbuf("m2T", [128, 32], BF16)
    b1f = kb.sbuf("b1f", [128, 8], F32)
    normw_bc = kb.sbuf("normw_bc", [128, 256], F32)
    crl_sig = kb.sbuf("crl_sig", [128, 4], F32)
    trec = kb.sbuf("trec", [128, 1], F32)
    srow = kb.sbuf("srow", [1, 8], F32)
    nbeta_cols = kb.sbuf("nbeta_cols", [128, NBLK * 4], F32)
    pre = kb.sbuf("pre", [128, 2 * 516], BF16)
    qf = kb.sbuf("qf", [128, 2 * 2048], BF16)
    kf = kb.sbuf("kf", [128, 2 * 2048], BF16)
    vfp = kb.sbuf("vfp", [128, 2 * 2112], BF16)
    kn_tok = kb.sbuf("kn_tok", [128, 4096], BF16)
    kbn_tok = kb.sbuf("kbn_tok", [128, 4096], BF16)
    v_tok = kb.sbuf("v_tok", [128, 4096], BF16)
    vb_tok = kb.sbuf("vb_tok", [128, 4096], BF16)
    copyA = kb.sbuf("copyA", [128, 4096], BF16)
    copyB = kb.sbuf("copyB", [128, 17 * 256], BF16)
    dout = kb.sbuf("dout", [128, 4096], BF16)
    ls_tok = qf
    ll_tok = kf
    S32 = kb.sbuf("S32", [128, 512], F32)
    Sbf = kb.sbuf("Sbf", [128, 512], BF16)
    dpar = []
    for pp in range(2):
        d = {}
        for nm, w_ in (("Y32", 128), ("Ybf", 128), ("Pt", 128), ("PTt", 128),
                       ("Q0", 128), ("QT0", 128), ("Q1", 128), ("QT1", 128),
                       ("attnT", 128), ("wT_sb", 256), ("u_sb", 256), ("uadj", 256)):
            dt_ = F32 if nm == "Y32" else BF16
            d[nm] = kb.sbuf(f"{nm}_{pp}", [128, w_], dt_)
        dpar.append(d)
    sq = trash
    rq = kb.sbuf("rq", [1, 512], F32)
    stcr = [kb.sbuf(f"stcr{p}", [128, 16], F32) for p in range(2)]
    stc = [kb.sbuf(f"stc{p}", [128, 16], F32) for p in range(2)]
    statsT = kb.sbuf("statsT", [16, 512], BF16)
    h1s = kb.sbuf("h1s", [128, 1024], BF16)
    sshd = kb.sbuf("sshd", [128, 4096], BF16)
    h1f = copyB
    sshin = None
    lgr = kb.sbuf("lgr", [4, 512], F32)
    lgc = kb.sbuf("lgc", [128, 16], F32)
    pc = kb.sbuf("pc", [128, 16], F32)
    pcol = [kb.sbuf(f"pcol{p}", [128, 1], F32) for p in range(2)]
    o_t = [kb.sbuf(f"o_t{p}", [128, 256], BF16) for p in range(2)]
    o_t2 = [kb.sbuf(f"o_t2{p}", [128, 256], BF16) for p in range(2)]
    ohn_t = kb.sbuf("ohn_t", [128, 1024], BF16)
    trash2 = trash

    kb.dma("pool", m1tail[:].rearrange("s (kt o) -> s kt o", kt=8),
           win["m1tail"].rearrange("s (kt o) -> s kt o", kt=8), writes=[("m1tail", None)])
    kb.dma("pool", m2T[:].rearrange("p (kt f) -> p kt f", kt=8),
           win["m2T"].rearrange("(kt p) f -> p kt f", p=128), writes=[("m2T", None)])
    kb.dma("sp", b1f[:], win["b1f"], writes=[("b1f", None)])
    for nm, wkey in (("qcw", "qcw"), ("kcw", "kcw"), ("vcw", "vcw")):
        kb.dma("sp", kb.sb[nm][:], win[wkey].rearrange("p g j -> p (g j)"),
               writes=[(nm, None)])
    nwrow = xin[0:1, 2048:2304]
    kb.s.retag("xin")
    kb.dma("sp", nwrow, win["normw_row"], reads=[("xin", "nw")], writes=[("xin", "nw")])
    kb.mm(pf[5][:, 0:256], ones_row[:], nwrow, True, True,
          reads=[("ones_row", None), ("xin", "nw")], writes=[("pf5", None)], tag="nwbc")
    kb.cp(normw_bc[:], pf[5][:, 0:256], eng="dve", reads=[("pf5", None)],
          writes=[("normw_bc", None)])
    crl_row = kb.sbuf("crl_row", [1, 4], F32)
    kb.dma("sp", crl_row[:], win["crl_row"], writes=[("crl_row", None)])
    kb.act(crl_row[:], crl_row[:], AF.Sigmoid, reads=[("crl_row", None)],
           writes=[("crl_row", None)])
    kb.mm(pf[5][:, 256:260], ones_row[:], crl_row[:], True, True,
          reads=[("ones_row", None), ("crl_row", None)], writes=[("pf5", None)], tag="crlbc")
    kb.cp(crl_sig[:], pf[5][:, 256:260], eng="dve", reads=[("pf5", None)],
          writes=[("crl_sig", None)])
    ltrow = kb.sbuf("ltrow", [1, 4], F32)
    kb.dma("sp", ltrow[:, 0:1], win["ltemp"], writes=[("ltrow", 0)])
    kb.act(ltrow[:, 1:2], ltrow[:, 0:1], AF.Abs, reads=[("ltrow", 0)], writes=[("ltrow", 1)])
    kb.act(ltrow[:, 1:2], ltrow[:, 1:2], AF.Exp, scale=-1.0, reads=[("ltrow", 1)], writes=[("ltrow", 1)])
    kb.act(ltrow[:, 1:2], ltrow[:, 1:2], AF.Ln, bias=konst[0:1, 2:3], reads=[("ltrow", 1), ("konst", None)], writes=[("ltrow", 1)])
    kb.act(ltrow[:, 2:3], ltrow[:, 0:1], AF.Relu, reads=[("ltrow", 0)], writes=[("ltrow", 2)])
    kb.tt(ltrow[:, 3:4], ltrow[:, 1:2], ltrow[:, 2:3], ALU.add,
          reads=[("ltrow", 1), ("ltrow", 2)], writes=[("ltrow", 3)])
    kb.recip(ltrow[:, 3:4], ltrow[:, 3:4], reads=[("ltrow", 3)], writes=[("ltrow", 3)])
    kb.mm(pf[5][:, 260:261], ones_row[:], ltrow[:, 3:4], True, True,
          reads=[("ones_row", None), ("ltrow", 3)], writes=[("pf5", None)], tag="trbc")
    kb.cp(trec[:], pf[5][:, 260:261], eng="dve", reads=[("pf5", None)], writes=[("trec", None)])
    m2b_bc = kb.sbuf("m2b_bc", [128, 4], F32)
    m2brow = kb.sbuf("m2brow", [1, 4], F32)
    kb.dma("sp", m2brow[:], win["m2b_row"], writes=[("m2brow", None)])
    kb.mm(pf[5][:, 261:265], ones_row[:], m2brow[:], True, True,
          reads=[("ones_row", None), ("m2brow", None)], writes=[("pf5", None)], tag="m2bc")
    kb.cp(m2b_bc[:], pf[5][:, 261:265], eng="dve", reads=[("pf5", None)],
          writes=[("m2b_bc", None)])
    kb.ts(nbeta_cols[:], beta_cols[:], -1.0, reads=[("beta_cols", None)],
          writes=[("nbeta_cols", None)])

    qf_v = qf[:].rearrange("p (g t) -> p g t", g=2)
    kf_v = kf[:].rearrange("p (g t) -> p g t", g=2)
    vfp_v = vfp[:].rearrange("p (g t) -> p g t", g=2)

    def load_head_weights(h):
        for nm, wsrc in (("wq", "qwT"), ("wk", "kwT"), ("wv", "vwT")):
            dst = kb.sb[nm][:].rearrange("p (kt c) -> p kt c", kt=8)
            srcv = win[wsrc].rearrange("(kt p) c -> p kt c", p=128)[:, :, h * 256:(h + 1) * 256]
            kb.dma("pool", dst, srcv, reads=[(nm, None)], writes=[(nm, None)],
                   tag=f"ldw_{nm}{h}")

    def proj_conv(h, tensor, wname, cwname, dstv, dst_name, pad):
        for g in range(2):
            for t in range(NT):
                bank = pf[t % 2]
                bank_res = f"pf{t % 2}"
                w_sb = kb.sb[wname]
                for kt in range(8):
                    kb.mm(bank[:], w_sb[:, kt * 256 + g * 128: kt * 256 + (g + 1) * 128],
                          lnxf_v[:, kt, t * TT:(t + 1) * TT],
                          start=(kt == 0), stop=(kt == 7),
                          reads=[(wname, None)] + [("lnxf", (t * 4 + i2, hh)) for i2 in range(4) for hh in range(2)],
                          writes=[(bank_res, None)], tag=f"pj{tensor}{h}{g}{t}")
                e = (g * NT + t) % 2
                pe_sl = pre[:, e * 516:(e + 1) * 516]
                kb.act(pe_sl[:, 4:516], bank[:], AF.Copy,
                       reads=[(bank_res, None), ("pre", e)], writes=[("pre", e)],
                       tag=f"pcv{tensor}{h}{g}{t}")
                if t == 0:
                    kb.memset(pe_sl[:, 0:4], 0.0, reads=[("pre", e)], writes=[("pre", e)])
                else:
                    ep = (g * NT + t - 1) % 2
                    kb.cp(pe_sl[:, 0:4], pre[:, ep * 516 + 512: ep * 516 + 516],
                          eng="dve", reads=[("pre", ep), ("pre", e)], writes=[("pre", e)])
                dsl = dstv[:, g, pad + t * TT: pad + (t + 1) * TT]
                cw = kb.sb[cwname]
                gg = h * 2 + g
                kb.ts(dsl, pe_sl[:, 1:513], cw[:, gg * 4 + 0: gg * 4 + 1],
                      op0=ALU.mult,
                      reads=[("pre", e), (cwname, None), (dst_name, (g, t))],
                      writes=[(dst_name, (g, t))], tag=f"cv0{tensor}{h}{g}{t}")
                for j in range(1, 4):
                    kb.stt(dsl, pe_sl[:, 1 + j: 513 + j], cw[:, gg * 4 + j: gg * 4 + j + 1],
                           dsl, ALU.mult, ALU.add,
                           reads=[("pre", e), (cwname, None), (dst_name, (g, t))],
                           writes=[(dst_name, (g, t))], tag=f"cv{j}{tensor}{h}{g}{t}")
                kb.act(dsl, dsl, AF.Silu, reads=[(dst_name, (g, t))],
                       writes=[(dst_name, (g, t))], tag=f"silu{tensor}{h}{g}{t}")

    def l2norm(h, srcv, src_name, pad):
        for t in range(NT):
            r = 0
            for g in range(2):
                kb.act(sq[:, g * 512:(g + 1) * 512], srcv[:, g, pad + t * TT: pad + (t + 1) * TT],
                       AF.Square, reads=[(src_name, (g, t))], writes=[("trash", None)],
                       tag=f"l2sq{src_name}{h}{t}{g}")
            for g in range(2):
                kb.mm(pf[2][0:1, 0:512], ones_col_b[:], sq[:, g * 512:(g + 1) * 512],
                      start=(g == 0), stop=(g == 1),
                      reads=[("ones_col_b", None), ("trash", None)],
                      writes=[("pf2", None)], tag=f"l2mm{src_name}{h}{t}{g}")
            rsl = rq[:, r * 512:(r + 1) * 512]
            kb.act(rsl, pf[2][0:1, 0:512], AF.Sqrt, bias=konst[0:1, 1:2],
                   reads=[("pf2", None), ("konst", None), ("rq", r)], writes=[("rq", r)],
                   tag=f"l2rt{src_name}{h}{t}")
            kb.recip(rsl, rsl, reads=[("rq", r)], writes=[("rq", r)])
            kb.mm(pf[2][:, 512 - 512:512] if False else pf[2][:, 0:512],
                  ones_row[:], rsl, True, True,
                  reads=[("ones_row", None), ("rq", r)], writes=[("pf2", None)],
                  tag=f"l2bc{src_name}{h}{t}")
            for g in range(2):
                dsl = srcv[:, g, pad + t * TT: pad + (t + 1) * TT]
                kb.tt(dsl, dsl, pf[2][:, 0:512], ALU.mult,
                      reads=[(src_name, (g, t)), ("pf2", None)],
                      writes=[(src_name, (g, t))], tag=f"l2ap{src_name}{h}{t}{g}")

    def transpose_blocks(h, srcv, src_name, src_col0, dsts, blk_range=range(NBLK),
                         col_of_blk=None, tag=""):
        W_src = srcv.shape[2]
        for blk in blk_range:
            bank = pb[blk % 2]
            bank_res = f"pb{blk % 2}"
            c0 = col_of_blk(blk) if col_of_blk else (src_col0 + blk * 128)
            W = min(128, W_src - c0)
            for g in range(2):
                kb.tr(bank[0:W, g * 128:(g + 1) * 128], srcv[:, g, c0:c0 + W], idb[:],
                      reads=[(src_name, None), ("idb", None)], writes=[(bank_res, None)],
                      tag=f"tr{tag}{h}_{blk}_{g}")
            for (dst, dst_name, kind, scol_fn) in dsts:
                dsl = dst[0:W, blk * 256:(blk + 1) * 256]
                if kind == "copy":
                    kb.cp(dsl, bank[0:W, 0:256], eng="dve",
                          reads=[(bank_res, None)], writes=[(dst_name, blk)],
                          tag=f"ev{tag}{dst_name}{h}_{blk}")
                else:
                    kb.ts(dsl, bank[0:W, 0:256], scol_fn(blk), op0=ALU.mult,
                          reads=[(bank_res, None), ("beta_cols", None)],
                          writes=[(dst_name, blk)], tag=f"ev{tag}{dst_name}{h}_{blk}")

    def delta_env(c):
        pp = c % 2
        D_ = dpar[pp]
        def RN(b):
            return f"{b}_{pp}"
        bA, bB, bC = (("pf0", pf[0]), ("pf1", pf[1]), ("pf2", pf[2])) if pp == 0 \
            else (("pf3", pf[3]), ("pf4", pf[4]), ("pf5", pf[5]))
        pbank = ("pb0", pb[0]) if pp == 0 else ("pb1", pb[1])
        return D_, RN, bA, bB, bC, pbank

    def delta_pre(h, c):
        D_, RN, bA, bB, bC, pbank = delta_env(c)
        Ptp, PTtp, Ybfp = D_["Pt"], D_["PTt"], D_["Ybf"]
        u_p, wTp = D_["u_sb"], D_["wT_sb"]
        nbcol = nbeta_cols[:, (c * 4 + h):(c * 4 + h) + 1]
        ccols = slice(c * 128, (c + 1) * 128)
        for g in range(2):
            kb.mm(bA[1][:, 0:128], kf_v[:, g, ccols], kf_v[:, g, ccols],
                  start=(g == 0), stop=(g == 1),
                  reads=[("kf", None)], writes=[(bA[0], None)], tag=f"dM{h}_{c}")
        yield
        kb.stt(Ptp[:], bA[1][:, 0:128], nbcol, maskSL[:], ALU.mult, ALU.mult,
               reads=[(bA[0], None), ("nbeta_cols", None), ("maskSL", None)],
               writes=[(RN("Pt"), None)], tag=f"dP{h}_{c}")
        yield
        kb.tr(pbank[1][:, 0:128], Ptp[:], idb[:], reads=[(RN("Pt"), None), ("idb", None)],
              writes=[(pbank[0], None)], tag=f"dPT{h}_{c}")
        yield
        kb.cp(PTtp[:], pbank[1][:, 0:128], eng="dve", reads=[(pbank[0], None)],
              writes=[(RN("PTt"), None)], tag=f"dPTe{h}_{c}")
        yield
        kb.tt(Ybfp[:], idf[:], PTtp[:], ALU.add, reads=[("idf", None), (RN("PTt"), None)],
              writes=[(RN("Ybf"), None)], tag=f"dY0{h}_{c}")
        yield
        qcur, qtcur = (RN("Pt"), RN("PTt"))
        bufs = [(RN("Q0"), RN("QT0")), (RN("Q1"), RN("QT1"))]
        for i in range(1, 7):
            qn, qtn = bufs[i % 2]
            kb.mm(bA[1][:, 128:256], kb.sb[qtcur][:], kb.sb[qcur][:], True, True,
                  reads=[(qtcur, None), (qcur, None)], writes=[(bA[0], None)],
                  tag=f"dQ{h}_{c}_{i}")
            kb.mm(bB[1][:, 0:128], kb.sb[qcur][:], kb.sb[qtcur][:], True, True,
                  reads=[(qcur, None), (qtcur, None)], writes=[(bB[0], None)],
                  tag=f"dQT{h}_{c}_{i}")
            yield
            kb.cp(kb.sb[qn][:], bA[1][:, 128:256], eng="dve",
                  reads=[(bA[0], None)], writes=[(qn, None)], tag=f"dQe{h}_{c}_{i}")
            kb.cp(kb.sb[qtn][:], bB[1][:, 0:128], eng="act",
                  reads=[(bB[0], None)], writes=[(qtn, None)], tag=f"dQTe{h}_{c}_{i}")
            yield
            kb.mm(bC[1][:, 0:128], kb.sb[qn][:], Ybfp[:], True, True,
                  reads=[(qn, None), (RN("Ybf"), None)], writes=[(bC[0], None)],
                  tag=f"dYm{h}_{c}_{i}")
            yield
            kb.tt(Ybfp[:], Ybfp[:], bC[1][:, 0:128], ALU.add,
                  reads=[(RN("Ybf"), None), (bC[0], None)], writes=[(RN("Ybf"), None)],
                  tag=f"dYa{h}_{c}_{i}")
            yield
            qcur, qtcur = qn, qtn
        kb.mm(bA[1][:, 0:256], Ybfp[:], vb_tok[:, c * 256:(c + 1) * 256], True, True,
              reads=[(RN("Ybf"), None), ("v_tok_b", c)], writes=[(bA[0], None)],
              tag=f"du{h}_{c}")
        yield
        kb.act(u_p[:], bA[1][:, 0:256], AF.Copy, reads=[(bA[0], None)],
               writes=[(RN("u_sb"), None)], tag=f"due{h}_{c}")
        yield
        for g in range(2):
            kb.mm(bB[1][:, g * 128:(g + 1) * 128],
                  kbn_tok[:, c * 256 + g * 128: c * 256 + (g + 1) * 128], Ybfp[:],
                  True, True,
                  reads=[("kbn_tok", c), (RN("Ybf"), None)], writes=[(bB[0], None)],
                  tag=f"dwT{h}_{c}_{g}")
        yield
        kb.cp(wTp[:], bB[1][:, 0:256], eng="dve", reads=[(bB[0], None)],
              writes=[(RN("wT_sb"), None)], tag=f"dwTe{h}_{c}")
        yield

    def delta_tail(h, c):
        D_, RN, bA, bB, bC, pbank = delta_env(c)
        wTp, u_p, uadjp, attnTp = D_["wT_sb"], D_["u_sb"], D_["uadj"], D_["attnT"]
        ccols = slice(c * 128, (c + 1) * 128)
        for g in range(2):
            kb.mm(bC[1][:, 0:256], wTp[:, g * 128:(g + 1) * 128],
                  Sbf[:, g * 256:(g + 1) * 256], start=(g == 0), stop=(g == 1),
                  reads=[(RN("wT_sb"), None), ("Sbf", None)], writes=[(bC[0], None)],
                  tag=f"dwS{h}_{c}_{g}")
        yield
        kb.stt(uadjp[:], bC[1][:, 0:256], -1.0, u_p[:], ALU.mult, ALU.add,
               reads=[(bC[0], None), (RN("u_sb"), None)], writes=[(RN("uadj"), None)],
               tag=f"duadj{h}_{c}")
        yield
        for g in range(2):
            kb.mm(bA[1][:, 0:128], kf_v[:, g, ccols], qf_v[:, g, ccols],
                  start=(g == 0), stop=(g == 1),
                  reads=[("kf", None), ("qf", None)], writes=[(bA[0], None)],
                  tag=f"dat{h}_{c}_{g}")
        yield
        kb.tt(attnTp[:], bA[1][:, 0:128], maskIU[:], ALU.mult,
              reads=[(bA[0], None), ("maskIU", None)], writes=[(RN("attnT"), None)],
              tag=f"datm{h}_{c}")
        yield
        for g in range(2):
            kb.mm(bB[1][:, 0:256], qf_v[:, g, ccols], Sbf[:, g * 256:(g + 1) * 256],
                  start=(g == 0), stop=False,
                  reads=[("qf", None), ("Sbf", None)], writes=[(bB[0], None)],
                  tag=f"dqS{h}_{c}_{g}")
        yield
        kb.mm(bB[1][:, 0:256], attnTp[:], uadjp[:], start=False, stop=True,
              reads=[(RN("attnT"), None), (RN("uadj"), None)], writes=[(bB[0], None)],
              tag=f"dou{h}_{c}")
        yield
        kb.act(dout[:, c * 256:(c + 1) * 256], bB[1][:, 0:256], AF.Copy,
               reads=[(bB[0], None)], writes=[("dout", c)], tag=f"doute{h}_{c}")
        yield
        for g in range(2):
            kb.mm(bC[1][:, 0:256] if g == 0 else bC[1][:, 256:512],
                  kn_tok[:, c * 256 + g * 128: c * 256 + (g + 1) * 128], uadjp[:],
                  True, True,
                  reads=[("kn_tok", c), (RN("uadj"), None)], writes=[(bC[0], None)],
                  tag=f"dSu{h}_{c}_{g}")
        yield
        kb.tt(S32[:], S32[:], bC[1][:], ALU.add,
              reads=[("S32", None), (bC[0], None)], writes=[("S32", None)],
              tag=f"dSa{h}_{c}")
        yield
        kb.act(Sbf[:], S32[:], AF.Copy, reads=[("S32", None)], writes=[("Sbf", None)],
               tag=f"dSb{h}_{c}")
        yield

    def toep_ld_quarter(h, toep_win, q, tag):
        toep_v = toep_win.rearrange("s (c j) -> s c j", j=64)
        buf = q % 2
        tsl = m1main[0:127, buf * 4096:(buf + 1) * 4096]
        srcq = toep_v[:, h * 256 + q * 64: h * 256 + (q + 1) * 64, :]
        kb.dma("pool", tsl.rearrange("s (c j) -> s c j", j=64), srcq,
               reads=[("m1main", buf)], writes=[("m1main", buf)],
               tag=f"ldtoep{tag}{h}_{q}")

    def c1_stats(h):
        for blk in range(NBLK):
            for ti, (tn, tb) in ((0, ("dout", dout)), (1, ("v_tok", v_tok))):
                xsl = tb[:, blk * 256:(blk + 1) * 256]
                base = blk * 8 + ti * 3
                kb.red(xin[:, base:base + 1], xsl, ALU.add, reads=[(tn, None)],
                       writes=[("xin", ("c1", blk, ti, 0))], tag=f"c1s{h}_{blk}_{ti}")
                kb.red(xin[:, base + 1:base + 2], xsl, ALU.add, absval=True,
                       reads=[(tn, None)], writes=[("xin", ("c1", blk, ti, 1))],
                       tag=f"c1a{h}_{blk}_{ti}")
                kb.act(trash[:, 0:256], xsl, AF.Square, accum_out=xin[:, base + 2:base + 3],
                       reads=[(tn, None)], writes=[("trash", None), ("xin", ("c1", blk, ti, 2))],
                       tag=f"c1q{h}_{blk}_{ti}")
                yield

    def fir_gen(h, toep_win, dst, dst_name, tag, prefetched):
        def ld_quarter(q):
            toep_ld_quarter(h, toep_win, q, tag)
        if not prefetched:
            ld_quarter(0)
            ld_quarter(1)
        for q in range(4):
            buf = q % 2
            if q + 2 < 4:
                pass
            for ci in range(64):
                cih = q * 64 + ci
                bank = pf[(cih // 32) % 2]
                bank_res = f"pf{(cih // 32) % 2}"
                cslot = cih % 32
                lhsT = m1main[0:127, buf * 4096 + ci * 64: buf * 4096 + (ci + 1) * 64]
                rA = copyA[0:127, :].rearrange("s (m c) -> s m c", c=256)[:, :, cih:cih + 1]
                rB = copyB[0:127, :].rearrange("s (m c) -> s m c", c=256)[:, 1:17, cih:cih + 1]
                kb.mm(bank[0:64, cslot * 16:(cslot + 1) * 16], lhsT, rA, True, True,
                      reads=[("m1main", buf), ("copyA", None)], writes=[(bank_res, cslot // 8)],
                      tag=f"fA{tag}{h}_{cih}")
                kb.mm(bank[64:128, cslot * 16:(cslot + 1) * 16], lhsT, rB, True, True,
                      reads=[("m1main", buf), ("copyB", None)], writes=[(bank_res, cslot // 8)],
                      tile_position=(0, 64), tag=f"fB{tag}{h}_{cih}")
                if cslot == 31:
                    c0 = cih - 31
                    dstap = dst[:, :].rearrange("p (m c) -> p m c", c=256)[:, :, c0:c0 + 32]
                    srcap = bank[:, 0:512].rearrange("p (c m) -> p c m", m=16)
                    kb.cp(dstap, srcap.rearrange("p c m -> p m c"), eng="dve",
                          reads=[(bank_res, 0), (bank_res, 1), (bank_res, 2), (bank_res, 3)],
                          writes=[(dst_name, cih // 32)], tag=f"fE{tag}{h}_{cih}")
                if cih % 8 == 7:
                    yield
            if q + 2 < 4:
                ld_quarter(q + 2)

    for h in range(H):
        for rn in ("qf", "kf", "vfp", "kn_tok", "kbn_tok", "v_tok", "v_tok_b",
                   "copyA", "copyB", "dout"):
            kb.s.retag(rn)
        load_head_weights(h)
        proj_conv(h, "q", "wq", "qcw", qf_v, "qf", 0)
        proj_conv(h, "k", "wk", "kcw", kf_v, "kf", 0)
        proj_conv(h, "v", "wv", "vcw", vfp_v, "vfp", 64)
        for g in range(2):
            kb.memset(vfp_v[:, g, 0:64], 0.0, reads=[], writes=[("vfp", (g, "pad"))],
                      tag=f"vpad{h}{g}")
        l2norm(h, qf_v, "qf", 0)
        l2norm(h, kf_v, "kf", 0)
        transpose_blocks(h, kf_v, "kf", 0,
                         [(kn_tok, "kn_tok", "copy", None),
                          (kbn_tok, "kbn_tok", "ts", lambda blk: beta_cols[:, blk * 4 + h: blk * 4 + h + 1])],
                         tag="kn")
        transpose_blocks(h, vfp_v, "vfp", 64,
                         [(v_tok, "v_tok", "copy", None),
                          (vb_tok, "v_tok_b", "ts", lambda blk: beta_cols[:, blk * 4 + h: blk * 4 + h + 1])],
                         col_of_blk=lambda blk: 64 + blk * 128, tag="vt")
        transpose_blocks(h, vfp_v, "vfp", 0, [(copyA, "copyA", "copy", None)],
                         col_of_blk=lambda blk: blk * 128 + 1, tag="ca")
        transpose_blocks(h, vfp_v, "vfp", 0, [(copyB, "copyB", "copy", None)],
                         blk_range=range(1, 17),
                         col_of_blk=lambda blk: blk * 128 - 63, tag="cb")
        kb.memset(S32[:], 0.0, reads=[], writes=[("S32", None)], tag=f"S0{h}")
        kb.memset(Sbf[:], 0.0, reads=[], writes=[("Sbf", None)], tag=f"S0b{h}")
        toep_ld_quarter(h, win["toep_l"], 0, "l")
        toep_ld_quarter(h, win["toep_l"], 1, "l")
        weave(delta_pre(h, 0))
        for c in range(NBLK):
            weave(delta_tail(h, c),
                  delta_pre(h, c + 1) if c + 1 < NBLK else None)
        kb.s.retag("qf"); kb.s.retag("kf")
        kb.s.retag("xin")
        weave(fir_gen(h, win["toep_l"], ll_tok, "kf", "l", True),
              c1_stats(h))
        weave(fir_gen(h, win["toep_s"], ls_tok, "qf", "s", False))
        if dbg and stop_after == "head0" and h == 0:
            kb.dump("qf", qf[:], [("qf", None)], cast=True)
            kb.dump("kf", kf[:], [("kf", None)], cast=True)
            kb.dump("vfp", vfp[:], [("vfp", None)], cast=True)
            kb.dump("v_tok", v_tok[:], [("v_tok", None)], cast=True)
            kb.dump("kn_tok", kn_tok[:], [("kn_tok", None)], cast=True)
            kb.dump("dout", dout[:], [("dout", None)], cast=True)
            kb.dump("ll_tok", ll_tok[:], [("kf", None)], cast=True)
            kb.dump("ls_tok", ls_tok[:], [("qf", None)], cast=True)
        attn_c_phase(kb, h, locals())
        if dbg and stop_after == "head0" and h == 0:
            return nc, kb

    kb.s.retag("lnxf")
    kb.s.retag("xin")
    owT_sb = lnxf[:, 0:8192]
    ohnin = lnxf[:, 8192:12288]
    attn_tail(kb, locals())
    if stop_after == "attna":
        return nc, kb

    kb.free_from("lnxf")

    def alloc_ffn(sfx):
        R = lambda b: b + sfx
        Gd = {"R": R}
        Gd["outf"] = kb.sbuf(R("outf"), [128, 8 * T], F32)
        Gd["cn1"] = kb.sbuf(R("cn1"), [128, 8 * 1024], BF16)
        Gd["cn2"] = kb.sbuf(R("cn2"), [128, 8 * 512], BF16)
        Gd["cn3"] = kb.sbuf(R("cn3"), [128, 4 * 256], BF16)
        Gd["cn4"] = kb.sbuf(R("cn4"), [128, 2], BF16)
        Gd["cb1f"] = kb.sbuf(R("cb1f"), [128, 8], F32)
        Gd["cb2f"] = kb.sbuf(R("cb2f"), [128, 4], F32)
        Gd["cb3f"] = kb.sbuf(R("cb3f"), [128, 2], F32)
        Gd["cb4"] = kb.sbuf(R("cb4"), [1, 1], F32)
        Gd["gw1"] = kb.sbuf(R("gw1"), [128, 8 * 512], BF16)
        Gd["gw2"] = kb.sbuf(R("gw2"), [128, 4 * 256], BF16)
        Gd["gw3"] = kb.sbuf(R("gw3"), [128, 2], BF16)
        Gd["gb1"] = kb.sbuf(R("gb1"), [1, 512], F32)
        Gd["gb2"] = kb.sbuf(R("gb2"), [1, 256], F32)
        Gd["gb3"] = kb.sbuf(R("gb3"), [1, 1], F32)
        Gd["arr"] = kb.sbuf(R("arr"), [1, 1], F32)
        Gd["xt"] = kb.sbuf(R("xt"), [128, 4 * 1024], F32)
        Gd["lnt"] = kb.sbuf(R("lnt"), [128, 2 * 1024], BF16)
        Gd["trash"] = kb.sbuf(R("trash"), [128, 1024], BF16)
        Gd["statf"] = kb.sbuf(R("statf"), [128, 32], F32)
        Gd["ob"] = [kb.sbuf(R(f"ob{p}"), [128, 8 * 512], BF16) for p in range(2)]
        Gd["h1t"] = [kb.sbuf(R(f"h1t{p}"), [128, 8 * 512], BF16) for p in range(2)]
        Gd["h2t"] = [kb.sbuf(R(f"h2t{p}"), [128, 4 * 512], BF16) for p in range(2)]
        Gd["h3t"] = [kb.sbuf(R(f"h3t{p}"), [128, 2 * 512], BF16) for p in range(2)]
        Gd["gsc"] = kb.sbuf(R("gsc"), [1, 3640], F32)
        Gd["gcf"] = kb.sbuf(R("gcf"), [128, 8], BF16)
        Gd["grow"] = kb.sbuf(R("grow"), [1, 512], F32)
        Gd["tw"] = [kb.sbuf(R(f"tw{p}"), [128, 512], BF16) for p in range(2)]
        kb.dma("pool", Gd["cn1"][:].rearrange("p (kt o) -> p kt o", kt=8),
               win["cn_w1"].rearrange("(kt p) o -> p kt o", p=128), writes=[(R("cn1"), None)])
        kb.dma("pool", Gd["cn2"][:].rearrange("p (kt o) -> p kt o", kt=8),
               win["cn_w2"].rearrange("(kt p) o -> p kt o", p=128), writes=[(R("cn2"), None)])
        kb.dma("pool", Gd["cn3"][:].rearrange("p (kt o) -> p kt o", kt=4),
               win["cn_w3"].rearrange("(kt p) o -> p kt o", p=128), writes=[(R("cn3"), None)])
        kb.dma("pool", Gd["cn4"][:], win["cn_w4"], writes=[(R("cn4"), None)])
        kb.dma("pool", Gd["gw1"][:].rearrange("p (kt o) -> p kt o", kt=8),
               win["gc_w1"].rearrange("(kt p) o -> p kt o", p=128), writes=[(R("gw1"), None)])
        kb.dma("pool", Gd["gw2"][:].rearrange("p (kt o) -> p kt o", kt=4),
               win["gc_w2"].rearrange("(kt p) o -> p kt o", p=128), writes=[(R("gw2"), None)])
        kb.dma("pool", Gd["gw3"][:], win["gc_w3"], writes=[(R("gw3"), None)])
        kb.dma("sp", Gd["cb1f"][:], win["cb1f"], writes=[(R("cb1f"), None)])
        kb.dma("sp", Gd["cb2f"][:], win["cb2f"], writes=[(R("cb2f"), None)])
        kb.dma("sp", Gd["cb3f"][:], win["cb3f"], writes=[(R("cb3f"), None)])
        kb.dma("sp", Gd["cb4"][:], win["cb4"], writes=[(R("cb4"), None)])
        kb.dma("sp", Gd["gb1"][:], win["gb1_row"], writes=[(R("gb1"), None)])
        kb.dma("sp", Gd["gb2"][:], win["gb2_row"], writes=[(R("gb2"), None)])
        kb.dma("sp", Gd["gb3"][:], win["gb3"], writes=[(R("gb3"), None)])
        kb.dma("sp", Gd["arr"][:], win["ar"], writes=[(R("arr"), None)])
        Gd.update(pf=pf, pb=pb, idf=idf, idb=idb, konst=konst, ones_row=ones_row,
                  ones_col_b=ones_col_b)
        return Gd

    G = alloc_ffn("_f1")
    G["x_src"] = xa_scr
    G["x_src_name"] = "xa_scr"
    G["x_dst"] = xb_scr
    G["x_dst_name"] = "xb_scr"
    ffn_phase(kb, G, "1")
    if stop_after == "ffn1":
        return nc, kb

    kb.free_from("outf_f1")

    Bd = {}
    Bd["rkT"] = kb.sbuf("rkT", [128, 8 * 2049], BF16)
    Bd["Wst"] = kb.sbuf("Wst", [128, 8 * 1024], F32)
    Bd["Wbf"] = kb.sbuf("Wbf", [128, 8 * 1024], BF16)
    Bd["WwT"] = kb.sbuf("WwT", [128, 8 * 1024], BF16)
    Bd["WrT"] = kb.sbuf("WrT", [128, 8 * 1024], BF16)
    Bd["MT"] = kb.sbuf("MT", [128, 128], BF16)
    Bd["DTs"] = kb.sbuf("DTs", [128, 128], F32)
    Bd["CZs"] = kb.sbuf("CZs", [128, 128], F32)
    Bd["iop"] = kb.sbuf("iop", [128, 1], F32)
    Bd["ior"] = kb.sbuf("ior", [128, 1], F32)
    Bd["lgcol"] = kb.sbuf("lgcol", [128, 1], F32)
    Bd["gpcol"] = kb.sbuf("gpcol", [128, 1], F32)
    Bd["gwcol"] = kb.sbuf("gwcol", [128, 1], F32)
    Bd["gccol"] = kb.sbuf("gccol", [128, 1], F32)
    Bd["acol"] = kb.sbuf("acol", [128, 1], F32)
    Bd["brow"] = kb.sbuf("brow", [1, 8], F32)
    Bd["xt"] = [kb.sbuf(f"xtb{p}", [128, 1024], F32) for p in range(4)]
    Bd["lnt32"] = [kb.sbuf(f"lnt32{p}", [128, 1024], F32) for p in range(2)]
    Bd["lnbf"] = [kb.sbuf(f"lnbf{p}", [128, 1024], BF16) for p in range(2)]
    Bd["vt"] = [kb.sbuf(f"vt{p}", [128, 1024], BF16) for p in range(2)]
    Bd["vgw"] = [kb.sbuf(f"vgw{p}", [128, 1024], BF16) for p in range(2)]
    Bd["wkt"] = [kb.sbuf(f"wkt{p}", [128, 1024], BF16) for p in range(2)]
    Bd["statb2"] = kb.sbuf("statb2", [128, 8], F32)
    Bd["rdbf"] = kb.sbuf("rdbf", [128, 1024], BF16)
    Bd["rdT"] = kb.sbuf("rdT", [128, 1024], BF16)
    Bd["SMT"] = kb.sbuf("SMT", [128, 128], BF16)
    Bd["scrf"] = kb.sbuf("scrf", [128, 1024], F32)
    Bd["trash"] = kb.sbuf("trashb", [128, 1024], BF16)
    Bd["statb"] = kb.sbuf("statb", [128, 8], F32)
    Bd.update(pf=pf, pb=pb, idf=idf, idb=idb, konst=konst, ones_row=ones_row)
    Bd["x_src"] = xb_scr
    Bd["x_src_name"] = "xb_scr"
    Bd["x_dst"] = xa_scr
    Bd["x_dst_name"] = "xa2"
    kb.dma("pool", Bd["WwT"][:].rearrange("p (kt o) -> p kt o", kt=8),
           win["WwT"].rearrange("(kt p) o -> p kt o", p=128), writes=[("WwT", None)])
    kb.dma("pool", Bd["WrT"][:].rearrange("p (kt o) -> p kt o", kt=8),
           win["WrT"].rearrange("(kt p) o -> p kt o", p=128), writes=[("WrT", None)])
    kb.dma("sp", Bd["DTs"][:], cin["DT"], writes=[("DTs", None)])
    kb.dma("sp", Bd["CZs"][:], cin["CZ"], writes=[("CZs", None)])
    kb.dma("sp", Bd["iop"][:], cin["iota_p"], writes=[("iop", None)])
    kb.dma("sp", Bd["ior"][:], cin["iota_rev"], writes=[("ior", None)])
    kb.dma("sp", Bd["brow"][0:1, 0:1], win["decay"], writes=[("brow", 0)])
    kb.dma("sp", Bd["brow"][0:1, 1:2], win["log_alpha"], writes=[("brow", 1)])
    kb.memset(Bd["Wst"][:], 0.0, writes=[("Wst", (jt, half)) for jt in range(8) for half in range(2)], tag="bW0")
    kb.memset(Bd["Wbf"][:], 0.0, writes=[("Wbf", None)], tag="bWb0")
    for g in range(8):
        kb.memset(Bd["rkT"][:, g * 2049: g * 2049 + 1], 0.0,
                  writes=[("rkT", (-1, g // 4))], tag=f"brk0_{g}")
    kb.act(Bd["brow"][0:1, 2:3], Bd["brow"][0:1, 0:1], AF.Sigmoid,
           reads=[("brow", 0)], writes=[("brow", 2)], tag="bsig")
    kb.act(Bd["brow"][0:1, 3:4], Bd["brow"][0:1, 2:3], AF.Ln,
           reads=[("brow", 2)], writes=[("brow", 3)], tag="bln0")
    kb.act(Bd["brow"][0:1, 4:5], Bd["brow"][0:1, 1:2], AF.Exp,
           reads=[("brow", 1)], writes=[("brow", 4)], tag="bexp")
    kb.mm(pf[0][:, 0:1], ones_row[:], Bd["brow"][0:1, 3:4], True, True,
          reads=[("ones_row", None), ("brow", 3)], writes=[("pf0", None)], tag="blgb")
    kb.cp(Bd["lgcol"][:], pf[0][:, 0:1], eng="dve", reads=[("pf0", None)],
          writes=[("lgcol", None)])
    kb.mm(pf[0][:, 1:2], ones_row[:], Bd["brow"][0:1, 4:5], True, True,
          reads=[("ones_row", None), ("brow", 4)], writes=[("pf0", None)], tag="bacb")
    kb.cp(Bd["acol"][:], pf[0][:, 1:2], eng="dve", reads=[("pf0", None)],
          writes=[("acol", None)])
    kb.act(Bd["gpcol"][:], Bd["iop"][:], AF.Exp, scale=Bd["lgcol"][:],
           reads=[("iop", None), ("lgcol", None)], writes=[("gpcol", None)], tag="bgp")
    kb.act(Bd["gwcol"][:], Bd["ior"][:], AF.Exp, scale=Bd["lgcol"][:],
           reads=[("ior", None), ("lgcol", None)], writes=[("gwcol", None)], tag="bgw")
    kb.ts(Bd["scrf"][:, 0:1], Bd["lgcol"][:], 128.0, reads=[("lgcol", None)],
          writes=[("scrf", None)], tag="bgc1")
    kb.act(Bd["gccol"][:], Bd["scrf"][:, 0:1], AF.Exp,
           reads=[("scrf", None)], writes=[("gccol", None)], tag="bgc2")
    kb.act(Bd["scrf"][:, 0:128], Bd["DTs"][:], AF.Exp, scale=Bd["lgcol"][:],
           reads=[("DTs", None), ("lgcol", None), ("scrf", None)], writes=[("scrf", None)], tag="bmt1")
    kb.tt(Bd["MT"][:], Bd["scrf"][:, 0:128], Bd["CZs"][:], ALU.mult,
          reads=[("scrf", None), ("CZs", None)], writes=[("MT", None)], tag="bmt2")
    nck = NBLK
    if stop_after and stop_after.startswith("attnb") and len(stop_after) > 5:
        nck = int(stop_after[5:])
    attnb_phase(kb, Bd, nck)
    if stop_after and stop_after.startswith("attnb"):
        if kb.dbg:
            kb.dump("Wst", Bd["Wst"][:], [("Wst", (jt, hh)) for jt in range(8) for hh in range(2)])
            kb.dump("rdbf", Bd["rdbf"][:], [("rdbf", 0), ("rdbf", 1)], cast=True)
        return nc, kb

    kb.free_from("rkT")
    G2 = alloc_ffn("_f2")
    G2["x_src"] = xa_scr
    G2["x_src_name"] = "xa2"
    G2["x_dst"] = out_hbm
    G2["x_dst_name"] = "out"
    ffn_phase(kb, G2, "2")
    return nc, kb


def _dummy_ins():
    d = {}
    d["L0_attn_a_q_w"] = np.zeros((D, D), np.float32)
    d["L0_attn_a_k_w"] = np.zeros((D, D), np.float32)
    d["L0_attn_a_v_w"] = np.zeros((D, D), np.float32)
    d["L0_attn_a_b_w"] = np.zeros((H, D), np.float32)
    d["L0_attn_a_o_w"] = np.zeros((D, D), np.float32)
    d["L0_attn_a_mlp1_w"] = np.zeros((1024, D + 16), np.float32)
    d["L0_attn_a_mlp1_b"] = np.zeros((1024,), np.float32)
    d["L0_attn_a_mlp2_w"] = np.zeros((4, 1024), np.float32)
    d["L0_attn_a_mlp2_b"] = np.zeros((4,), np.float32)
    d["L0_attn_a_q_cw"] = np.zeros((D, 4), np.float32)
    d["L0_attn_a_k_cw"] = np.zeros((D, 4), np.float32)
    d["L0_attn_a_v_cw"] = np.zeros((D, 4), np.float32)
    d["L0_attn_a_fir_short_w"] = np.zeros((H, DV, 5), np.float32)
    d["L0_attn_a_fir_long_w"] = np.zeros((H, DV, 64), np.float32)
    d["L0_attn_a_logit_temp"] = np.zeros((1,), np.float32)
    d["L0_attn_a_conv_res_logit"] = np.zeros((H,), np.float32)
    d["L0_attn_a_norm_w"] = np.zeros((DV,), np.float32)
    d["L0_ffn_cn_w1"] = np.zeros((D, D), np.float32)
    d["L0_ffn_cn_w2"] = np.zeros((D, 512), np.float32)
    d["L0_ffn_cn_w3"] = np.zeros((512, 256), np.float32)
    d["L0_ffn_cn_w4"] = np.zeros((256, 1), np.float32)
    d["L0_ffn_cn_b1"] = np.zeros((D,), np.float32)
    d["L0_ffn_cn_b2"] = np.zeros((512,), np.float32)
    d["L0_ffn_cn_b3"] = np.zeros((256,), np.float32)
    d["L0_ffn_cn_b4"] = np.zeros((1,), np.float32)
    d["L0_ffn_gc_w1"] = np.zeros((D, 512), np.float32)
    d["L0_ffn_gc_w2"] = np.zeros((512, 256), np.float32)
    d["L0_ffn_gc_w3"] = np.zeros((256, 1), np.float32)
    d["L0_ffn_gc_b1"] = np.zeros((512,), np.float32)
    d["L0_ffn_gc_b2"] = np.zeros((256,), np.float32)
    d["L0_ffn_gc_b3"] = np.zeros((1,), np.float32)
    d["L0_ffn_adaptive_range"] = np.zeros((1,), np.float32)
    d["L0_attn_b_W_write"] = np.zeros((D, D), np.float32)
    d["L0_attn_b_W_read"] = np.zeros((D, D), np.float32)
    d["L0_attn_b_decay"] = np.zeros((), np.float32)
    d["L0_attn_b_log_alpha"] = np.zeros((), np.float32)
    return d


def finish(nc, kb):
    sems = {e: nc.semaphore("sem_" + e).__enter__() for e in ENGINES}
    dma_sems = {q: [nc.semaphore(f"dma_{q}{i}").__enter__() for i in range(Sched.NDMA)]
                for q in ("sp", "pool")}
    run = kb.s.emit_all(sems, dma_sems)
    with nc.Block() as block:
        @block.tensor
        def _(t):
            run("pe", t)

        @block.scalar
        def _(sc):
            run("act", sc)

        @block.vector
        def _(v):
            run("dve", v)

        @block.gpsimd
        def _(g):
            run("pool", g)

        @block.sync
        def _(sy):
            run("sp", sy)
            for q in ("sp", "pool"):
                for slot in range(Sched.NDMA):
                    fin = kb.s.dma_final.get((q, slot))
                    if fin:
                        sy.wait_ge(dma_sems[q][slot], 16 * fin)
    kb.es.close()
    return nc


def make_in_map(ins, b, consts, weights):
    m = {"x": np.ascontiguousarray(ins["L0_attn_a_hidden_states"][b]).astype(np.float32)}
    for k, v in consts.items():
        m["c_" + k] = v
    for k, v in weights.items():
        m["w_" + k] = np.ascontiguousarray(v).astype(np.float32)
    return m



def kernel(**inputs):
    ins = {k: np.asarray(v) for k, v in inputs.items()}
    consts = prep_consts()
    weights = prep_weights(ins)
    nc, kb = build_program(dbg=False, stop_after=None)
    finish(nc, kb)
    in_maps = [make_in_map(ins, b, consts, weights) for b in range(8)]
    res = run_bass_kernel_spmd(nc, in_maps, core_ids=list(range(8)))
    out = np.stack([np.asarray(res.results[b]["out"], dtype=np.float32)
                    for b in range(8)], axis=0)
    return out


# version 20
# speedup vs baseline: 1.2349x; 1.0098x over previous
import math
import numpy as np
import ml_dtypes
import concourse.bass as bass
import concourse.mybir as mybir
from concourse.bass_utils import run_bass_kernel_spmd
from contextlib import ExitStack

F32 = mybir.dt.float32
BF16 = mybir.dt.bfloat16
AF = mybir.ActivationFunctionType
ALU = mybir.AluOpType
AX = mybir.AxisListType

T = 2048
D = 1024
H, DK, DV = 4, 256, 256
NBLK = 16
CH = 128
NT = 4
TT = 512

ENGINES = ("pe", "act", "dve", "pool", "sp")
ATTNB_PARTS = set("ABCDEFG")
EMIT_DRAINS = True


def weave(*gens):
    gens = [g for g in gens if g is not None]
    while gens:
        for g in list(gens):
            try:
                next(g)
            except StopIteration:
                gens.remove(g)



class Op:
    __slots__ = ("eng", "emit", "order_i", "mark", "is_dma", "dma_idx", "waits",
                 "tag", "spos", "drain_pos")

    def __init__(self, eng, emit, is_dma=False, tag=""):
        self.eng = eng
        self.emit = emit
        self.mark = False
        self.is_dma = is_dma
        self.dma_idx = -1
        self.waits = []
        self.order_i = -1
        self.spos = -1
        self.drain_pos = -1
        self.tag = tag


class Sched:

    def __init__(self, nc):
        self.nc = nc
        self.ops = {e: [] for e in ENGINES}
        self.order = []
        self.res = {}
        self.carry = {}
        self._dma_counts = {}

    def _st(self, name, key):
        d = self.res.setdefault(name, {})
        return d.setdefault(key, {"w": None, "r": []})

    def _dep_ops(self, o, reads, writes):
        deps = []
        for (name, key) in reads:
            keys = list(self.res.get(name, {})) if key is None else [key, None]
            psum = name.startswith("pf") or name.startswith("pb")
            for k in keys:
                st = self.res.get(name, {}).get(k)
                if st and st["w"] is not None:
                    deps.append(st["w"])
                if psum and st:
                    deps.extend(r for r in st["r"] if r.eng != o.eng)
            deps.extend(self.carry.get(name, ()))
        for (name, key) in writes:
            keys = list(self.res.get(name, {})) if key is None else [key, None]
            for k in keys:
                st = self.res.get(name, {}).get(k)
                if st:
                    if st["w"] is not None:
                        deps.append(st["w"])
                    deps.extend(st["r"])
            deps.extend(self.carry.get(name, ()))
        return deps

    def _commit(self, o, reads, writes):
        for (name, key) in reads:
            self._st(name, key)["r"].append(o)
        for (name, key) in writes:
            if key is None:
                d = self.res.setdefault(name, {})
                d.clear()
                self.carry.pop(name, None)
            st = self._st(name, key)
            st["w"] = o
            st["r"] = []

    def op(self, eng, emit, reads=(), writes=(), is_dma=False, tag=""):
        o = Op(eng, emit, is_dma=is_dma, tag=tag)
        deps = self._dep_ops(o, reads, writes)
        best = {}
        for d in deps:
            if d is o:
                continue
            if not d.is_dma and d.eng == eng:
                if eng != "pe":
                    o.drain_pos = max(o.drain_pos, d.spos)
                continue
            stream = ("dma", d.eng, d.dma_idx % self.NDMA) if d.is_dma else ("eng", d.eng)
            cur = best.get(stream)
            if cur is None or d.order_i > cur.order_i:
                best[stream] = d
        o.waits = list(best.values())
        o.spos = len(self.ops[eng])
        if is_dma:
            o.dma_idx = self._dma_counts.get(eng, 0)
            self._dma_counts[eng] = o.dma_idx + 1
        o.order_i = len(self.order)
        self.order.append(o)
        self.ops[eng].append(o)
        self._commit(o, reads, writes)
        return o

    @staticmethod
    def _best_per_stream(ops):
        best = {}
        for d in ops:
            stream = ("dma", d.eng, d.dma_idx % Sched.NDMA) if d.is_dma else ("eng", d.eng)
            cur = best.get(stream)
            if cur is None or d.order_i > cur.order_i:
                best[stream] = d
        return list(best.values())

    def retag(self, name):
        ops = []
        for st in self.res.get(name, {}).values():
            if st["w"] is not None:
                ops.append(st["w"])
            ops.extend(st["r"])
        ops.extend(self.carry.get(name, ()))
        self.res[name] = {}
        self.carry[name] = self._best_per_stream(ops)

    NDMA = 16

    def finalize_marks(self):
        for o in self.order:
            for d in o.waits:
                if not d.is_dma:
                    d.mark = True
        self.mnum = {}
        self.dma_slot = {}
        self.dma_final = {}
        for e in ENGINES:
            n = 0
            for o in self.ops[e]:
                if o.is_dma:
                    slot = o.dma_idx % self.NDMA
                    use = o.dma_idx // self.NDMA + 1
                    self.dma_slot[id(o)] = (slot, use)
                    self.dma_final[(e, slot)] = use
                elif o.mark:
                    n += 1
                self.mnum[id(o)] = n if not o.is_dma else None

    def emit_all(self, sems, dma_sems):
        self.finalize_marks()
        seen = {e: {} for e in ENGINES}

        def run_stream(e, engine):
            sn = seen[e]
            drained_upto = -1
            for o in self.ops[e]:
                if EMIT_DRAINS and o.drain_pos >= 0 and o.drain_pos > drained_upto:
                    engine.drain()
                    drained_upto = o.spos - 1
                for d in o.waits:
                    if d.is_dma:
                        slot, use = self.dma_slot[id(d)]
                        key = ("dma", d.eng, slot)
                        val = 16 * use
                        sem = dma_sems[d.eng][slot]
                    else:
                        key = ("eng", d.eng)
                        val = self.mnum[id(d)]
                        sem = sems[d.eng]
                    if sn.get(key, 0) >= val:
                        continue
                    engine.wait_ge(sem, val)
                    sn[key] = val
                if o.is_dma:
                    slot, use = self.dma_slot[id(o)]
                    key = ("dma", o.eng, slot)
                    if use >= 2 and sn.get(key, 0) < 16 * (use - 1):
                        engine.wait_ge(dma_sems[o.eng][slot], 16 * (use - 1))
                        sn[key] = 16 * (use - 1)
                    inst = o.emit(engine)
                    inst.then_inc(dma_sems[o.eng][slot], 16)
                else:
                    inst = o.emit(engine)
                    if o.mark:
                        if inst is None:
                            raise RuntimeError(f"marked op has no instruction: {o.tag}")
                        inst.then_inc(sems[o.eng], 1)
        return run_stream



def build_toeplitz(filt, K):
    C = filt.shape[0]
    out = np.zeros((127, C, 64), np.float32)
    s = np.arange(127)[:, None]
    i = np.arange(64)[None, :]
    j = s - i + K - 64
    valid = (j >= 0) & (j < K)
    js = np.clip(j, 0, K - 1)
    for c in range(C):
        out[:, c, :] = np.where(valid, filt[c][js], 0.0)
    return out.reshape(127, C * 64)


def prep_consts():
    c = {}
    c["identity"] = np.eye(128, dtype=np.float32)
    c["ones_row"] = np.ones((1, 128), np.float32)
    c["ones_col"] = np.ones((128, 1), np.float32)
    c["maskSL"] = np.tril(np.ones((128, 128), np.float32), -1)
    c["maskIU"] = np.triu(np.ones((128, 128), np.float32), 0)
    p = np.arange(128, dtype=np.float32)
    c["iota_p"] = p[:, None].copy()
    c["iota_rev"] = (127.0 - p)[:, None].copy()
    DT = np.clip(p[None, :] - 1 - p[:, None], 0, None)
    c["DT"] = DT.astype(np.float32)
    c["CZ"] = (p[None, :] > p[:, None]).astype(np.float32)
    k = np.zeros((128, 8), np.float32)
    k[:, 0] = 1e-5; k[:, 1] = 1e-6; k[:, 2] = 1.0; k[:, 3] = 0.02
    c["konst"] = k
    return c


def prep_weights(ins):
    w = {}
    w["qwT"] = np.ascontiguousarray(ins["L0_attn_a_q_w"].T)
    w["kwT"] = np.ascontiguousarray(ins["L0_attn_a_k_w"].T)
    w["vwT"] = np.ascontiguousarray(ins["L0_attn_a_v_w"].T)
    w["bwT"] = np.ascontiguousarray(ins["L0_attn_a_b_w"].T)
    w["owT"] = np.ascontiguousarray(ins["L0_attn_a_o_w"].T)
    w["m1main"] = np.ascontiguousarray(ins["L0_attn_a_mlp1_w"][:, :D].T)
    w["m1tail"] = np.ascontiguousarray(ins["L0_attn_a_mlp1_w"][:, D:].T)
    w["m2T"] = np.ascontiguousarray(ins["L0_attn_a_mlp2_w"].T)
    for nm, key in (("qcw", "L0_attn_a_q_cw"), ("kcw", "L0_attn_a_k_cw"), ("vcw", "L0_attn_a_v_cw")):
        w[nm] = np.ascontiguousarray(ins[key].reshape(8, 128, 4).transpose(1, 0, 2))
    w["toep_s"] = build_toeplitz(ins["L0_attn_a_fir_short_w"].reshape(H * DV, 5), 5)
    w["toep_l"] = build_toeplitz(ins["L0_attn_a_fir_long_w"].reshape(H * DV, 64), 64)
    w["b1f"] = np.ascontiguousarray(ins["L0_attn_a_mlp1_b"].reshape(8, 128).T)
    w["normw_row"] = ins["L0_attn_a_norm_w"][None, :].copy()
    w["m2b_row"] = ins["L0_attn_a_mlp2_b"][None, :].copy()
    w["crl_row"] = ins["L0_attn_a_conv_res_logit"][None, :].copy()
    w["ltemp"] = ins["L0_attn_a_logit_temp"].reshape(1, 1).copy()
    w["cn_w1"] = ins["L0_ffn_cn_w1"]
    w["cn_w2"] = ins["L0_ffn_cn_w2"]
    w["cn_w3"] = ins["L0_ffn_cn_w3"]
    w["cn_w4"] = np.ascontiguousarray(ins["L0_ffn_cn_w4"].reshape(2, 128).T)
    w["cb1f"] = np.ascontiguousarray(ins["L0_ffn_cn_b1"].reshape(8, 128).T)
    w["cb2f"] = np.ascontiguousarray(ins["L0_ffn_cn_b2"].reshape(4, 128).T)
    w["cb3f"] = np.ascontiguousarray(ins["L0_ffn_cn_b3"].reshape(2, 128).T)
    w["cb4"] = ins["L0_ffn_cn_b4"].reshape(1, 1).copy()
    w["gc_w1"] = ins["L0_ffn_gc_w1"]
    w["gc_w2"] = ins["L0_ffn_gc_w2"]
    w["gc_w3"] = np.ascontiguousarray(ins["L0_ffn_gc_w3"].reshape(2, 128).T)
    w["gb1_row"] = ins["L0_ffn_gc_b1"][None, :].copy()
    w["gb2_row"] = ins["L0_ffn_gc_b2"][None, :].copy()
    w["gb3"] = ins["L0_ffn_gc_b3"].reshape(1, 1).copy()
    w["ar"] = ins["L0_ffn_adaptive_range"].reshape(1, 1).copy()
    w["WwT"] = np.ascontiguousarray(ins["L0_attn_b_W_write"].T)
    w["WrT"] = np.ascontiguousarray(ins["L0_attn_b_W_read"].T)
    w["decay"] = np.asarray(ins["L0_attn_b_decay"]).reshape(1, 1).copy()
    w["log_alpha"] = np.asarray(ins["L0_attn_b_log_alpha"]).reshape(1, 1).copy()
    return w



class KB:
    def __init__(self, nc, dbg=False):
        self.nc = nc
        self.s = Sched(nc)
        self.es = ExitStack()
        self.sb = {}
        self.dbg = dbg
        self.dbg_views = {}

    def sbuf(self, name, shape, dt):
        cm = self.nc.sbuf_tensor(name, shape, dt)
        t = cm.__enter__()
        self.sb[name] = t
        self._cms = getattr(self, "_cms", {})
        self._cms[name] = cm
        self._order = getattr(self, "_order", [])
        self._order.append(name)
        pend = getattr(self, "_pending_carry", None)
        if pend:
            self.s.carry.setdefault(name, []).extend(pend)
        return t

    def free(self, *names):
        pend = getattr(self, "_pending_carry", None)
        if pend is None:
            pend = self._pending_carry = []
        for name in names:
            for st in self.s.res.get(name, {}).values():
                if st["w"] is not None:
                    pend.append(st["w"])
                pend.extend(st["r"])
            pend.extend(self.s.carry.get(name, ()))
            self.s.res.pop(name, None)
            self.s.carry.pop(name, None)
            self._cms[name].__exit__(None, None, None)
            del self.sb[name]
            self._order.remove(name)
        self._pending_carry = pend = self.s._best_per_stream(pend)

    def free_from(self, marker):
        i = self._order.index(marker)
        for name in reversed(self._order[i:]):
            self.free(name)

    def psum(self, name, shape, dt):
        t = self.es.enter_context(self.nc.psum_tensor(name, shape, dt))
        self.sb[name] = t
        return t

    def din(self, name, shape, dt=F32):
        return self.nc.dram_tensor(name, shape, dt, kind="ExternalInput").ap()

    def dout(self, name, shape, dt=F32):
        return self.nc.dram_tensor(name, shape, dt, kind="ExternalOutput").ap()

    def dint(self, name, shape, dt=F32):
        kind = "ExternalOutput" if self.dbg else "Internal"
        return self.nc.dram_tensor(name, shape, dt, kind=kind).ap()

    def dma(self, q, out_ap, in_ap, reads=(), writes=(), tag=""):
        return self.s.op(q, lambda e, o=out_ap, i=in_ap: e.dma_start(out=o, in_=i),
                         reads=reads, writes=writes, is_dma=True, tag=tag)

    def mm(self, out_ap, lhsT, rhs, start, stop, reads=(), writes=(), tag="", tile_position=None):
        return self.s.op("pe",
                         lambda e, o=out_ap, l=lhsT, r=rhs, st=start, sp=stop, tp=tile_position:
                             e.matmul(o, l, r, start=st, stop=sp, tile_position=tp),
                         reads=reads, writes=writes, tag=tag)

    def tr(self, out_ap, in_ap, ident, reads=(), writes=(), tag=""):
        return self.s.op("pe",
                         lambda e, o=out_ap, i=in_ap, d=ident: e.transpose(o, i, d),
                         reads=reads, writes=writes, tag=tag)

    def act(self, out_ap, in_ap, func, bias=None, scale=None, accum_out=None,
            reads=(), writes=(), tag=""):
        kw = {}
        if bias is not None:
            kw["bias"] = bias
        if scale is not None:
            kw["scale"] = scale
        if accum_out is not None:
            kw["accum_out"] = accum_out
        return self.s.op("act",
                         lambda e, o=out_ap, i=in_ap, f=func, kw=kw:
                             e.activation(out=o, in_=i, func=f, **kw),
                         reads=reads, writes=writes, tag=tag)

    def tt(self, out, in0, in1, op, eng="dve", reads=(), writes=(), tag=""):
        return self.s.op(eng,
                         lambda e, o=out, a=in0, b=in1, p=op: e.tensor_tensor(out=o, in0=a, in1=b, op=p),
                         reads=reads, writes=writes, tag=tag)

    def ts(self, out, in0, s1, s2=None, op0=ALU.mult, op1=None, eng="dve",
           reads=(), writes=(), tag=""):
        def f(e, o=out, a=in0, s1=s1, s2=s2, p0=op0, p1=op1):
            kw = dict(out=o, in0=a, scalar1=s1, scalar2=s2, op0=p0)
            if p1 is not None:
                kw["op1"] = p1
            return e.tensor_scalar(**kw)
        return self.s.op(eng, f, reads=reads, writes=writes, tag=tag)

    def stt(self, out, in0, scalar, in1, op0, op1, eng="dve", reads=(), writes=(), tag=""):
        return self.s.op(eng,
                         lambda e, o=out, a=in0, s=scalar, b=in1, p0=op0, p1=op1:
                             e.scalar_tensor_tensor(out=o, in0=a, scalar=s, in1=b, op0=p0, op1=p1),
                         reads=reads, writes=writes, tag=tag)

    def red(self, out, in_, op, absval=False, eng="dve", reads=(), writes=(), tag=""):
        return self.s.op(eng,
                         lambda e, o=out, i=in_, p=op, av=absval:
                             e.tensor_reduce(out=o, in_=i, axis=AX.X, op=p, apply_absolute_value=av),
                         reads=reads, writes=writes, tag=tag)

    def cp(self, out, in_, eng="dve", reads=(), writes=(), tag=""):
        if eng == "act":
            return self.s.op(eng, lambda e, o=out, i=in_: e.copy(out=o, in_=i),
                             reads=reads, writes=writes, tag=tag)
        return self.s.op(eng, lambda e, o=out, i=in_: e.tensor_copy(out=o, in_=i),
                         reads=reads, writes=writes, tag=tag)

    def memset(self, ap, val, eng="dve", reads=(), writes=(), tag=""):
        return self.s.op(eng, lambda e, a=ap, v=val: e.memset(a, v),
                         reads=reads, writes=writes, tag=tag)

    def recip(self, out, in_, reads=(), writes=(), tag=""):
        return self.s.op("dve", lambda e, o=out, i=in_: e.reciprocal(out=o, in_=i),
                         reads=reads, writes=writes, tag=tag)

    def dump(self, name, src_ap, reads, cast=False):
        if not self.dbg:
            return
        shape = list(src_ap.shape)
        d = self.dout("dbg_" + name, shape, F32)
        q = "pool" if cast else "sp"
        self.dma(q, d, src_ap, reads=reads, tag="dump_" + name)
        self.dbg_views[name] = shape


def attn_c_phase(kb, h, L):
    pf, pb = L["pf"], L["pb"]
    konst = L["konst"]; idf = L["idf"]; idb = L["idb"]
    stcrP, stcP, statsT = L["stcr"], L["stc"], L["statsT"]
    trash2 = L["trash2"]; sshin = L["sshin"]; h1s = L["h1s"]; h1f = L["h1f"]
    m1tail, m2T, b1f = L["m1tail"], L["m2T"], L["b1f"]
    lgr, lgc, pc, pcolP = L["lgr"], L["lgc"], L["pc"], L["pcol"]
    ls_tok, ll_tok, dout, v_tok = L["ls_tok"], L["ll_tok"], L["dout"], L["v_tok"]
    o_tP, o_t2P, ohn_t = L["o_t"], L["o_t2"], L["ohn_t"]
    crl_sig, trec, normw_bc = L["crl_sig"], L["trec"], L["normw_bc"]
    ssh_scr, ohn_scr = L["ssh_scr"], L["ohn_scr"]
    kb.s.retag("copyA"); kb.s.retag("copyB")
    sshd = L["sshd"]

    def ssh_load(t):
        kb.dma("sp", sshd[:].rearrange("p (g t) -> p g t", g=8),
               ssh_scr[:, :, t * TT:(t + 1) * TT].rearrange("g p t -> p g t"),
               reads=[("sshd", None)] + [("ssh_scr", (t, go)) for go in range(8)],
               writes=[("sshd", None)], tag=f"sshld{h}_{t}")
    tensors = [("qf", ls_tok), ("kf", ll_tok), ("dout", dout), ("v_tok", v_tok)]
    for t in range(NT):
        sshin = sshd[:]
        ssh_n = ("sshd", None)
        ssh_load(t)
        for i in range(4):
            blk = t * 4 + i
            pp2 = i % 2
            stcr = stcrP[pp2]; stc = stcP[pp2]
            scr_n = f"stcr{pp2}"; stc_n = f"stc{pp2}"
            for ti, (tn, tb) in enumerate(tensors[:2]):
                xsl = tb[:, blk * 256:(blk + 1) * 256]
                kb.red(stcr[:, ti:ti + 1], xsl, ALU.add, reads=[(tn, None)],
                       writes=[(scr_n, ti)], tag=f"csum{h}_{blk}_{ti}")
                kb.red(stcr[:, 4 + ti:5 + ti], xsl, ALU.add, absval=True, reads=[(tn, None)],
                       writes=[(scr_n, 4 + ti)], tag=f"cabs{h}_{blk}_{ti}")
                kb.act(trash2[:, 0:256], xsl, AF.Square, accum_out=stcr[:, 8 + ti:9 + ti],
                       reads=[(tn, None)], writes=[("trash", None), (scr_n, 8 + ti)],
                       tag=f"csq{h}_{blk}_{ti}")
            xin_c1 = L["xin"]
            c1r = [("xin", ("c1", blk, ti2, j2)) for ti2 in range(2) for j2 in range(3)]
            kb.cp(stcr[:, 2:4], xin_c1[:, blk * 8: blk * 8 + 4: 3], eng="dve",
                  reads=c1r, writes=[(scr_n, 2), (scr_n, 3)], tag=f"c1cp0{h}_{blk}")
            kb.cp(stcr[:, 6:8], xin_c1[:, blk * 8 + 1: blk * 8 + 5: 3], eng="dve",
                  reads=c1r, writes=[(scr_n, 6), (scr_n, 7)], tag=f"c1cp1{h}_{blk}")
            kb.cp(stcr[:, 10:12], xin_c1[:, blk * 8 + 2: blk * 8 + 6: 3], eng="dve",
                  reads=c1r, writes=[(scr_n, 10), (scr_n, 11)], tag=f"c1cp2{h}_{blk}")
            stc_v = stc[:].rearrange("p (t s) -> p t s", s=4)
            raw_r = [(scr_n, j) for j in range(12)]
            kb.ts(stc_v[:, :, 0], stcr[:, 0:4], 1.0 / 256, reads=raw_r,
                  writes=[(stc_n, 0)], tag=f"cmean{h}_{blk}")
            kb.ts(stcr[:, 12:16], stcr[:, 8:12], 1.0 / 256, reads=raw_r,
                  writes=[(scr_n, 12)], tag=f"cex2{h}_{blk}")
            kb.tt(stc_v[:, :, 1], stc_v[:, :, 0], stc_v[:, :, 0], ALU.mult,
                  reads=[(stc_n, 0)], writes=[(stc_n, 1)], tag=f"cmsq{h}_{blk}")
            kb.tt(stc_v[:, :, 1], stcr[:, 12:16], stc_v[:, :, 1], ALU.subtract,
                  reads=[(scr_n, 12), (stc_n, 1)], writes=[(stc_n, 1)], tag=f"cvar{h}_{blk}")
            kb.ts(stc_v[:, :, 2], stcr[:, 4:8], 1.0 / 256, reads=raw_r,
                  writes=[(stc_n, 2)], tag=f"cam{h}_{blk}")
            kb.act(stc_v[:, :, 3], stcr[:, 8:12], AF.Sqrt, reads=raw_r,
                   writes=[(stc_n, 3)], tag=f"cl2{h}_{blk}")
            kb.tr(pf[5][0:16, 0:128], stc[:], idf[:],
                  reads=[(stc_n, kk) for kk in range(4)] + [("idf", None)],
                  writes=[("pf5", None)], tag=f"cstT{h}_{blk}")
            kb.cp(statsT[0:16, i * 128:(i + 1) * 128], pf[5][0:16, 0:128],
                  eng="dve", reads=[("pf5", None)], writes=[("statsT", i)],
                  tag=f"cstTe{h}_{blk}")
        for go in range(8):
            bank = pf[go % 2]
            bres = f"pf{go % 2}"
            kb.mm(bank[:], m1tail[0:16, go * 128:(go + 1) * 128], statsT[0:16, 0:512],
                  True, True,
                  reads=[("m1tail", None)] + [("statsT", ii) for ii in range(4)],
                  writes=[(bres, None)], tag=f"gmm{h}_{t}_{go}")
            r2 = go % 2
            hsl = h1s[:, r2 * 512:(r2 + 1) * 512]
            kb.tt(hsl, bank[:], sshin[:, go * 512:(go + 1) * 512], ALU.add,
                  reads=[(bres, None), ("copyA", None), ("h1s", r2)], writes=[("h1s", r2)],
                  tag=f"gadd{h}_{t}_{go}")
            kb.act(h1f[:, go * 512:(go + 1) * 512], hsl, AF.Gelu_apprx_tanh,
                   bias=b1f[:, go:go + 1],
                   reads=[("h1s", r2), ("b1f", None)], writes=[("copyB", go)],
                   tag=f"gact{h}_{t}_{go}")
        for kt in range(8):
            kb.mm(pf[2][0:4, 0:512], m2T[:, kt * 4:(kt + 1) * 4],
                  h1f[:, kt * 512:(kt + 1) * 512], start=(kt == 0), stop=(kt == 7),
                  reads=[("m2T", None), ("copyB", kt)], writes=[("pf2", None)],
                  tag=f"glog{h}_{t}_{kt}")
        kb.act(lgr[:], pf[2][0:4, 0:512], AF.Copy, reads=[("pf2", None)],
               writes=[("lgr", None)], tag=f"glge{h}_{t}")
        for i in range(4):
            kb.tr(pf[2][:, 128 + i * 4: 128 + (i + 1) * 4], lgr[0:4, i * 128:(i + 1) * 128],
                  idf[0:4, 0:4], reads=[("lgr", None), ("idf", None)],
                  writes=[("pf2", None)], tag=f"glgT{h}_{t}_{i}")
        kb.cp(lgc[:], pf[2][:, 128:144], eng="dve", reads=[("pf2", None)],
              writes=[("lgc", None)], tag=f"glgc{h}_{t}")

        for i in range(4):
            blk = t * 4 + i
            pp2 = i % 2
            pcol = pcolP[pp2]; o_t = o_tP[pp2]; o_t2 = o_t2P[pp2]
            pcl_n = f"pcol{pp2}"; ot_n = f"o_t{pp2}"; ot2_n = f"o_t2{pp2}"
            lsl = lgc[:, i * 4:(i + 1) * 4]
            kb.tt(lsl, lsl, L["m2b_bc"][:], ALU.add,
                  reads=[("lgc", None), ("m2b_bc", None)], writes=[("lgc", None)],
                  tag=f"sb{h}_{blk}")
            kb.red(pcol[:], lsl, ALU.max, reads=[("lgc", None)], writes=[(pcl_n, None)],
                   tag=f"smax{h}_{blk}")
            kb.ts(lsl, lsl, pcol[:], op0=ALU.subtract, reads=[("lgc", None), (pcl_n, None)],
                  writes=[("lgc", None)], tag=f"ssub{h}_{blk}")
            psl = pc[:, i * 4:(i + 1) * 4]
            kb.act(psl, lsl, AF.Exp, scale=trec[:],
                   reads=[("lgc", None), ("trec", None)], writes=[("pc", i)],
                   tag=f"sexp{h}_{blk}")
            kb.red(pcol[:], psl, ALU.add, reads=[("pc", i)], writes=[(pcl_n, None)],
                   tag=f"ssum{h}_{blk}")
            kb.recip(pcol[:], pcol[:], reads=[(pcl_n, None)], writes=[(pcl_n, None)])
            kb.ts(psl, psl, pcol[:], op0=ALU.mult, reads=[("pc", i), (pcl_n, None)],
                  writes=[("pc", i)], tag=f"sdiv{h}_{blk}")
            kb.ts(psl, psl, 0.02, op0=ALU.max, reads=[("pc", i)], writes=[("pc", i)],
                  tag=f"sflr{h}_{blk}")
            kb.red(pcol[:], psl, ALU.add, reads=[("pc", i)], writes=[(pcl_n, None)],
                   tag=f"ssum2{h}_{blk}")
            kb.recip(pcol[:], pcol[:], reads=[(pcl_n, None)], writes=[(pcl_n, None)])
            kb.ts(psl, psl, pcol[:], op0=ALU.mult, reads=[("pc", i), (pcl_n, None)],
                  writes=[("pc", i)], tag=f"sdiv2{h}_{blk}")
            kb.tt(psl[:, 0:1], psl[:, 0:1], crl_sig[:, h:h + 1], ALU.add,
                  reads=[("pc", i), ("crl_sig", None)], writes=[("pc", i)],
                  tag=f"sp0{h}_{blk}")
            kb.ts(o_t[:], ls_tok[:, blk * 256:(blk + 1) * 256], psl[:, 0:1],
                  op0=ALU.mult, reads=[("qf", None), ("pc", i), (ot_n, None)],
                  writes=[(ot_n, None)], tag=f"cb0{h}_{blk}")
            for bi, (bn, bb) in enumerate((("kf", ll_tok), ("dout", dout), ("v_tok", v_tok))):
                kb.stt(o_t[:], bb[:, blk * 256:(blk + 1) * 256], psl[:, bi + 1: bi + 2],
                       o_t[:], ALU.mult, ALU.add,
                       reads=[(bn, None), ("pc", i), (ot_n, None)], writes=[(ot_n, None)],
                       tag=f"cb{bi + 1}{h}_{blk}")
            kb.act(trash2[:, 256:512], o_t[:], AF.Square, accum_out=pcol[:],
                   reads=[(ot_n, None)], writes=[("trash", None), (pcl_n, None)],
                   tag=f"crms{h}_{blk}")
            kb.act(pcol[:], pcol[:], AF.Sqrt, scale=1.0 / 256, bias=konst[:, 0:1],
                   reads=[(pcl_n, None), ("konst", None)], writes=[(pcl_n, None)],
                   tag=f"crs{h}_{blk}")
            kb.recip(pcol[:], pcol[:], reads=[(pcl_n, None)], writes=[(pcl_n, None)])
            kb.stt(o_t2[:], o_t[:], pcol[:], normw_bc[:], ALU.mult, ALU.mult,
                   reads=[(ot_n, None), (pcl_n, None), ("normw_bc", None), (ot2_n, None)],
                   writes=[(ot2_n, None)], tag=f"cnw{h}_{blk}")
            for g in range(2):
                kb.tr(pb[1][:, g * 128:(g + 1) * 128], o_t2[:, g * 128:(g + 1) * 128],
                      idb[:], reads=[(ot2_n, None), ("idb", None)], writes=[("pb1", None)],
                      tag=f"coT{h}_{blk}_{g}")
            for g in range(2):
                kb.cp(ohn_t[:, g * 512 + i * 128: g * 512 + (i + 1) * 128],
                      pb[1][:, g * 128:(g + 1) * 128], eng="dve",
                      reads=[("pb1", None)], writes=[("ohn_t", (g, i))],
                      tag=f"coTe{h}_{blk}_{g}")
        for g in range(2):
            kb.dma("sp", ohn_scr[h * 2 + g, :, t * TT:(t + 1) * TT],
                   ohn_t[:, g * 512:(g + 1) * 512],
                   reads=[("ohn_t", (g, 0)), ("ohn_t", (g, 1)), ("ohn_t", (g, 2)), ("ohn_t", (g, 3))],
                   writes=[("ohn_scr", (h * 2 + g, t))], tag=f"ohnst{h}_{t}_{g}")


def attn_tail(kb, L):
    pf = L["pf"]
    xr, xa_scr, ohn_scr = L["xr"], L["xa_scr"], L["ohn_scr"]
    owT_sb, ohnin, xin = L["owT_sb"], L["ohnin"], L["xin"]
    win = L["win"]
    kb.s.retag("xin")
    kb.dma("pool", owT_sb[:].rearrange("p (kt o) -> p kt o", kt=8),
           win["owT"].rearrange("(kt p) o -> p kt o", p=128), writes=[("lnxf", "ow")],
           tag="ldow")
    for t in range(NT):
        kb.dma("sp", ohnin[:].rearrange("p (kt c) -> p kt c", kt=8),
               ohn_scr[:, :, t * TT:(t + 1) * TT].rearrange("g p t -> p g t"),
               reads=[("lnxf", "ohin")] + [("ohn_scr", (kt, t)) for kt in range(8)],
               writes=[("lnxf", "ohin")], tag=f"ohld{t}")
        for i in range(4):
            blk = t * 4 + i
            xsl = xin[:, i * 1024:(i + 1) * 1024]
            kb.dma("sp", xsl, xr[blk], reads=[("xin", i)], writes=[("xin", i)],
                   tag=f"tx{blk}")
            for half in range(2):
                bank = pf[3 + half]
                bres = f"pf{3 + half}"
                for kt in range(8):
                    kb.mm(bank[:], ohnin[:, kt * 512 + i * 128: kt * 512 + (i + 1) * 128],
                          owT_sb[:, kt * 1024 + half * 512: kt * 1024 + (half + 1) * 512],
                          start=(kt == 0), stop=(kt == 7),
                          reads=[("lnxf", "ohin"), ("lnxf", "ow")], writes=[(bres, None)],
                          tag=f"tmm{blk}_{half}_{kt}")
                kb.tt(xsl[:, half * 512:(half + 1) * 512], xsl[:, half * 512:(half + 1) * 512],
                      bank[:], ALU.add,
                      reads=[("xin", i), (bres, None)], writes=[("xin", i)],
                      tag=f"tadd{blk}_{half}")
            kb.dma("sp", xa_scr.rearrange("(b p) d -> b p d", p=128)[blk], xsl,
                   reads=[("xin", i)], writes=[("xa_scr", blk)], tag=f"txs{blk}")


def ffn_phase(kb, G, tag):
    R = G["R"]
    """One FFN application: x_src -> x_dst. G holds buffers/weights APs."""
    pf, pb = G["pf"], G["pb"]
    idf, idb, konst = G["idf"], G["idb"], G["konst"]
    ones_row, ones_col_b = G["ones_row"], G["ones_col_b"]
    outf, xt, lnt = G["outf"], G["xt"], G["lnt"]
    obP, h1tP, h2tP, h3tP, twP = G["ob"], G["h1t"], G["h2t"], G["h3t"], G["tw"]
    cn1, cn2, cn3, cn4 = G["cn1"], G["cn2"], G["cn3"], G["cn4"]
    cb1f, cb2f, cb3f, cb4 = G["cb1f"], G["cb2f"], G["cb3f"], G["cb4"]
    gw1, gw2, gw3 = G["gw1"], G["gw2"], G["gw3"]
    gb1, gb2, gb3, arr = G["gb1"], G["gb2"], G["gb3"], G["arr"]
    gsc, gcf, statf = G["gsc"], G["gcf"], G["statf"]
    x_src, x_dst = G["x_src"], G["x_dst"]
    xsr = x_src.rearrange("(b p) d -> b p d", p=128)
    xdr = x_dst.rearrange("(b p) d -> b p d", p=128)
    outf_v = outf[:].rearrange("p (g t) -> p g t", g=8)
    T_ = T

    for blk in range(NBLK):
        r = blk % 4
        r2_ = blk % 2
        xv = xt[:, r * 1024:(r + 1) * 1024]
        kb.dma("sp", xv, xsr[blk], reads=[(R("xt"), r), (G["x_src_name"], blk)],
               writes=[(R("xt"), r)], tag=f"f{tag}lx{blk}")
        sc = statf[:, r * 8:(r + 1) * 8]
        kb.act(G["trash"][:], xv, AF.Square, accum_out=sc[:, 0:1],
               reads=[(R("xt"), r)], writes=[(R("trash"), None), (R("statf"), (r, 0))], tag=f"f{tag}sq{blk}")
        kb.act(G["trash"][:], xv, AF.Copy, accum_out=sc[:, 1:2],
               reads=[(R("xt"), r)], writes=[(R("trash"), None), (R("statf"), (r, 1))], tag=f"f{tag}sm{blk}")
        kb.ts(sc[:, 2:3], sc[:, 1:2], 1.0 / D, reads=[(R("statf"), (r, 1))], writes=[(R("statf"), (r, 2))])
        kb.ts(sc[:, 3:4], sc[:, 0:1], 1.0 / D, reads=[(R("statf"), (r, 0))], writes=[(R("statf"), (r, 3))])
        kb.tt(sc[:, 4:5], sc[:, 2:3], sc[:, 2:3], ALU.mult, reads=[(R("statf"), (r, 2))], writes=[(R("statf"), (r, 4))])
        kb.tt(sc[:, 5:6], sc[:, 3:4], sc[:, 4:5], ALU.subtract,
              reads=[(R("statf"), (r, 3)), (R("statf"), (r, 4))], writes=[(R("statf"), (r, 5))])
        kb.act(sc[:, 6:7], sc[:, 5:6], AF.Sqrt, bias=konst[:, 0:1],
               reads=[(R("statf"), (r, 5)), ("konst", None)], writes=[(R("statf"), (r, 6))], tag=f"f{tag}rt{blk}")
        kb.recip(sc[:, 7:8], sc[:, 6:7], reads=[(R("statf"), (r, 6))], writes=[(R("statf"), (r, 7))])
        lv = lnt[:, r2_ * 1024:(r2_ + 1) * 1024]
        kb.ts(lv, xv, sc[:, 2:3], sc[:, 7:8], ALU.subtract, ALU.mult,
              reads=[(R("xt"), r), (R("statf"), (r, 2)), (R("statf"), (r, 7)), (R("lnt"), r2_)],
              writes=[(R("lnt"), r2_)], tag=f"f{tag}ln{blk}")
        for half in range(2):
            kb.mm(pf[4 + half][0:1, 0:512], ones_col_b[:], lv[:, half * 512:(half + 1) * 512],
                  start=(blk == 0), stop=(blk == NBLK - 1),
                  reads=[("ones_col_b", None), (R("lnt"), r2_)], writes=[(f"pf{4 + half}", None)],
                  tag=f"f{tag}gc{blk}_{half}")
        for half in range(2):
            bank = pb[half]
            for g4 in range(4):
                g = half * 4 + g4
                kb.tr(bank[:, g4 * 128:(g4 + 1) * 128], lv[:, g * 128:(g + 1) * 128], idb[:],
                      reads=[(R("lnt"), r2_), ("idb", None)], writes=[(f"pb{half}", None)],
                      tag=f"f{tag}tr{blk}_{g}")
            kb.cp(outf_v[:, half * 4:(half + 1) * 4, blk * 128:(blk + 1) * 128],
                  bank[:, 0:512].rearrange("p (g t) -> p g t", g=4), eng="dve",
                  reads=[(f"pb{half}", None)], writes=[(R("outf"), (blk, half))],
                  tag=f"f{tag}ev{blk}_{half}")
    for half in range(2):
        kb.act(gsc[0:1, half * 512:(half + 1) * 512], pf[4 + half][0:1, 0:512], AF.Copy,
               scale=1.0 / T_, reads=[(f"pf{4 + half}", None)], writes=[(R("gsc"), ("gc", half))],
               tag=f"f{tag}gcr{half}")
    def rowmat(src_row_keys, src_row, n_in, w_sb, w_name, n_out, out_key, bias_row, bias_key, fn):
        nk = n_in // 128
        for kt in range(nk):
            kb.tr(pf[3][:, kt:kt + 1], src_row[0:1, kt * 128:(kt + 1) * 128], idf[0:1, 0:1],
                  reads=[(R("gsc"), k2) for k2 in src_row_keys] + [("idf", None)],
                  writes=[("pf3", None)], tag=f"f{tag}gT{out_key}_{kt}")
        kb.act(gcf[:, 0:nk], pf[3][:, 0:nk], AF.Copy, reads=[("pf3", None)],
               writes=[(R("gcf"), None)], tag=f"f{tag}gTe{out_key}")
        for kt in range(nk):
            kb.mm(pf[3][0:1, 0:n_out], gcf[:, kt:kt + 1], w_sb[:, kt * n_out:(kt + 1) * n_out],
                  start=(kt == 0), stop=(kt == nk - 1),
                  reads=[(R("gcf"), None), (w_name, None)], writes=[("pf3", None)],
                  tag=f"f{tag}gm{out_key}_{kt}")
        dst = G["grow"][0:1, 0:n_out]
        kb.tt(dst, pf[3][0:1, 0:n_out], bias_row[0:1, 0:n_out], ALU.add,
              reads=[("pf3", None), (bias_key, None), (R("grow"), None)], writes=[(R("grow"), None)],
              tag=f"f{tag}gb{out_key}")
        kb.act(gsc[0:1, 1024 + out_key[1] * 512: 1024 + out_key[1] * 512 + n_out],
               G["grow"][0:1, 0:n_out], fn,
               reads=[(R("grow"), None)], writes=[(R("gsc"), out_key)], tag=f"f{tag}ga{out_key}")
    rowmat([("gc", 0), ("gc", 1)], gsc[0:1, 0:1024], 1024, gw1, R("gw1"), 512, ("g1", 0), G["gb1"], R("gb1"), AF.Gelu)
    rowmat([("g1", 0)], gsc[0:1, 1024:1536], 512, gw2, R("gw2"), 256, ("g2", 1), G["gb2"], R("gb2"), AF.Gelu)
    rowmat([("g2", 1)], gsc[0:1, 1536:1792], 256, gw3, R("gw3"), 1, ("g3", 2), G["gb3"], R("gb3"), AF.Sigmoid)
    gfv = gsc[0:1, 2048:2049]
    c1 = gsc[0:1, 2560:2561]
    c2 = gsc[0:1, 2561:2562]
    t0_ = gsc[0:1, 2562:2563]
    kb.ts(c1, arr[0:1, 0:1], 0.6, reads=[(R("arr"), None)], writes=[(R("gsc"), "c1")])
    kb.tt(t0_, arr[0:1, 0:1], gfv, ALU.mult, reads=[(R("arr"), None), (R("gsc"), ("g3", 2))],
          writes=[(R("gsc"), "t0")])
    kb.ts(t0_, t0_, 1.4, reads=[(R("gsc"), "t0")], writes=[(R("gsc"), "t0")])
    kb.stt(c2, arr[0:1, 0:1], -1.0, t0_, ALU.mult, ALU.add,
           reads=[(R("arr"), None), (R("gsc"), "t0")], writes=[(R("gsc"), "c2")])
    kb.ts(c2, c2, 0.1, op0=ALU.add, reads=[(R("gsc"), "c2")], writes=[(R("gsc"), "c2")])

    def ffn_tile(t):
        fp = t % 2
        ob = obP[fp]; h1t = h1tP[fp]; h2t = h2tP[fp]; h3t = h3tP[fp]; tw = twP[fp]
        ob_n = R(f"ob{fp}"); h1_n = R(f"h1t{fp}"); h2_n = R(f"h2t{fp}")
        h3_n = R(f"h3t{fp}"); tw_n = R(f"tw{fp}")
        bh1 = (pf[0], "pf0", pf[1], "pf1") if fp == 0 else (pf[4], "pf4", pf[5], "pf5")
        bh2 = (pf[2], "pf2") if fp == 0 else (pf[3], "pf3")
        bh3 = (pf[0], "pf0") if fp == 0 else (pf[4], "pf4")
        bt4 = (pf[1], "pf1") if fp == 0 else (pf[5], "pf5")
        for cyc in range(3):
            for go in range(8):
                kb.act(ob[:, go * 512:(go + 1) * 512], outf_v[:, go, t * TT:(t + 1) * TT],
                       AF.Copy,
                       reads=[(R("outf"), (t * 4 + i2, go // 4)) for i2 in range(4)],
                       writes=[(ob_n, go)], tag=f"f{tag}ob{t}_{cyc}_{go}")
            yield
            for go in range(8):
                bank = bh1[0] if go % 2 == 0 else bh1[2]
                bres = bh1[1] if go % 2 == 0 else bh1[3]
                for kt in range(8):
                    kb.mm(bank[:], cn1[:, kt * 1024 + go * 128: kt * 1024 + (go + 1) * 128],
                          ob[:, kt * 512:(kt + 1) * 512], start=(kt == 0), stop=(kt == 7),
                          reads=[(R("cn1"), None), (ob_n, kt)], writes=[(bres, None)],
                          tag=f"f{tag}h1m{t}_{cyc}_{go}_{kt}")
                kb.act(h1t[:, go * 512:(go + 1) * 512], bank[:], AF.Gelu,
                       bias=cb1f[:, go:go + 1],
                       reads=[(bres, None), (R("cb1f"), None)], writes=[(h1_n, go)],
                       tag=f"f{tag}h1a{t}_{cyc}_{go}")
            yield
            for go in range(4):
                bank, bres = bh2
                for kt in range(8):
                    kb.mm(bank[:], cn2[:, kt * 512 + go * 128: kt * 512 + (go + 1) * 128],
                          h1t[:, kt * 512:(kt + 1) * 512], start=(kt == 0), stop=(kt == 7),
                          reads=[(R("cn2"), None), (h1_n, kt)], writes=[(bres, None)],
                          tag=f"f{tag}h2m{t}_{cyc}_{go}_{kt}")
                kb.act(h2t[:, go * 512:(go + 1) * 512], bank[:], AF.Gelu,
                       bias=cb2f[:, go:go + 1],
                       reads=[(bres, None), (R("cb2f"), None)], writes=[(h2_n, go)],
                       tag=f"f{tag}h2a{t}_{cyc}_{go}")
            yield
            for go in range(2):
                bank, bres = bh3
                for kt in range(4):
                    kb.mm(bank[:], cn3[:, kt * 256 + go * 128: kt * 256 + (go + 1) * 128],
                          h2t[:, kt * 512:(kt + 1) * 512], start=(kt == 0), stop=(kt == 3),
                          reads=[(R("cn3"), None), (h2_n, kt)], writes=[(bres, None)],
                          tag=f"f{tag}h3m{t}_{cyc}_{go}_{kt}")
                kb.act(h3t[:, go * 512:(go + 1) * 512], bank[:], AF.Gelu,
                       bias=cb3f[:, go:go + 1],
                       reads=[(bres, None), (R("cb3f"), None)], writes=[(h3_n, go)],
                       tag=f"f{tag}h3a{t}_{cyc}_{go}")
            yield
            for kt in range(2):
                kb.mm(bt4[0][0:1, 0:512], cn4[:, kt:kt + 1], h3t[:, kt * 512:(kt + 1) * 512],
                      start=(kt == 0), stop=(kt == 1),
                      reads=[(R("cn4"), None), (h3_n, kt)], writes=[(bt4[1], None)],
                      tag=f"f{tag}t4m{t}_{cyc}_{kt}")
            tfrow = gsc[0:1, 2563 + fp * 520: 3075 + fp * 520]
            kb.act(tfrow, bt4[0][0:1, 0:512], AF.Sigmoid, bias=cb4[0:1, 0:1],
                   reads=[(bt4[1], None), (R("cb4"), None), (R("gsc"), ("tf", fp))], writes=[(R("gsc"), ("tf", fp))],
                   tag=f"f{tag}t4a{t}_{cyc}")
            kb.ts(tfrow, tfrow, c1, c2, ALU.mult, ALU.add,
                  reads=[(R("gsc"), ("tf", fp)), (R("gsc"), "c1"), (R("gsc"), "c2")],
                  writes=[(R("gsc"), ("tf", fp))],
                  tag=f"f{tag}cpl{t}_{cyc}")
            kb.mm(bt4[0][:, 0:512], ones_row[:], tfrow, True, True,
                  reads=[("ones_row", None), (R("gsc"), ("tf", fp))], writes=[(bt4[1], None)],
                  tag=f"f{tag}cbc{t}_{cyc}")
            yield
            for g in range(8):
                gsrc = (g + 4) % 8
                sign = 1.0 if g in (0, 1, 6, 7) else -1.0
                kb.stt(tw[:], bt4[0][:, 0:512], sign, ob[:, gsrc * 512:(gsrc + 1) * 512],
                       ALU.mult, ALU.mult,
                       reads=[(bt4[1], None), (ob_n, gsrc), (tw_n, None)], writes=[(tw_n, None)],
                       tag=f"f{tag}tw{t}_{cyc}_{g}")
                kb.tt(outf_v[:, g, t * TT:(t + 1) * TT],
                      outf_v[:, g, t * TT:(t + 1) * TT], tw[:], ALU.add,
                      reads=[(tw_n, None)] + [(R("outf"), (t * 4 + i2, g // 4)) for i2 in range(4)],
                      writes=[(R("outf"), (t * 4 + i2, g // 4)) for i2 in range(4)],
                      tag=f"f{tag}up{t}_{cyc}_{g}")
        yield
        for i in range(4):
            blk = t * 4 + i
            r = blk % 4
            xv = xt[:, r * 1024:(r + 1) * 1024]
            kb.dma("sp", xv, xsr[blk], reads=[(R("xt"), r), (G["x_src_name"], blk)],
               writes=[(R("xt"), r)], tag=f"f{tag}rx{blk}")
            for half in range(2):
                bank = (bh1[0] if half == 0 else bh1[2])
                bres = (bh1[1] if half == 0 else bh1[3])
                for g4 in range(4):
                    g = half * 4 + g4
                    kb.tr(bank[:, g4 * 128:(g4 + 1) * 128],
                          outf_v[:, g, t * TT + i * 128: t * TT + (i + 1) * 128], idf[:],
                          reads=[(R("outf"), (t * 4, g // 4)), (R("outf"), (t * 4 + i, g // 4)), ("idf", None)],
                          writes=[(bres, None)], tag=f"f{tag}rT{blk}_{g}")
                kb.tt(xv[:, half * 512:(half + 1) * 512], xv[:, half * 512:(half + 1) * 512],
                      bank[:], ALU.add, reads=[(R("xt"), r), (bres, None)], writes=[(R("xt"), r)],
                      tag=f"f{tag}ra{blk}_{half}")
            kb.dma("sp", xdr[blk], xv, reads=[(R("xt"), r)], writes=[(G["x_dst_name"], blk)],
                   tag=f"f{tag}rs{blk}")
            yield


    weave(ffn_tile(0), ffn_tile(1))
    weave(ffn_tile(2), ffn_tile(3))

def attnb_phase(kb, B, n_chunks=NBLK):
    pf, pb = B["pf"], B["pb"]
    idf, idb, konst = B["idf"], B["idb"], B["konst"]
    ones_row = B["ones_row"]
    rkT, Wst, Wbf, WwT, WrT = B["rkT"], B["Wst"], B["Wbf"], B["WwT"], B["WrT"]
    MT, lgcol, gpcol, gwcol, gccol, acol = B["MT"], B["lgcol"], B["gpcol"], B["gwcol"], B["gccol"], B["acol"]
    rdbf, rdT, SMT, scrf = B["rdbf"], B["rdT"], B["SMT"], B["scrf"]
    x_src, x_dst = B["x_src"], B["x_dst"]
    xsr = x_src.rearrange("(b p) d -> b p d", p=128)
    xdr = x_dst.rearrange("(b p) d -> b p d", p=128)
    rkT_v = rkT[:].rearrange("p (g t) -> p g t", g=8)
    def chunk_front(c):
        pp = c % 2
        xt = B["xt"][c % 4]; lnt32 = B["lnt32"][pp]; lnbf = B["lnbf"][pp]
        vt = B["vt"][pp]; vgw = B["vgw"][pp]; wkt = B["wkt"][pp]
        statb = B["statb"] if pp == 0 else B["statb2"]
        sb_n = "statb" if pp == 0 else "statb2"
        xt_n = f"xtb{c % 4}"; ln_n = f"lnt32{pp}"; lb_n = f"lnbf{pp}"
        vt_n = f"vt{pp}"; vg_n = f"vgw{pp}"; wk_n = f"wkt{pp}"
        vbank, vb_n = (pf[0], "pf0") if pp == 0 else (pf[3], "pf3")
        kb.dma("sp", xt[:], xsr[c], reads=[(xt_n, None), (B["x_src_name"], c)],
               writes=[(xt_n, None)], tag=f"blx{c}")
        yield
        sc = statb
        kb.act(B["trash"][:], xt[:], AF.Square, accum_out=sc[:, 0:1],
               reads=[(xt_n, None)], writes=[("trashb", None), (sb_n, 0)], tag=f"bsq{c}")
        kb.act(B["trash"][:], xt[:], AF.Copy, accum_out=sc[:, 1:2],
               reads=[(xt_n, None)], writes=[("trashb", None), (sb_n, 1)], tag=f"bsm{c}")
        kb.ts(sc[:, 2:3], sc[:, 1:2], 1.0 / D, reads=[(sb_n, 1)], writes=[(sb_n, 2)])
        kb.ts(sc[:, 3:4], sc[:, 0:1], 1.0 / D, reads=[(sb_n, 0)], writes=[(sb_n, 3)])
        kb.tt(sc[:, 4:5], sc[:, 2:3], sc[:, 2:3], ALU.mult, reads=[(sb_n, 2)], writes=[(sb_n, 4)])
        kb.tt(sc[:, 5:6], sc[:, 3:4], sc[:, 4:5], ALU.subtract,
              reads=[(sb_n, 3), (sb_n, 4)], writes=[(sb_n, 5)])
        kb.act(sc[:, 6:7], sc[:, 5:6], AF.Sqrt, bias=konst[:, 0:1],
               reads=[(sb_n, 5), ("konst", None)], writes=[(sb_n, 6)], tag=f"brt{c}")
        kb.recip(sc[:, 7:8], sc[:, 6:7], reads=[(sb_n, 6)], writes=[(sb_n, 7)])
        yield
        kb.ts(lnt32[:], xt[:], sc[:, 2:3], sc[:, 7:8], ALU.subtract, ALU.mult,
              reads=[(xt_n, None), (sb_n, 2), (sb_n, 7), (ln_n, None)],
              writes=[(ln_n, None)], tag=f"bln{c}")
        kb.act(lnbf[:], lnt32[:], AF.Copy, reads=[(ln_n, None)], writes=[(lb_n, None)],
               tag=f"blnb{c}")
        yield
        for half in range(2):
            bank = pb[half]
            for g4 in range(4):
                g = half * 4 + g4
                kb.tr(bank[:, g4 * 128:(g4 + 1) * 128], lnbf[:, g * 128:(g + 1) * 128], idb[:],
                      reads=[(lb_n, None), ("idb", None)], writes=[(f"pb{half}", None)],
                      tag=f"btr{c}_{g}")
            kb.cp(rkT_v[:, half * 4:(half + 1) * 4, 1 + c * 128: 1 + (c + 1) * 128],
                  bank[:, 0:512].rearrange("p (g t) -> p g t", g=4), eng="dve",
                  reads=[(f"pb{half}", None)], writes=[("rkT", (c, half))],
                  tag=f"bev{c}_{half}")
        yield
        rk_reads = [("rkT", (c, 0)), ("rkT", (c, 1))]
        for half in range(2) if "B" in ATTNB_PARTS else []:
            for kt in range(8):
                kb.mm(vbank[:], rkT_v[:, kt, 1 + c * 128: 1 + (c + 1) * 128],
                      WwT[:, kt * 1024 + half * 512: kt * 1024 + (half + 1) * 512],
                      start=(kt == 0), stop=(kt == 7),
                      reads=rk_reads + [("WwT", None)], writes=[(vb_n, None)],
                      tag=f"bvm{c}_{half}_{kt}")
            kb.act(vt[:, half * 512:(half + 1) * 512], vbank[:], AF.Copy,
                   reads=[(vb_n, None)], writes=[(vt_n, half)], tag=f"bve{c}_{half}")
            kb.ts(vgw[:, half * 512:(half + 1) * 512], vbank[:], gwcol[:], op0=ALU.mult,
                  reads=[(vb_n, None), ("gwcol", None)], writes=[(vg_n, half)],
                  tag=f"bvg{c}_{half}")
        yield
        for half in range(2) if "C" in ATTNB_PARTS else []:
            bank = pb[half]
            for g4 in range(4):
                g = half * 4 + g4
                kb.tr(bank[:, g4 * 128:(g4 + 1) * 128], rkT_v[:, g, c * 128: (c + 1) * 128],
                      idb[:],
                      reads=[("rkT", (c, half)), ("rkT", (c - 1, half)), ("idb", None)],
                      writes=[(f"pb{half}", None)], tag=f"bwk{c}_{g}")
            kb.cp(wkt[:, half * 512:(half + 1) * 512],
                  bank[:, 0:512], eng="dve",
                  reads=[(f"pb{half}", None)], writes=[(wk_n, half)],
                  tag=f"bwke{c}_{half}")
        yield

    def chunk_tail(c):
        pp = c % 2
        xt = B["xt"][c % 4]; lnt32 = B["lnt32"][pp]
        vt = B["vt"][pp]; vgw = B["vgw"][pp]; wkt = B["wkt"][pp]
        xt_n = f"xtb{c % 4}"; ln_n = f"lnt32{pp}"
        vt_n = f"vt{pp}"; vg_n = f"vgw{pp}"; wk_n = f"wkt{pp}"
        rk_reads = [("rkT", (c, 0)), ("rkT", (c, 1))]
        for half in range(2) if "D" in ATTNB_PARTS else []:
            bank = pf[1 + half]
            for kt in range(8):
                kb.mm(bank[:], rkT_v[:, kt, 1 + c * 128: 1 + (c + 1) * 128],
                      Wbf[:, kt * 1024 + half * 512: kt * 1024 + (half + 1) * 512],
                      start=(kt == 0), stop=(kt == 7),
                      reads=rk_reads + [("Wbf", None)], writes=[(f"pf{1 + half}", None)],
                      tag=f"bim{c}_{half}_{kt}")
        yield
        for kt in range(8) if "E" in ATTNB_PARTS else []:
            kb.mm(pf[3][:, 0:128], rkT_v[:, kt, c * 128:(c + 1) * 128],
                  rkT_v[:, kt, 1 + c * 128: 1 + (c + 1) * 128],
                  start=(kt == 0), stop=(kt == 7),
                  reads=rk_reads + [("rkT", (c - 1, 0)), ("rkT", (c - 1, 1))],
                  writes=[("pf3", None)], tag=f"bsm{c}_{kt}")
        if "E" in ATTNB_PARTS:
            kb.tt(SMT[:], pf[3][:, 0:128], MT[:], ALU.mult,
                  reads=[("pf3", None), ("MT", None)], writes=[("SMT", None)], tag=f"bsmt{c}")
        yield
        for half in range(2) if "E" in ATTNB_PARTS else []:
            kb.mm(pf[4 + half][:], SMT[:], vt[:, half * 512:(half + 1) * 512], True, True,
                  reads=[("SMT", None), (vt_n, half)], writes=[(f"pf{4 + half}", None)],
                  tag=f"bia{c}_{half}")
        yield
        for half in range(2) if "E" in ATTNB_PARTS and "D" in ATTNB_PARTS else []:
            kb.act(scrf[:, 0:512], pf[4 + half][:], AF.Copy,
                   reads=[(f"pf{4 + half}", None), ("scrf", None)], writes=[("scrf", None)],
                   tag=f"bre{c}_{half}")
            kb.stt(rdbf[:, half * 512:(half + 1) * 512], pf[1 + half][:], gpcol[:],
                   scrf[:, 0:512], ALU.mult, ALU.add,
                   reads=[(f"pf{1 + half}", None), ("gpcol", None), ("scrf", None)],
                   writes=[("rdbf", half)], tag=f"brd{c}_{half}")
        yield
        for half in range(2) if "F" in ATTNB_PARTS else []:
            bank = pb[half]
            for g4 in range(4):
                g = half * 4 + g4
                kb.tr(bank[:, g4 * 128:(g4 + 1) * 128], rdbf[:, g * 128:(g + 1) * 128], idb[:],
                      reads=[("rdbf", 0), ("rdbf", 1), ("idb", None)],
                      writes=[(f"pb{half}", None)], tag=f"brT{c}_{g}")
            kb.cp(rdT[:, half * 512:(half + 1) * 512], bank[:, 0:512], eng="dve",
                  reads=[(f"pb{half}", None)], writes=[("rdT", half)], tag=f"brTe{c}_{half}")
        yield
        for half in range(2) if "F" in ATTNB_PARTS else []:
            bank = pf[1 + half]
            for kt in range(8):
                kb.mm(bank[:], rdT[:, kt * 128:(kt + 1) * 128],
                      WrT[:, kt * 1024 + half * 512: kt * 1024 + (half + 1) * 512],
                      start=(kt == 0), stop=(kt == 7),
                      reads=[("rdT", 0), ("rdT", 1), ("WrT", None)],
                      writes=[(f"pf{1 + half}", None)], tag=f"brom{c}_{half}_{kt}")
            kb.stt(scrf[:, 512:1024], bank[:], acol[:], lnt32[:, half * 512:(half + 1) * 512],
                   ALU.mult, ALU.add,
                   reads=[(f"pf{1 + half}", None), ("acol", None), (ln_n, None), ("scrf", None)],
                   writes=[("scrf", None)], tag=f"bxu{c}_{half}")
            kb.tt(xt[:, half * 512:(half + 1) * 512], xt[:, half * 512:(half + 1) * 512],
                  scrf[:, 512:1024], ALU.add,
                  reads=[(xt_n, None), ("scrf", None)], writes=[(xt_n, None)],
                  tag=f"bxa{c}_{half}")
        yield
        kb.dma("sp", xdr[c], xt[:], reads=[(xt_n, None)], writes=[(B["x_dst_name"], c)],
               tag=f"bxs{c}")
        yield
        for jt in range(8) if "G" in ATTNB_PARTS else []:
            for half in range(2):
                bank = pf[4 + half]
                kb.mm(bank[:], wkt[:, jt * 128:(jt + 1) * 128],
                      vgw[:, half * 512:(half + 1) * 512], True, True,
                      reads=[(wk_n, jt // 4), (vg_n, half)], writes=[(f"pf{4 + half}", None)],
                      tag=f"bwu{c}_{jt}_{half}")
                dst = Wst[:, jt * 1024 + half * 512: jt * 1024 + (half + 1) * 512]
                kb.stt(dst, dst, gccol[:], bank[:], ALU.mult, ALU.add,
                       reads=[("Wst", (jt, half)), ("gccol", None), (f"pf{4 + half}", None)],
                       writes=[("Wst", (jt, half))], tag=f"bws{c}_{jt}_{half}")
                kb.cp(Wbf[:, jt * 1024 + half * 512: jt * 1024 + (half + 1) * 512], dst,
                      eng="act", reads=[("Wst", (jt, half)), ("Wbf", (jt, half))],
                      writes=[("Wbf", (jt, half))], tag=f"bwb{c}_{jt}_{half}")


    weave(chunk_front(0))
    for c in range(n_chunks):
        weave(chunk_tail(c), chunk_front(c + 1) if c + 1 < n_chunks else None)

def build_program(dbg=False, stop_after=None):
    nc = bass.Bass("TRN2", target_bir_lowering=False,
                   detect_race_conditions=EMIT_DRAINS)
    kb = KB(nc, dbg=dbg)
    s = kb.s

    x_hbm = kb.din("x", [T, D])
    xr = x_hbm.rearrange("(b p) d -> b p d", p=128)
    consts = prep_consts()
    cin = {k: kb.din("c_" + k, list(v.shape)) for k, v in consts.items()}
    wshapes = {k: list(v.shape) for k, v in prep_weights(_dummy_ins()).items()}
    win = {k: kb.din("w_" + k, v) for k, v in wshapes.items()}
    out_hbm = kb.dout("out", [T, D])
    outr = out_hbm.rearrange("(b p) d -> b p d", p=128)
    ssh_scr = kb.dint("ssh_scr", [8, 128, T], BF16)
    ohn_scr = kb.dint("ohn_scr", [8, 128, T], BF16)
    xa_scr = kb.dint("xa_scr", [T, D], F32)
    xb_scr = kb.dint("xb_scr", [T, D], F32)

    idf = kb.sbuf("idf", [128, 128], F32)
    idb = kb.sbuf("idb", [128, 128], BF16)
    ones_row = kb.sbuf("ones_row", [1, 128], F32)
    ones_row_b = kb.sbuf("ones_row_b", [1, 128], BF16)
    ones_col_b = kb.sbuf("ones_col_b", [128, 1], BF16)
    konst = kb.sbuf("konst", [128, 8], F32)
    maskSL = kb.sbuf("maskSL", [128, 128], BF16)
    maskIU = kb.sbuf("maskIU", [128, 128], BF16)
    lnxf = kb.sbuf("lnxf", [128, 8 * T], BF16)
    m1main = kb.sbuf("m1main", [128, 8 * 1024], BF16)
    bwT = kb.sbuf("bwT", [128, 8 * 4], BF16)
    beta_cols = kb.sbuf("beta_cols", [128, NBLK * 4], F32)
    xin = kb.sbuf("xin", [128, 4 * 1024], F32)
    lntok = kb.sbuf("lntok", [128, 4 * 1024], BF16)
    trash = kb.sbuf("trash", [128, 1024], BF16)
    statc = kb.sbuf("statc", [128, 4 * 8], F32)
    ssh_t = kb.sbuf("ssh_t", [128, 2 * 512], BF16)
    beta_rows = None

    pf = [kb.psum(f"pf{i}", [128, 512], F32) for i in range(6)]
    pb = [kb.psum(f"pb{i}", [128, 1024], BF16) for i in range(2)]

    kb.dma("sp", idf[:], cin["identity"], writes=[("idf", None)])
    kb.dma("pool", idb[:], cin["identity"], writes=[("idb", None)])
    kb.dma("sp", ones_row[:], cin["ones_row"], writes=[("ones_row", None)])
    kb.dma("sp", konst[:], cin["konst"], writes=[("konst", None)])
    kb.dma("pool", ones_row_b[:], cin["ones_row"], writes=[("ones_row_b", None)])
    kb.dma("pool", ones_col_b[:], cin["ones_col"], writes=[("ones_col_b", None)])
    kb.dma("pool", maskSL[:], cin["maskSL"], writes=[("maskSL", None)])
    kb.dma("pool", maskIU[:], cin["maskIU"], writes=[("maskIU", None)])
    kb.dma("pool", m1main[:].rearrange("p (kt o) -> p kt o", kt=8),
           win["m1main"].rearrange("(kt p) o -> p kt o", p=128),
           writes=[("m1main", None)])
    kb.dma("pool", bwT[:].rearrange("p (kt h) -> p kt h", kt=8),
           win["bwT"].rearrange("(kt p) h -> p kt h", p=128),
           writes=[("bwT", None)])

    RL = 4
    for blk in range(NBLK):
        r = blk % RL
        xv = xin[:, r * 1024:(r + 1) * 1024]
        kb.dma("sp", xv, xr[blk], reads=[("xin", r)], writes=[("xin", r)], tag=f"ldx{blk}")
        sc = statc[:, r * 8:(r + 1) * 8]
        kb.act(trash[:], xv, AF.Square, accum_out=sc[:, 0:1],
               reads=[("xin", r)], writes=[("trash", None), ("statc", (r, 0))], tag=f"sq{blk}")
        kb.act(trash[:], xv, AF.Copy, accum_out=sc[:, 1:2],
               reads=[("xin", r)], writes=[("trash", None), ("statc", (r, 1))], tag=f"sum{blk}")
        kb.ts(sc[:, 2:3], sc[:, 1:2], 1.0 / D, reads=[("statc", (r, 1))],
              writes=[("statc", (r, 2))])
        kb.ts(sc[:, 3:4], sc[:, 0:1], 1.0 / D, reads=[("statc", (r, 0))],
              writes=[("statc", (r, 3))])
        kb.tt(sc[:, 4:5], sc[:, 2:3], sc[:, 2:3], ALU.mult,
              reads=[("statc", (r, 2))], writes=[("statc", (r, 4))])
        kb.tt(sc[:, 5:6], sc[:, 3:4], sc[:, 4:5], ALU.subtract,
              reads=[("statc", (r, 3)), ("statc", (r, 4))], writes=[("statc", (r, 5))])
        kb.act(sc[:, 6:7], sc[:, 5:6], AF.Sqrt, bias=konst[:, 0:1],
               reads=[("statc", (r, 5)), ("konst", None)], writes=[("statc", (r, 6))])
        kb.recip(sc[:, 7:8], sc[:, 6:7], reads=[("statc", (r, 6))],
                 writes=[("statc", (r, 7))])
        lv = lntok[:, r * 1024:(r + 1) * 1024]
        kb.ts(lv, xv, sc[:, 2:3], sc[:, 7:8], ALU.subtract, ALU.mult,
              reads=[("xin", r), ("statc", (r, 2)), ("statc", (r, 7)), ("lntok", r)],
              writes=[("lntok", r)], tag=f"ln{blk}")
        for half in range(2):
            bank = pb[half]
            for g4 in range(4):
                g = half * 4 + g4
                kb.tr(bank[:, g4 * 128:(g4 + 1) * 128], lv[:, g * 128:(g + 1) * 128], idb[:],
                      reads=[("lntok", r), ("idb", None)], writes=[(f"pb{half}", None)],
                      tag=f"trln{blk}_{g}")
            lnxf_v = lnxf[:].rearrange("p (g t) -> p g t", g=8)
            kb.cp(lnxf_v[:, half * 4:(half + 1) * 4, blk * 128:(blk + 1) * 128],
                  bank[:, 0:512].rearrange("p (g t) -> p g t", g=4),
                  eng="dve",
                  reads=[(f"pb{half}", None)], writes=[("lnxf", (blk, half))],
                  tag=f"evln{blk}_{half}")

    kb.dump("lnxf", lnxf[:], [("lnxf", None)], cast=True)

    kb.s.retag("xin")
    beta_rows = xin[0:4, 0:T]
    for t in range(NT):
        bank = pf[t % 2]
        for kt in range(8):
            kb.mm(bank[0:4, 0:512], bwT[:, kt * 4:(kt + 1) * 4],
                  lnxf[:].rearrange("p (g t) -> p g t", g=8)[:, kt, t * TT:(t + 1) * TT],
                  start=(kt == 0), stop=(kt == 7),
                  reads=[("bwT", None)] + [("lnxf", (t * 4 + i, h)) for i in range(4) for h in range(2)],
                  writes=[(f"pf{t % 2}", None)], tag=f"betamm{t}_{kt}")
        kb.act(beta_rows[:, t * TT:(t + 1) * TT], bank[0:4, 0:512], AF.Sigmoid,
               reads=[(f"pf{t % 2}", None)], writes=[("xin", ("beta", t))], tag=f"betasig{t}")
        kb.ts(beta_rows[:, t * TT:(t + 1) * TT], beta_rows[:, t * TT:(t + 1) * TT],
              1e-6, op0=ALU.max, reads=[("xin", ("beta", t))], writes=[("xin", ("beta", t))])
    for q in range(4):
        bank = pf[2]
        for i in range(4):
            blk = q * 4 + i
            kb.tr(bank[:, i * 4:(i + 1) * 4], beta_rows[0:4, blk * 128:(blk + 1) * 128],
                  idf[0:4, 0:4],
                  reads=[("xin", ("beta", blk // 4)), ("idf", None)], writes=[("pf2", None)],
                  tag=f"trbeta{blk}")
        kb.cp(beta_cols[:, q * 16:(q + 1) * 16], bank[:, 0:16], eng="dve",
              reads=[("pf2", None)], writes=[("beta_cols", q)], tag=f"evbeta{q}")
    kb.dump("beta_cols", beta_cols[:], [("beta_cols", None)])

    lnxf_v = lnxf[:].rearrange("p (g t) -> p g t", g=8)
    for t in range(NT):
        ln_reads = [("lnxf", (t * 4 + i, h)) for i in range(4) for h in range(2)]
        for go in range(8):
            bank = pf[go % 2]
            for kt in range(8):
                kb.mm(bank[:], m1main[:, kt * 1024 + go * 128: kt * 1024 + (go + 1) * 128],
                      lnxf_v[:, kt, t * TT:(t + 1) * TT],
                      start=(kt == 0), stop=(kt == 7),
                      reads=[("m1main", None)] + ln_reads,
                      writes=[(f"pf{go % 2}", None)], tag=f"sshmm{t}_{go}_{kt}")
            rr = (t * 8 + go) % 2
            sshv = ssh_t[:, rr * 512:(rr + 1) * 512]
            kb.act(sshv, bank[:], AF.Copy,
                   reads=[(f"pf{go % 2}", None), ("ssh_t", rr)], writes=[("ssh_t", rr)],
                   tag=f"sshev{t}_{go}")
            kb.dma("sp", ssh_scr[go, :, t * TT:(t + 1) * TT], sshv,
                   reads=[("ssh_t", rr)], writes=[("ssh_scr", (t, go))], tag=f"sshst{t}_{go}")

    if stop_after == "phaseA":
        return nc, kb

    wq = kb.sbuf("wq", [128, 2048], BF16)
    wk = kb.sbuf("wk", [128, 2048], BF16)
    wv = kb.sbuf("wv", [128, 2048], BF16)
    qcw = kb.sbuf("qcw", [128, 32], F32)
    kcw = kb.sbuf("kcw", [128, 32], F32)
    vcw = kb.sbuf("vcw", [128, 32], F32)
    m1tail = kb.sbuf("m1tail", [16, 1024], BF16)
    m2T = kb.sbuf("m2T", [128, 32], BF16)
    b1f = kb.sbuf("b1f", [128, 8], F32)
    normw_bc = kb.sbuf("normw_bc", [128, 256], F32)
    crl_sig = kb.sbuf("crl_sig", [128, 4], F32)
    trec = kb.sbuf("trec", [128, 1], F32)
    srow = kb.sbuf("srow", [1, 8], F32)
    nbeta_cols = kb.sbuf("nbeta_cols", [128, NBLK * 4], F32)
    pre = kb.sbuf("pre", [128, 2 * 516], BF16)
    qf = kb.sbuf("qf", [128, 2 * 2048], BF16)
    kf = kb.sbuf("kf", [128, 2 * 2048], BF16)
    vfp = kb.sbuf("vfp", [128, 2 * 2112], BF16)
    kn_tok = kb.sbuf("kn_tok", [128, 4096], BF16)
    kbn_tok = kb.sbuf("kbn_tok", [128, 4096], BF16)
    v_tok = kb.sbuf("v_tok", [128, 4096], BF16)
    vb_tok = kb.sbuf("vb_tok", [128, 4096], BF16)
    copyA = kb.sbuf("copyA", [128, 4096], BF16)
    copyB = kb.sbuf("copyB", [128, 17 * 256], BF16)
    dout = kb.sbuf("dout", [128, 4096], BF16)
    ls_tok = qf
    ll_tok = kf
    S32 = kb.sbuf("S32", [128, 512], F32)
    Sbf = kb.sbuf("Sbf", [128, 512], BF16)
    dpar = []
    for pp in range(2):
        d = {}
        for nm, w_ in (("Y32", 128), ("Ybf", 128), ("Pt", 128), ("PTt", 128),
                       ("Q0", 128), ("QT0", 128), ("Q1", 128), ("QT1", 128),
                       ("attnT", 128), ("wT_sb", 256), ("u_sb", 256), ("uadj", 256)):
            dt_ = F32 if nm == "Y32" else BF16
            d[nm] = kb.sbuf(f"{nm}_{pp}", [128, w_], dt_)
        dpar.append(d)
    sq = trash
    rq = kb.sbuf("rq", [1, 512], F32)
    stcr = [kb.sbuf(f"stcr{p}", [128, 16], F32) for p in range(2)]
    stc = [kb.sbuf(f"stc{p}", [128, 16], F32) for p in range(2)]
    statsT = kb.sbuf("statsT", [16, 512], BF16)
    h1s = kb.sbuf("h1s", [128, 1024], BF16)
    sshd = kb.sbuf("sshd", [128, 4096], BF16)
    h1f = copyB
    sshin = None
    lgr = kb.sbuf("lgr", [4, 512], F32)
    lgc = kb.sbuf("lgc", [128, 16], F32)
    pc = kb.sbuf("pc", [128, 16], F32)
    pcol = [kb.sbuf(f"pcol{p}", [128, 1], F32) for p in range(2)]
    o_t = [kb.sbuf(f"o_t{p}", [128, 256], BF16) for p in range(2)]
    o_t2 = [kb.sbuf(f"o_t2{p}", [128, 256], BF16) for p in range(2)]
    ohn_t = kb.sbuf("ohn_t", [128, 1024], BF16)
    trash2 = trash

    kb.dma("pool", m1tail[:].rearrange("s (kt o) -> s kt o", kt=8),
           win["m1tail"].rearrange("s (kt o) -> s kt o", kt=8), writes=[("m1tail", None)])
    kb.dma("pool", m2T[:].rearrange("p (kt f) -> p kt f", kt=8),
           win["m2T"].rearrange("(kt p) f -> p kt f", p=128), writes=[("m2T", None)])
    kb.dma("sp", b1f[:], win["b1f"], writes=[("b1f", None)])
    for nm, wkey in (("qcw", "qcw"), ("kcw", "kcw"), ("vcw", "vcw")):
        kb.dma("sp", kb.sb[nm][:], win[wkey].rearrange("p g j -> p (g j)"),
               writes=[(nm, None)])
    nwrow = xin[0:1, 2048:2304]
    kb.s.retag("xin")
    kb.dma("sp", nwrow, win["normw_row"], reads=[("xin", "nw")], writes=[("xin", "nw")])
    kb.mm(pf[5][:, 0:256], ones_row[:], nwrow, True, True,
          reads=[("ones_row", None), ("xin", "nw")], writes=[("pf5", None)], tag="nwbc")
    kb.cp(normw_bc[:], pf[5][:, 0:256], eng="dve", reads=[("pf5", None)],
          writes=[("normw_bc", None)])
    crl_row = kb.sbuf("crl_row", [1, 4], F32)
    kb.dma("sp", crl_row[:], win["crl_row"], writes=[("crl_row", None)])
    kb.act(crl_row[:], crl_row[:], AF.Sigmoid, reads=[("crl_row", None)],
           writes=[("crl_row", None)])
    kb.mm(pf[5][:, 256:260], ones_row[:], crl_row[:], True, True,
          reads=[("ones_row", None), ("crl_row", None)], writes=[("pf5", None)], tag="crlbc")
    kb.cp(crl_sig[:], pf[5][:, 256:260], eng="dve", reads=[("pf5", None)],
          writes=[("crl_sig", None)])
    ltrow = kb.sbuf("ltrow", [1, 4], F32)
    kb.dma("sp", ltrow[:, 0:1], win["ltemp"], writes=[("ltrow", 0)])
    kb.act(ltrow[:, 1:2], ltrow[:, 0:1], AF.Abs, reads=[("ltrow", 0)], writes=[("ltrow", 1)])
    kb.act(ltrow[:, 1:2], ltrow[:, 1:2], AF.Exp, scale=-1.0, reads=[("ltrow", 1)], writes=[("ltrow", 1)])
    kb.act(ltrow[:, 1:2], ltrow[:, 1:2], AF.Ln, bias=konst[0:1, 2:3], reads=[("ltrow", 1), ("konst", None)], writes=[("ltrow", 1)])
    kb.act(ltrow[:, 2:3], ltrow[:, 0:1], AF.Relu, reads=[("ltrow", 0)], writes=[("ltrow", 2)])
    kb.tt(ltrow[:, 3:4], ltrow[:, 1:2], ltrow[:, 2:3], ALU.add,
          reads=[("ltrow", 1), ("ltrow", 2)], writes=[("ltrow", 3)])
    kb.recip(ltrow[:, 3:4], ltrow[:, 3:4], reads=[("ltrow", 3)], writes=[("ltrow", 3)])
    kb.mm(pf[5][:, 260:261], ones_row[:], ltrow[:, 3:4], True, True,
          reads=[("ones_row", None), ("ltrow", 3)], writes=[("pf5", None)], tag="trbc")
    kb.cp(trec[:], pf[5][:, 260:261], eng="dve", reads=[("pf5", None)], writes=[("trec", None)])
    m2b_bc = kb.sbuf("m2b_bc", [128, 4], F32)
    m2brow = kb.sbuf("m2brow", [1, 4], F32)
    kb.dma("sp", m2brow[:], win["m2b_row"], writes=[("m2brow", None)])
    kb.mm(pf[5][:, 261:265], ones_row[:], m2brow[:], True, True,
          reads=[("ones_row", None), ("m2brow", None)], writes=[("pf5", None)], tag="m2bc")
    kb.cp(m2b_bc[:], pf[5][:, 261:265], eng="dve", reads=[("pf5", None)],
          writes=[("m2b_bc", None)])
    kb.ts(nbeta_cols[:], beta_cols[:], -1.0, reads=[("beta_cols", None)],
          writes=[("nbeta_cols", None)])

    qf_v = qf[:].rearrange("p (g t) -> p g t", g=2)
    kf_v = kf[:].rearrange("p (g t) -> p g t", g=2)
    vfp_v = vfp[:].rearrange("p (g t) -> p g t", g=2)

    def load_head_weights(h):
        for nm, wsrc in (("wq", "qwT"), ("wk", "kwT"), ("wv", "vwT")):
            dst = kb.sb[nm][:].rearrange("p (kt c) -> p kt c", kt=8)
            srcv = win[wsrc].rearrange("(kt p) c -> p kt c", p=128)[:, :, h * 256:(h + 1) * 256]
            kb.dma("pool", dst, srcv, reads=[(nm, None)], writes=[(nm, None)],
                   tag=f"ldw_{nm}{h}")

    def proj_conv(h, tensor, wname, cwname, dstv, dst_name, pad):
        for g in range(2):
            for t in range(NT):
                bank = pf[t % 2]
                bank_res = f"pf{t % 2}"
                w_sb = kb.sb[wname]
                for kt in range(8):
                    kb.mm(bank[:], w_sb[:, kt * 256 + g * 128: kt * 256 + (g + 1) * 128],
                          lnxf_v[:, kt, t * TT:(t + 1) * TT],
                          start=(kt == 0), stop=(kt == 7),
                          reads=[(wname, None)] + [("lnxf", (t * 4 + i2, hh)) for i2 in range(4) for hh in range(2)],
                          writes=[(bank_res, None)], tag=f"pj{tensor}{h}{g}{t}")
                e = (g * NT + t) % 2
                pe_sl = pre[:, e * 516:(e + 1) * 516]
                kb.act(pe_sl[:, 4:516], bank[:], AF.Copy,
                       reads=[(bank_res, None), ("pre", e)], writes=[("pre", e)],
                       tag=f"pcv{tensor}{h}{g}{t}")
                if t == 0:
                    kb.memset(pe_sl[:, 0:4], 0.0, reads=[("pre", e)], writes=[("pre", e)])
                else:
                    ep = (g * NT + t - 1) % 2
                    kb.cp(pe_sl[:, 0:4], pre[:, ep * 516 + 512: ep * 516 + 516],
                          eng="dve", reads=[("pre", ep), ("pre", e)], writes=[("pre", e)])
                dsl = dstv[:, g, pad + t * TT: pad + (t + 1) * TT]
                cw = kb.sb[cwname]
                gg = h * 2 + g
                kb.ts(dsl, pe_sl[:, 1:513], cw[:, gg * 4 + 0: gg * 4 + 1],
                      op0=ALU.mult,
                      reads=[("pre", e), (cwname, None), (dst_name, (g, t))],
                      writes=[(dst_name, (g, t))], tag=f"cv0{tensor}{h}{g}{t}")
                for j in range(1, 4):
                    kb.stt(dsl, pe_sl[:, 1 + j: 513 + j], cw[:, gg * 4 + j: gg * 4 + j + 1],
                           dsl, ALU.mult, ALU.add,
                           reads=[("pre", e), (cwname, None), (dst_name, (g, t))],
                           writes=[(dst_name, (g, t))], tag=f"cv{j}{tensor}{h}{g}{t}")
                kb.act(dsl, dsl, AF.Silu, reads=[(dst_name, (g, t))],
                       writes=[(dst_name, (g, t))], tag=f"silu{tensor}{h}{g}{t}")

    def l2norm(h, srcv, src_name, pad):
        for t in range(NT):
            r = 0
            for g in range(2):
                kb.act(sq[:, g * 512:(g + 1) * 512], srcv[:, g, pad + t * TT: pad + (t + 1) * TT],
                       AF.Square, reads=[(src_name, (g, t))], writes=[("trash", None)],
                       tag=f"l2sq{src_name}{h}{t}{g}")
            for g in range(2):
                kb.mm(pf[2][0:1, 0:512], ones_col_b[:], sq[:, g * 512:(g + 1) * 512],
                      start=(g == 0), stop=(g == 1),
                      reads=[("ones_col_b", None), ("trash", None)],
                      writes=[("pf2", None)], tag=f"l2mm{src_name}{h}{t}{g}")
            rsl = rq[:, r * 512:(r + 1) * 512]
            kb.act(rsl, pf[2][0:1, 0:512], AF.Sqrt, bias=konst[0:1, 1:2],
                   reads=[("pf2", None), ("konst", None), ("rq", r)], writes=[("rq", r)],
                   tag=f"l2rt{src_name}{h}{t}")
            kb.recip(rsl, rsl, reads=[("rq", r)], writes=[("rq", r)])
            kb.mm(pf[2][:, 512 - 512:512] if False else pf[2][:, 0:512],
                  ones_row[:], rsl, True, True,
                  reads=[("ones_row", None), ("rq", r)], writes=[("pf2", None)],
                  tag=f"l2bc{src_name}{h}{t}")
            for g in range(2):
                dsl = srcv[:, g, pad + t * TT: pad + (t + 1) * TT]
                kb.tt(dsl, dsl, pf[2][:, 0:512], ALU.mult,
                      reads=[(src_name, (g, t)), ("pf2", None)],
                      writes=[(src_name, (g, t))], tag=f"l2ap{src_name}{h}{t}{g}")

    def transpose_blocks(h, srcv, src_name, src_col0, dsts, blk_range=range(NBLK),
                         col_of_blk=None, tag=""):
        W_src = srcv.shape[2]
        for blk in blk_range:
            bank = pb[blk % 2]
            bank_res = f"pb{blk % 2}"
            c0 = col_of_blk(blk) if col_of_blk else (src_col0 + blk * 128)
            W = min(128, W_src - c0)
            for g in range(2):
                kb.tr(bank[0:W, g * 128:(g + 1) * 128], srcv[:, g, c0:c0 + W], idb[:],
                      reads=[(src_name, None), ("idb", None)], writes=[(bank_res, None)],
                      tag=f"tr{tag}{h}_{blk}_{g}")
            for (dst, dst_name, kind, scol_fn) in dsts:
                dsl = dst[0:W, blk * 256:(blk + 1) * 256]
                if kind == "copy":
                    kb.cp(dsl, bank[0:W, 0:256], eng="dve",
                          reads=[(bank_res, None)], writes=[(dst_name, blk)],
                          tag=f"ev{tag}{dst_name}{h}_{blk}")
                else:
                    kb.ts(dsl, bank[0:W, 0:256], scol_fn(blk), op0=ALU.mult,
                          reads=[(bank_res, None), ("beta_cols", None)],
                          writes=[(dst_name, blk)], tag=f"ev{tag}{dst_name}{h}_{blk}")

    def delta_env(c):
        pp = c % 2
        D_ = dpar[pp]
        def RN(b):
            return f"{b}_{pp}"
        bA, bB, bC = (("pf0", pf[0]), ("pf1", pf[1]), ("pf2", pf[2])) if pp == 0 \
            else (("pf3", pf[3]), ("pf4", pf[4]), ("pf5", pf[5]))
        pbank = ("pb0", pb[0]) if pp == 0 else ("pb1", pb[1])
        return D_, RN, bA, bB, bC, pbank

    def delta_pre(h, c):
        D_, RN, bA, bB, bC, pbank = delta_env(c)
        Ptp, PTtp, Ybfp = D_["Pt"], D_["PTt"], D_["Ybf"]
        u_p, wTp = D_["u_sb"], D_["wT_sb"]
        nbcol = nbeta_cols[:, (c * 4 + h):(c * 4 + h) + 1]
        ccols = slice(c * 128, (c + 1) * 128)
        for g in range(2):
            kb.mm(bA[1][:, 0:128], kf_v[:, g, ccols], kf_v[:, g, ccols],
                  start=(g == 0), stop=(g == 1),
                  reads=[("kf", None)], writes=[(bA[0], None)], tag=f"dM{h}_{c}")
        yield
        kb.stt(Ptp[:], bA[1][:, 0:128], nbcol, maskSL[:], ALU.mult, ALU.mult,
               reads=[(bA[0], None), ("nbeta_cols", None), ("maskSL", None)],
               writes=[(RN("Pt"), None)], tag=f"dP{h}_{c}")
        yield
        kb.tr(pbank[1][:, 0:128], Ptp[:], idb[:], reads=[(RN("Pt"), None), ("idb", None)],
              writes=[(pbank[0], None)], tag=f"dPT{h}_{c}")
        yield
        kb.cp(PTtp[:], pbank[1][:, 0:128], eng="dve", reads=[(pbank[0], None)],
              writes=[(RN("PTt"), None)], tag=f"dPTe{h}_{c}")
        yield
        kb.tt(Ybfp[:], idf[:], PTtp[:], ALU.add, reads=[("idf", None), (RN("PTt"), None)],
              writes=[(RN("Ybf"), None)], tag=f"dY0{h}_{c}")
        yield
        qcur, qtcur = (RN("Pt"), RN("PTt"))
        bufs = [(RN("Q0"), RN("QT0")), (RN("Q1"), RN("QT1"))]
        for i in range(1, 7):
            qn, qtn = bufs[i % 2]
            kb.mm(bA[1][:, 128:256], kb.sb[qtcur][:], kb.sb[qcur][:], True, True,
                  reads=[(qtcur, None), (qcur, None)], writes=[(bA[0], None)],
                  tag=f"dQ{h}_{c}_{i}")
            kb.mm(bB[1][:, 0:128], kb.sb[qcur][:], kb.sb[qtcur][:], True, True,
                  reads=[(qcur, None), (qtcur, None)], writes=[(bB[0], None)],
                  tag=f"dQT{h}_{c}_{i}")
            yield
            kb.cp(kb.sb[qn][:], bA[1][:, 128:256], eng="dve",
                  reads=[(bA[0], None)], writes=[(qn, None)], tag=f"dQe{h}_{c}_{i}")
            kb.cp(kb.sb[qtn][:], bB[1][:, 0:128], eng="act",
                  reads=[(bB[0], None)], writes=[(qtn, None)], tag=f"dQTe{h}_{c}_{i}")
            yield
            kb.mm(bC[1][:, 0:128], kb.sb[qn][:], Ybfp[:], True, True,
                  reads=[(qn, None), (RN("Ybf"), None)], writes=[(bC[0], None)],
                  tag=f"dYm{h}_{c}_{i}")
            yield
            kb.tt(Ybfp[:], Ybfp[:], bC[1][:, 0:128], ALU.add,
                  reads=[(RN("Ybf"), None), (bC[0], None)], writes=[(RN("Ybf"), None)],
                  tag=f"dYa{h}_{c}_{i}")
            yield
            qcur, qtcur = qn, qtn
        kb.mm(bA[1][:, 0:256], Ybfp[:], vb_tok[:, c * 256:(c + 1) * 256], True, True,
              reads=[(RN("Ybf"), None), ("v_tok_b", c)], writes=[(bA[0], None)],
              tag=f"du{h}_{c}")
        yield
        kb.act(u_p[:], bA[1][:, 0:256], AF.Copy, reads=[(bA[0], None)],
               writes=[(RN("u_sb"), None)], tag=f"due{h}_{c}")
        yield
        for g in range(2):
            kb.mm(bB[1][:, g * 128:(g + 1) * 128],
                  kbn_tok[:, c * 256 + g * 128: c * 256 + (g + 1) * 128], Ybfp[:],
                  True, True,
                  reads=[("kbn_tok", c), (RN("Ybf"), None)], writes=[(bB[0], None)],
                  tag=f"dwT{h}_{c}_{g}")
        yield
        kb.cp(wTp[:], bB[1][:, 0:256], eng="dve", reads=[(bB[0], None)],
              writes=[(RN("wT_sb"), None)], tag=f"dwTe{h}_{c}")
        yield

    def delta_tail(h, c):
        D_, RN, bA, bB, bC, pbank = delta_env(c)
        wTp, u_p, uadjp, attnTp = D_["wT_sb"], D_["u_sb"], D_["uadj"], D_["attnT"]
        ccols = slice(c * 128, (c + 1) * 128)
        for g in range(2):
            kb.mm(bC[1][:, 0:256], wTp[:, g * 128:(g + 1) * 128],
                  Sbf[:, g * 256:(g + 1) * 256], start=(g == 0), stop=(g == 1),
                  reads=[(RN("wT_sb"), None), ("Sbf", None)], writes=[(bC[0], None)],
                  tag=f"dwS{h}_{c}_{g}")
        yield
        kb.stt(uadjp[:], bC[1][:, 0:256], -1.0, u_p[:], ALU.mult, ALU.add,
               reads=[(bC[0], None), (RN("u_sb"), None)], writes=[(RN("uadj"), None)],
               tag=f"duadj{h}_{c}")
        yield
        for g in range(2):
            kb.mm(bA[1][:, 0:128], kf_v[:, g, ccols], qf_v[:, g, ccols],
                  start=(g == 0), stop=(g == 1),
                  reads=[("kf", None), ("qf", None)], writes=[(bA[0], None)],
                  tag=f"dat{h}_{c}_{g}")
        yield
        kb.tt(attnTp[:], bA[1][:, 0:128], maskIU[:], ALU.mult,
              reads=[(bA[0], None), ("maskIU", None)], writes=[(RN("attnT"), None)],
              tag=f"datm{h}_{c}")
        yield
        for g in range(2):
            kb.mm(bB[1][:, 0:256], qf_v[:, g, ccols], Sbf[:, g * 256:(g + 1) * 256],
                  start=(g == 0), stop=False,
                  reads=[("qf", None), ("Sbf", None)], writes=[(bB[0], None)],
                  tag=f"dqS{h}_{c}_{g}")
        yield
        kb.mm(bB[1][:, 0:256], attnTp[:], uadjp[:], start=False, stop=True,
              reads=[(RN("attnT"), None), (RN("uadj"), None)], writes=[(bB[0], None)],
              tag=f"dou{h}_{c}")
        yield
        kb.act(dout[:, c * 256:(c + 1) * 256], bB[1][:, 0:256], AF.Copy,
               reads=[(bB[0], None)], writes=[("dout", c)], tag=f"doute{h}_{c}")
        yield
        for g in range(2):
            kb.mm(bC[1][:, 0:256] if g == 0 else bC[1][:, 256:512],
                  kn_tok[:, c * 256 + g * 128: c * 256 + (g + 1) * 128], uadjp[:],
                  True, True,
                  reads=[("kn_tok", c), (RN("uadj"), None)], writes=[(bC[0], None)],
                  tag=f"dSu{h}_{c}_{g}")
        yield
        kb.tt(S32[:], S32[:], bC[1][:], ALU.add,
              reads=[("S32", None), (bC[0], None)], writes=[("S32", None)],
              tag=f"dSa{h}_{c}")
        yield
        kb.act(Sbf[:], S32[:], AF.Copy, reads=[("S32", None)], writes=[("Sbf", None)],
               tag=f"dSb{h}_{c}")
        yield

    def toep_ld_quarter(h, toep_win, q, tag):
        toep_v = toep_win.rearrange("s (c j) -> s c j", j=64)
        buf = q % 2
        tsl = m1main[0:127, buf * 4096:(buf + 1) * 4096]
        srcq = toep_v[:, h * 256 + q * 64: h * 256 + (q + 1) * 64, :]
        kb.dma("pool", tsl.rearrange("s (c j) -> s c j", j=64), srcq,
               reads=[("m1main", buf)], writes=[("m1main", buf)],
               tag=f"ldtoep{tag}{h}_{q}")

    def c1_stats(h):
        for blk in range(NBLK):
            for ti, (tn, tb) in ((0, ("dout", dout)), (1, ("v_tok", v_tok))):
                xsl = tb[:, blk * 256:(blk + 1) * 256]
                base = blk * 8 + ti * 3
                kb.red(xin[:, base:base + 1], xsl, ALU.add, reads=[(tn, None)],
                       writes=[("xin", ("c1", blk, ti, 0))], tag=f"c1s{h}_{blk}_{ti}")
                kb.red(xin[:, base + 1:base + 2], xsl, ALU.add, absval=True,
                       reads=[(tn, None)], writes=[("xin", ("c1", blk, ti, 1))],
                       tag=f"c1a{h}_{blk}_{ti}")
                kb.act(trash[:, 0:256], xsl, AF.Square, accum_out=xin[:, base + 2:base + 3],
                       reads=[(tn, None)], writes=[("trash", None), ("xin", ("c1", blk, ti, 2))],
                       tag=f"c1q{h}_{blk}_{ti}")
                yield

    def fir_gen(h, toep_win, dst, dst_name, tag, prefetched):
        def ld_quarter(q):
            toep_ld_quarter(h, toep_win, q, tag)
        if not prefetched:
            ld_quarter(0)
            ld_quarter(1)
        for q in range(4):
            buf = q % 2
            if q + 2 < 4:
                pass
            for ci in range(64):
                cih = q * 64 + ci
                bank = pf[(cih // 32) % 2]
                bank_res = f"pf{(cih // 32) % 2}"
                cslot = cih % 32
                lhsT = m1main[0:127, buf * 4096 + ci * 64: buf * 4096 + (ci + 1) * 64]
                rA = copyA[0:127, :].rearrange("s (m c) -> s m c", c=256)[:, :, cih:cih + 1]
                rB = copyB[0:127, :].rearrange("s (m c) -> s m c", c=256)[:, 1:17, cih:cih + 1]
                kb.mm(bank[0:64, cslot * 16:(cslot + 1) * 16], lhsT, rA, True, True,
                      reads=[("m1main", buf), ("copyA", None)], writes=[(bank_res, cslot // 8)],
                      tag=f"fA{tag}{h}_{cih}")
                kb.mm(bank[64:128, cslot * 16:(cslot + 1) * 16], lhsT, rB, True, True,
                      reads=[("m1main", buf), ("copyB", None)], writes=[(bank_res, cslot // 8)],
                      tile_position=(0, 64), tag=f"fB{tag}{h}_{cih}")
                if cslot == 31:
                    c0 = cih - 31
                    dstap = dst[:, :].rearrange("p (m c) -> p m c", c=256)[:, :, c0:c0 + 32]
                    srcap = bank[:, 0:512].rearrange("p (c m) -> p c m", m=16)
                    kb.cp(dstap, srcap.rearrange("p c m -> p m c"), eng="dve",
                          reads=[(bank_res, 0), (bank_res, 1), (bank_res, 2), (bank_res, 3)],
                          writes=[(dst_name, cih // 32)], tag=f"fE{tag}{h}_{cih}")
                if cih % 8 == 7:
                    yield
            if q + 2 < 4:
                ld_quarter(q + 2)

    for h in range(H):
        for rn in ("qf", "kf", "vfp", "kn_tok", "kbn_tok", "v_tok", "v_tok_b",
                   "copyA", "copyB", "dout"):
            kb.s.retag(rn)
        load_head_weights(h)
        proj_conv(h, "q", "wq", "qcw", qf_v, "qf", 0)
        proj_conv(h, "k", "wk", "kcw", kf_v, "kf", 0)
        proj_conv(h, "v", "wv", "vcw", vfp_v, "vfp", 64)
        for g in range(2):
            kb.memset(vfp_v[:, g, 0:64], 0.0, reads=[], writes=[("vfp", (g, "pad"))],
                      tag=f"vpad{h}{g}")
        l2norm(h, qf_v, "qf", 0)
        l2norm(h, kf_v, "kf", 0)
        transpose_blocks(h, kf_v, "kf", 0,
                         [(kn_tok, "kn_tok", "copy", None),
                          (kbn_tok, "kbn_tok", "ts", lambda blk: beta_cols[:, blk * 4 + h: blk * 4 + h + 1])],
                         tag="kn")
        transpose_blocks(h, vfp_v, "vfp", 64,
                         [(v_tok, "v_tok", "copy", None),
                          (vb_tok, "v_tok_b", "ts", lambda blk: beta_cols[:, blk * 4 + h: blk * 4 + h + 1])],
                         col_of_blk=lambda blk: 64 + blk * 128, tag="vt")
        transpose_blocks(h, vfp_v, "vfp", 0, [(copyA, "copyA", "copy", None)],
                         col_of_blk=lambda blk: blk * 128 + 1, tag="ca")
        transpose_blocks(h, vfp_v, "vfp", 0, [(copyB, "copyB", "copy", None)],
                         blk_range=range(1, 17),
                         col_of_blk=lambda blk: blk * 128 - 63, tag="cb")
        kb.memset(S32[:], 0.0, reads=[], writes=[("S32", None)], tag=f"S0{h}")
        kb.memset(Sbf[:], 0.0, reads=[], writes=[("Sbf", None)], tag=f"S0b{h}")
        toep_ld_quarter(h, win["toep_l"], 0, "l")
        toep_ld_quarter(h, win["toep_l"], 1, "l")
        weave(delta_pre(h, 0))
        for c in range(NBLK):
            weave(delta_tail(h, c),
                  delta_pre(h, c + 1) if c + 1 < NBLK else None)
        kb.s.retag("qf"); kb.s.retag("kf")
        kb.s.retag("xin")
        weave(fir_gen(h, win["toep_l"], ll_tok, "kf", "l", True),
              c1_stats(h))
        weave(fir_gen(h, win["toep_s"], ls_tok, "qf", "s", False))
        if dbg and stop_after == "head0" and h == 0:
            kb.dump("qf", qf[:], [("qf", None)], cast=True)
            kb.dump("kf", kf[:], [("kf", None)], cast=True)
            kb.dump("vfp", vfp[:], [("vfp", None)], cast=True)
            kb.dump("v_tok", v_tok[:], [("v_tok", None)], cast=True)
            kb.dump("kn_tok", kn_tok[:], [("kn_tok", None)], cast=True)
            kb.dump("dout", dout[:], [("dout", None)], cast=True)
            kb.dump("ll_tok", ll_tok[:], [("kf", None)], cast=True)
            kb.dump("ls_tok", ls_tok[:], [("qf", None)], cast=True)
        attn_c_phase(kb, h, locals())
        if dbg and stop_after == "head0" and h == 0:
            return nc, kb

    kb.s.retag("lnxf")
    kb.s.retag("xin")
    owT_sb = lnxf[:, 0:8192]
    ohnin = lnxf[:, 8192:12288]
    attn_tail(kb, locals())
    if stop_after == "attna":
        return nc, kb

    kb.free_from("lnxf")

    def alloc_ffn(sfx):
        R = lambda b: b + sfx
        Gd = {"R": R}
        Gd["outf"] = kb.sbuf(R("outf"), [128, 8 * T], F32)
        Gd["cn1"] = kb.sbuf(R("cn1"), [128, 8 * 1024], BF16)
        Gd["cn2"] = kb.sbuf(R("cn2"), [128, 8 * 512], BF16)
        Gd["cn3"] = kb.sbuf(R("cn3"), [128, 4 * 256], BF16)
        Gd["cn4"] = kb.sbuf(R("cn4"), [128, 2], BF16)
        Gd["cb1f"] = kb.sbuf(R("cb1f"), [128, 8], F32)
        Gd["cb2f"] = kb.sbuf(R("cb2f"), [128, 4], F32)
        Gd["cb3f"] = kb.sbuf(R("cb3f"), [128, 2], F32)
        Gd["cb4"] = kb.sbuf(R("cb4"), [1, 1], F32)
        Gd["gw1"] = kb.sbuf(R("gw1"), [128, 8 * 512], BF16)
        Gd["gw2"] = kb.sbuf(R("gw2"), [128, 4 * 256], BF16)
        Gd["gw3"] = kb.sbuf(R("gw3"), [128, 2], BF16)
        Gd["gb1"] = kb.sbuf(R("gb1"), [1, 512], F32)
        Gd["gb2"] = kb.sbuf(R("gb2"), [1, 256], F32)
        Gd["gb3"] = kb.sbuf(R("gb3"), [1, 1], F32)
        Gd["arr"] = kb.sbuf(R("arr"), [1, 1], F32)
        Gd["xt"] = kb.sbuf(R("xt"), [128, 4 * 1024], F32)
        Gd["lnt"] = kb.sbuf(R("lnt"), [128, 2 * 1024], BF16)
        Gd["trash"] = kb.sbuf(R("trash"), [128, 1024], BF16)
        Gd["statf"] = kb.sbuf(R("statf"), [128, 32], F32)
        Gd["ob"] = [kb.sbuf(R(f"ob{p}"), [128, 8 * 512], BF16) for p in range(2)]
        Gd["h1t"] = [kb.sbuf(R(f"h1t{p}"), [128, 8 * 512], BF16) for p in range(2)]
        Gd["h2t"] = [kb.sbuf(R(f"h2t{p}"), [128, 4 * 512], BF16) for p in range(2)]
        Gd["h3t"] = [kb.sbuf(R(f"h3t{p}"), [128, 2 * 512], BF16) for p in range(2)]
        Gd["gsc"] = kb.sbuf(R("gsc"), [1, 3640], F32)
        Gd["gcf"] = kb.sbuf(R("gcf"), [128, 8], BF16)
        Gd["grow"] = kb.sbuf(R("grow"), [1, 512], F32)
        Gd["tw"] = [kb.sbuf(R(f"tw{p}"), [128, 512], BF16) for p in range(2)]
        kb.dma("pool", Gd["cn1"][:].rearrange("p (kt o) -> p kt o", kt=8),
               win["cn_w1"].rearrange("(kt p) o -> p kt o", p=128), writes=[(R("cn1"), None)])
        kb.dma("pool", Gd["cn2"][:].rearrange("p (kt o) -> p kt o", kt=8),
               win["cn_w2"].rearrange("(kt p) o -> p kt o", p=128), writes=[(R("cn2"), None)])
        kb.dma("pool", Gd["cn3"][:].rearrange("p (kt o) -> p kt o", kt=4),
               win["cn_w3"].rearrange("(kt p) o -> p kt o", p=128), writes=[(R("cn3"), None)])
        kb.dma("pool", Gd["cn4"][:], win["cn_w4"], writes=[(R("cn4"), None)])
        kb.dma("pool", Gd["gw1"][:].rearrange("p (kt o) -> p kt o", kt=8),
               win["gc_w1"].rearrange("(kt p) o -> p kt o", p=128), writes=[(R("gw1"), None)])
        kb.dma("pool", Gd["gw2"][:].rearrange("p (kt o) -> p kt o", kt=4),
               win["gc_w2"].rearrange("(kt p) o -> p kt o", p=128), writes=[(R("gw2"), None)])
        kb.dma("pool", Gd["gw3"][:], win["gc_w3"], writes=[(R("gw3"), None)])
        kb.dma("sp", Gd["cb1f"][:], win["cb1f"], writes=[(R("cb1f"), None)])
        kb.dma("sp", Gd["cb2f"][:], win["cb2f"], writes=[(R("cb2f"), None)])
        kb.dma("sp", Gd["cb3f"][:], win["cb3f"], writes=[(R("cb3f"), None)])
        kb.dma("sp", Gd["cb4"][:], win["cb4"], writes=[(R("cb4"), None)])
        kb.dma("sp", Gd["gb1"][:], win["gb1_row"], writes=[(R("gb1"), None)])
        kb.dma("sp", Gd["gb2"][:], win["gb2_row"], writes=[(R("gb2"), None)])
        kb.dma("sp", Gd["gb3"][:], win["gb3"], writes=[(R("gb3"), None)])
        kb.dma("sp", Gd["arr"][:], win["ar"], writes=[(R("arr"), None)])
        Gd.update(pf=pf, pb=pb, idf=idf, idb=idb, konst=konst, ones_row=ones_row,
                  ones_col_b=ones_col_b)
        return Gd

    G = alloc_ffn("_f1")
    G["x_src"] = xa_scr
    G["x_src_name"] = "xa_scr"
    G["x_dst"] = xb_scr
    G["x_dst_name"] = "xb_scr"
    ffn_phase(kb, G, "1")
    if stop_after == "ffn1":
        return nc, kb

    kb.free_from("outf_f1")

    Bd = {}
    Bd["rkT"] = kb.sbuf("rkT", [128, 8 * 2049], BF16)
    Bd["Wst"] = kb.sbuf("Wst", [128, 8 * 1024], F32)
    Bd["Wbf"] = kb.sbuf("Wbf", [128, 8 * 1024], BF16)
    Bd["WwT"] = kb.sbuf("WwT", [128, 8 * 1024], BF16)
    Bd["WrT"] = kb.sbuf("WrT", [128, 8 * 1024], BF16)
    Bd["MT"] = kb.sbuf("MT", [128, 128], BF16)
    Bd["DTs"] = kb.sbuf("DTs", [128, 128], F32)
    Bd["CZs"] = kb.sbuf("CZs", [128, 128], F32)
    Bd["iop"] = kb.sbuf("iop", [128, 1], F32)
    Bd["ior"] = kb.sbuf("ior", [128, 1], F32)
    Bd["lgcol"] = kb.sbuf("lgcol", [128, 1], F32)
    Bd["gpcol"] = kb.sbuf("gpcol", [128, 1], F32)
    Bd["gwcol"] = kb.sbuf("gwcol", [128, 1], F32)
    Bd["gccol"] = kb.sbuf("gccol", [128, 1], F32)
    Bd["acol"] = kb.sbuf("acol", [128, 1], F32)
    Bd["brow"] = kb.sbuf("brow", [1, 8], F32)
    Bd["xt"] = [kb.sbuf(f"xtb{p}", [128, 1024], F32) for p in range(4)]
    Bd["lnt32"] = [kb.sbuf(f"lnt32{p}", [128, 1024], F32) for p in range(2)]
    Bd["lnbf"] = [kb.sbuf(f"lnbf{p}", [128, 1024], BF16) for p in range(2)]
    Bd["vt"] = [kb.sbuf(f"vt{p}", [128, 1024], BF16) for p in range(2)]
    Bd["vgw"] = [kb.sbuf(f"vgw{p}", [128, 1024], BF16) for p in range(2)]
    Bd["wkt"] = [kb.sbuf(f"wkt{p}", [128, 1024], BF16) for p in range(2)]
    Bd["statb2"] = kb.sbuf("statb2", [128, 8], F32)
    Bd["rdbf"] = kb.sbuf("rdbf", [128, 1024], BF16)
    Bd["rdT"] = kb.sbuf("rdT", [128, 1024], BF16)
    Bd["SMT"] = kb.sbuf("SMT", [128, 128], BF16)
    Bd["scrf"] = kb.sbuf("scrf", [128, 1024], F32)
    Bd["trash"] = kb.sbuf("trashb", [128, 1024], BF16)
    Bd["statb"] = kb.sbuf("statb", [128, 8], F32)
    Bd.update(pf=pf, pb=pb, idf=idf, idb=idb, konst=konst, ones_row=ones_row)
    Bd["x_src"] = xb_scr
    Bd["x_src_name"] = "xb_scr"
    Bd["x_dst"] = xa_scr
    Bd["x_dst_name"] = "xa2"
    kb.dma("pool", Bd["WwT"][:].rearrange("p (kt o) -> p kt o", kt=8),
           win["WwT"].rearrange("(kt p) o -> p kt o", p=128), writes=[("WwT", None)])
    kb.dma("pool", Bd["WrT"][:].rearrange("p (kt o) -> p kt o", kt=8),
           win["WrT"].rearrange("(kt p) o -> p kt o", p=128), writes=[("WrT", None)])
    kb.dma("sp", Bd["DTs"][:], cin["DT"], writes=[("DTs", None)])
    kb.dma("sp", Bd["CZs"][:], cin["CZ"], writes=[("CZs", None)])
    kb.dma("sp", Bd["iop"][:], cin["iota_p"], writes=[("iop", None)])
    kb.dma("sp", Bd["ior"][:], cin["iota_rev"], writes=[("ior", None)])
    kb.dma("sp", Bd["brow"][0:1, 0:1], win["decay"], writes=[("brow", 0)])
    kb.dma("sp", Bd["brow"][0:1, 1:2], win["log_alpha"], writes=[("brow", 1)])
    kb.memset(Bd["Wst"][:], 0.0, writes=[("Wst", (jt, half)) for jt in range(8) for half in range(2)], tag="bW0")
    kb.memset(Bd["Wbf"][:], 0.0, writes=[("Wbf", None)], tag="bWb0")
    for g in range(8):
        kb.memset(Bd["rkT"][:, g * 2049: g * 2049 + 1], 0.0,
                  writes=[("rkT", (-1, g // 4))], tag=f"brk0_{g}")
    kb.act(Bd["brow"][0:1, 2:3], Bd["brow"][0:1, 0:1], AF.Sigmoid,
           reads=[("brow", 0)], writes=[("brow", 2)], tag="bsig")
    kb.act(Bd["brow"][0:1, 3:4], Bd["brow"][0:1, 2:3], AF.Ln,
           reads=[("brow", 2)], writes=[("brow", 3)], tag="bln0")
    kb.act(Bd["brow"][0:1, 4:5], Bd["brow"][0:1, 1:2], AF.Exp,
           reads=[("brow", 1)], writes=[("brow", 4)], tag="bexp")
    kb.mm(pf[0][:, 0:1], ones_row[:], Bd["brow"][0:1, 3:4], True, True,
          reads=[("ones_row", None), ("brow", 3)], writes=[("pf0", None)], tag="blgb")
    kb.cp(Bd["lgcol"][:], pf[0][:, 0:1], eng="dve", reads=[("pf0", None)],
          writes=[("lgcol", None)])
    kb.mm(pf[0][:, 1:2], ones_row[:], Bd["brow"][0:1, 4:5], True, True,
          reads=[("ones_row", None), ("brow", 4)], writes=[("pf0", None)], tag="bacb")
    kb.cp(Bd["acol"][:], pf[0][:, 1:2], eng="dve", reads=[("pf0", None)],
          writes=[("acol", None)])
    kb.act(Bd["gpcol"][:], Bd["iop"][:], AF.Exp, scale=Bd["lgcol"][:],
           reads=[("iop", None), ("lgcol", None)], writes=[("gpcol", None)], tag="bgp")
    kb.act(Bd["gwcol"][:], Bd["ior"][:], AF.Exp, scale=Bd["lgcol"][:],
           reads=[("ior", None), ("lgcol", None)], writes=[("gwcol", None)], tag="bgw")
    kb.ts(Bd["scrf"][:, 0:1], Bd["lgcol"][:], 128.0, reads=[("lgcol", None)],
          writes=[("scrf", None)], tag="bgc1")
    kb.act(Bd["gccol"][:], Bd["scrf"][:, 0:1], AF.Exp,
           reads=[("scrf", None)], writes=[("gccol", None)], tag="bgc2")
    kb.act(Bd["scrf"][:, 0:128], Bd["DTs"][:], AF.Exp, scale=Bd["lgcol"][:],
           reads=[("DTs", None), ("lgcol", None), ("scrf", None)], writes=[("scrf", None)], tag="bmt1")
    kb.tt(Bd["MT"][:], Bd["scrf"][:, 0:128], Bd["CZs"][:], ALU.mult,
          reads=[("scrf", None), ("CZs", None)], writes=[("MT", None)], tag="bmt2")
    nck = NBLK
    if stop_after and stop_after.startswith("attnb") and len(stop_after) > 5:
        nck = int(stop_after[5:])
    attnb_phase(kb, Bd, nck)
    if stop_after and stop_after.startswith("attnb"):
        if kb.dbg:
            kb.dump("Wst", Bd["Wst"][:], [("Wst", (jt, hh)) for jt in range(8) for hh in range(2)])
            kb.dump("rdbf", Bd["rdbf"][:], [("rdbf", 0), ("rdbf", 1)], cast=True)
        return nc, kb

    kb.free_from("rkT")
    G2 = alloc_ffn("_f2")
    G2["x_src"] = xa_scr
    G2["x_src_name"] = "xa2"
    G2["x_dst"] = out_hbm
    G2["x_dst_name"] = "out"
    ffn_phase(kb, G2, "2")
    return nc, kb


def _dummy_ins():
    d = {}
    d["L0_attn_a_q_w"] = np.zeros((D, D), np.float32)
    d["L0_attn_a_k_w"] = np.zeros((D, D), np.float32)
    d["L0_attn_a_v_w"] = np.zeros((D, D), np.float32)
    d["L0_attn_a_b_w"] = np.zeros((H, D), np.float32)
    d["L0_attn_a_o_w"] = np.zeros((D, D), np.float32)
    d["L0_attn_a_mlp1_w"] = np.zeros((1024, D + 16), np.float32)
    d["L0_attn_a_mlp1_b"] = np.zeros((1024,), np.float32)
    d["L0_attn_a_mlp2_w"] = np.zeros((4, 1024), np.float32)
    d["L0_attn_a_mlp2_b"] = np.zeros((4,), np.float32)
    d["L0_attn_a_q_cw"] = np.zeros((D, 4), np.float32)
    d["L0_attn_a_k_cw"] = np.zeros((D, 4), np.float32)
    d["L0_attn_a_v_cw"] = np.zeros((D, 4), np.float32)
    d["L0_attn_a_fir_short_w"] = np.zeros((H, DV, 5), np.float32)
    d["L0_attn_a_fir_long_w"] = np.zeros((H, DV, 64), np.float32)
    d["L0_attn_a_logit_temp"] = np.zeros((1,), np.float32)
    d["L0_attn_a_conv_res_logit"] = np.zeros((H,), np.float32)
    d["L0_attn_a_norm_w"] = np.zeros((DV,), np.float32)
    d["L0_ffn_cn_w1"] = np.zeros((D, D), np.float32)
    d["L0_ffn_cn_w2"] = np.zeros((D, 512), np.float32)
    d["L0_ffn_cn_w3"] = np.zeros((512, 256), np.float32)
    d["L0_ffn_cn_w4"] = np.zeros((256, 1), np.float32)
    d["L0_ffn_cn_b1"] = np.zeros((D,), np.float32)
    d["L0_ffn_cn_b2"] = np.zeros((512,), np.float32)
    d["L0_ffn_cn_b3"] = np.zeros((256,), np.float32)
    d["L0_ffn_cn_b4"] = np.zeros((1,), np.float32)
    d["L0_ffn_gc_w1"] = np.zeros((D, 512), np.float32)
    d["L0_ffn_gc_w2"] = np.zeros((512, 256), np.float32)
    d["L0_ffn_gc_w3"] = np.zeros((256, 1), np.float32)
    d["L0_ffn_gc_b1"] = np.zeros((512,), np.float32)
    d["L0_ffn_gc_b2"] = np.zeros((256,), np.float32)
    d["L0_ffn_gc_b3"] = np.zeros((1,), np.float32)
    d["L0_ffn_adaptive_range"] = np.zeros((1,), np.float32)
    d["L0_attn_b_W_write"] = np.zeros((D, D), np.float32)
    d["L0_attn_b_W_read"] = np.zeros((D, D), np.float32)
    d["L0_attn_b_decay"] = np.zeros((), np.float32)
    d["L0_attn_b_log_alpha"] = np.zeros((), np.float32)
    return d


def finish(nc, kb):
    sems = {e: nc.semaphore("sem_" + e).__enter__() for e in ENGINES}
    dma_sems = {q: [nc.semaphore(f"dma_{q}{i}").__enter__() for i in range(Sched.NDMA)]
                for q in ("sp", "pool")}
    run = kb.s.emit_all(sems, dma_sems)
    with nc.Block() as block:
        @block.tensor
        def _(t):
            run("pe", t)

        @block.scalar
        def _(sc):
            run("act", sc)

        @block.vector
        def _(v):
            run("dve", v)

        @block.gpsimd
        def _(g):
            run("pool", g)

        @block.sync
        def _(sy):
            run("sp", sy)
            for q in ("sp", "pool"):
                for slot in range(Sched.NDMA):
                    fin = kb.s.dma_final.get((q, slot))
                    if fin:
                        sy.wait_ge(dma_sems[q][slot], 16 * fin)
    kb.es.close()
    return nc


def make_in_map(ins, b, consts, weights):
    m = {"x": np.ascontiguousarray(ins["L0_attn_a_hidden_states"][b]).astype(np.float32)}
    for k, v in consts.items():
        m["c_" + k] = v
    for k, v in weights.items():
        m["w_" + k] = np.ascontiguousarray(v).astype(np.float32)
    return m



def kernel(**inputs):
    ins = {k: np.asarray(v) for k, v in inputs.items()}
    consts = prep_consts()
    weights = prep_weights(ins)
    nc, kb = build_program(dbg=False, stop_after=None)
    finish(nc, kb)
    in_maps = [make_in_map(ins, b, consts, weights) for b in range(8)]
    res = run_bass_kernel_spmd(nc, in_maps, core_ids=list(range(8)))
    out = np.stack([np.asarray(res.results[b]["out"], dtype=np.float32)
                    for b in range(8)], axis=0)
    return out
